# version 01
# baseline (speedup 1.0000x reference)
import numpy as np
import concourse.bass as bass
import concourse.mybir as mybir
from concourse.bass_utils import run_bass_kernel_spmd

F32 = mybir.dt.float32
BF16 = mybir.dt.bfloat16
I32 = mybir.dt.int32
AF = mybir.ActivationFunctionType
ALU = mybir.AluOpType

NCORES = 8
TOK = 1024
TOKH = TOK + 3
DM = 2048
DIN = 4096
NH = 64
PH = 64
NG = 34
CK = 8
EXP = 16
FF = 1408
FC = 11
EPS = 1e-5

MT_TAP = (16, 0, 1)
XP_TAP = 0
FLUSH = True
RACECHECK = True
NO_CC = False
DEBUG = []
SKIP_TAIL = False

ENGINES = ("sync", "tensor", "vector", "scalar", "gpsimd")


class P:
    def __init__(self, nc):
        self.nc = nc
        self.ops = {e: [] for e in ENGINES}
        self.sems = {}
        self.cnt = {}
        self._guards = {}
        self._gorder = []
        self.bg = {}

    def sem(self, name):
        g = self.nc.semaphore(name)
        h = g.__enter__()
        self._guards[name] = g
        self._gorder.append(name)
        self.sems[name] = h
        self.cnt[name] = 0
        return h

    def alloc(self, name, shape, dtype, psum=False):
        g = (self.nc.psum_tensor if psum else self.nc.sbuf_tensor)(name, shape, dtype)
        t = g.__enter__()
        self._guards[name] = g
        self._gorder.append(name)
        return t

    def free(self, name):
        self._guards.pop(name).__exit__(None, None, None)
        self._gorder.remove(name)

    def close(self):
        for n in reversed(self._gorder):
            self._guards[n].__exit__(None, None, None)
        self._guards = {}
        self._gorder = []

    def emit(self, eng, fn):
        self.ops[eng].append(fn)

    def bump(self, sem_name, k):
        self.cnt[sem_name] += k
        return self.cnt[sem_name]

    def dma(self, eng, out, in_, sem_name="LDG", slow=False):
        s = self.sems[sem_name]
        v = self.bump(sem_name, 16)
        if slow:
            self.emit(eng, lambda e, out=out, in_=in_, s=s:
                      e.dma_start(out=out, in_=in_, allow_slow_non_contiguous=True).then_inc(s, 16))
        else:
            self.emit(eng, lambda e, out=out, in_=in_, s=s:
                      e.dma_start(out=out, in_=in_).then_inc(s, 16))
        return v

    def wait(self, eng, sem_name, val):
        if val <= 0:
            return
        s = self.sems[sem_name]
        self.emit(eng, lambda e, s=s, val=val: e.wait_ge(s, val))

    def op(self, eng, fn, inc=None, flush=True):
        if inc is None:
            self.emit(eng, fn)
            return None
        s = self.sems[inc[0]]
        v = self.bump(inc[0], inc[1])
        if flush and FLUSH and eng in ("vector", "scalar", "tensor"):
            def emitfn(e, fn=fn, s=s, k=inc[1]):
                fn(e)
                e.drain().then_inc(s, k)
            self.emit(eng, emitfn)
        else:
            self.emit(eng, lambda e, fn=fn, s=s, k=inc[1]: fn(e).then_inc(s, k))
        return v

    def spacer(self, eng, cycles=512):
        if FLUSH:
            self.emit(eng, lambda e, c=cycles: e.nop(cycle_cnt=c, nofuse=True))

    def gwait(self, eng, name):
        for sem, val in self.bg.get(name, []):
            self.wait(eng, sem, val)

    def gset(self, name, sem, val):
        self.bg[name] = [(sem, val)]

    def gadd(self, name, sem, val):
        self.bg.setdefault(name, []).append((sem, val))


def build():
    nc = bass.Bass("TRN2", target_bir_lowering=False, dynamic_dma_scratch_size=6144,
                   detect_race_conditions=RACECHECK)
    p = P(nc)

    def din(name, shape):
        return nc.dram_tensor(name, shape, F32, kind="ExternalInput").ap()

    uT = din("uT", [DM, TOKH])
    uTm = din("uTm", [DM, TOK])
    UF = din("UF", [TOK, DM])
    NWR = din("NWR", [1, DIN])
    WXT = din("WXT", [NG, 128, 16, 128])
    WDT = din("WDT", [128, 16, 64])
    WZT = din("WZT", [8, 128, 16, 512])
    WOT = din("WOT", [16, 128, 32, 128])
    CDIAG = din("CDIAG", [NG, 128, 4, 128])
    CB = din("CB", [128, NG])
    NW = din("NW", [128, 32])
    DTB = din("DTB", [NH, 1])
    ALOG = din("ALOG", [NH, 1])
    DVEC = din("DVEC", [1, NH])
    GW = din("GW", [EXP, DM])
    EB = din("EB", [EXP, 1])
    WG = din("WG", [17, FC, 128, 16, 128])
    WU = din("WU", [17, FC, 128, 16, 128])
    WD = din("WD", [17, 4, 128, FC, 4, 128])
    IDENT = din("IDENT", [128, 128])
    ONES1 = din("ONES1", [1, 128])
    OH16 = din("OH16", [24, 24 * 128])
    RSEL = din("RSEL", [128, 4])

    out = nc.dram_tensor("out", [TOK, DM], F32, kind="ExternalOutput").ap()
    cc_in = nc.dram_tensor("cc_in", [128, 4104], F32)
    cc_out = nc.dram_tensor("cc_out", [512, 4104], F32)
    x1_dram = nc.dram_tensor("x1_dram", [128, 8 * DM], F32)
    comb_dram = nc.dram_tensor("comb_dram", [EXP, TOK], BF16)
    cum_dram = nc.dram_tensor("cum_dram", [NH, TOK], F32)

    dbg_shapes = {"xbc": [128, NG * TOK], "dt": [NH, TOK], "y": [128, CK * DIN],
                  "x1": [128, 16 * TOK], "comb": [128, 8 * 16], "acc": [128, 16 * TOK],
                  "sc": [128, 6 * 512], "bc": [128, 2 * TOK], "xp": [128, TOK],
                  "T": [128, NH * PH], "y0": [128, CK * DIN], "rel": [128, NH * PH],
                  "tri": [128, 9 * 128], "mt": [128, 5 * 128], "tt": [128, 8 * 144],
                  "yn": [128, CK * DIN]}
    dbg = {n: nc.dram_tensor("dbg_" + n, dbg_shapes[n], F32, kind="ExternalOutput").ap()
           for n in DEBUG}

    for s in ("LDG", "LDS", "PE", "VE", "SE", "GE", "CC", "STS",
              "LW0", "LW1", "LC0", "LC1", "LR", "LRS", "CST", "CCS", "DBS", "XT", "LZ0", "LZ1", "LO0", "LO1", "LU0", "LU1", "LG0", "LG1", "LD0", "LD1", "LCB", "LX0", "LX1"):
        p.sem(s)

    id_f = p.alloc("id_f", [128, 128], F32)
    id_b = p.alloc("id_b", [128, 128], BF16)
    triu = p.alloc("triu", [128, 128], F32)
    mask_i = p.alloc("mask_i", [128, 128], I32)
    ones_f = p.alloc("ones_f", [1, 128], F32)
    ones_c = p.alloc("ones_c", [128, 1], F32)
    ones_b = p.alloc("ones_b", [1, 128], BF16)
    oh_sb = p.alloc("oh_sb", [24, 24 * 128], F32)
    cb_sb = p.alloc("cb_sb", [128, NG], F32)
    nw_sb = p.alloc("nw_sb", [128, 32], F32)
    dtb_sb = p.alloc("dtb_sb", [NH, 1], F32)
    alog_sb = p.alloc("alog_sb", [NH, 1], F32)
    A_sb = p.alloc("A_sb", [NH, 1], F32)
    dv_sb = p.alloc("dv_sb", [1, NH], F32)
    rsel_sb = p.alloc("rsel_sb", [128, 4], F32)
    rsinv_sb = p.alloc("rsinv_sb", [128, 4], F32)
    u_fm = p.alloc("u_fm", [128, 16 * TOKH], BF16)
    y_sb = p.alloc("y_sb", [128, CK * DIN], BF16)
    BCp = p.alloc("BCp", [128, 2 * TOK], BF16)
    ab_h = p.alloc("ab_h", [NH, 24], F32)
    ab_T = p.alloc("ab_T", [24, NH], F32)
    aseg = p.alloc("aseg", [NH, 1], F32)
    azp = p.alloc("azp", [128, 8], F32)
    s_T = p.alloc("s_T", [128, CK * NH], F32)
    T0 = p.alloc("T0", [128, NH * PH], F32)
    T1 = p.alloc("T1", [128, NH * PH], F32)
    GmT = p.alloc("GmT", [128, CK * 128], F32)
    B_tm = p.alloc("B_tm", [128, CK * 128], BF16)
    cum_T = p.alloc("cum_T", [128, CK * NH], F32)
    evec_T = p.alloc("evec_T", [128, CK * NH], F32)
    ecum_T = p.alloc("ecum_T", [128, CK * NH], F32)
    w_T = p.alloc("w_T", [128, CK * NH], F32)
    adb = p.alloc("adb", [128, CK * NH], F32)
    D_bc = p.alloc("D_bc", [128, NH], F32)
    T_bf = p.alloc("T_bf", [128, NH * PH], BF16)
    wst = p.alloc("wst", [128, 2 * 2048], BF16)
    cdst = p.alloc("cdst", [128, 2 * 512], BF16)
    xpost = p.alloc("xpost", [128, 2 * TOK], BF16)
    xraw = p.alloc("xraw", [128, 2 * TOKH], BF16)
    dtraw = p.alloc("dtraw", [NH, TOKH], F32)
    dts = p.alloc("dts", [NH, TOK], F32)
    cum = p.alloc("cum", [NH, TOK], F32)
    evl = p.alloc("evl", [NH, TOK], F32)
    scr64 = p.alloc("scr64", [NH, TOK], F32)
    z64 = p.alloc("z64", [NH, 128], F32)
    ecm = dtraw[:, 0:TOK]

    psA = [p.alloc(f"psA{i}", [128, 512], F32, psum=True) for i in range(2)]
    psC = [p.alloc(f"psC{i}", [128, 512], F32, psum=True) for i in range(2)]
    psY = p.alloc("psY", [128, 512], F32, psum=True)
    psS = p.alloc("psS", [128, 512], F32, psum=True)
    psG = p.alloc("psG", [128, 512], F32, psum=True)
    psTR = p.alloc("psTR", [128, 128], F32, psum=True)

    def vop(fn):
        return p.op("vector", fn, inc=("VE", 1))

    def sop(fn):
        return p.op("scalar", fn, inc=("SE", 1))

    def top(fn):
        return p.op("tensor", fn, inc=("PE", 1))

    def vsync():
        p.wait("vector", "VE", p.cnt["VE"])

    for dst, src in ((id_f, IDENT), (ones_f, ONES1), (oh_sb, OH16), (cb_sb, CB),
                     (nw_sb, NW), (dtb_sb, DTB), (alog_sb, ALOG), (dv_sb, DVEC),
                     (rsel_sb, RSEL)):
        p.dma("sync", dst[:, :], src[:, :], "LDS")
    lds_c = p.cnt["LDS"]

    for k in range(16):
        p.dma("gpsimd", u_fm[:, k * TOKH:(k + 1) * TOKH], uT[k * 128:(k + 1) * 128, :], "LDG")
    ldg_ufm = p.cnt["LDG"]

    ge_mask = p.op("gpsimd", lambda e: e.iota(mask_i[:, :], pattern=[[1, 128]], base=0,
                                              channel_multiplier=-1), inc=("GE", 1))
    p.wait("vector", "LDS", lds_c)
    p.op("vector", lambda e: e.tensor_copy(id_b[:, :], id_f[:, :]))
    p.op("vector", lambda e: e.memset(ones_c[:, :], 1.0))
    p.op("vector", lambda e: e.tensor_copy(ones_b[:, :], ones_f[:, :]))
    p.op("vector", lambda e: e.memset(z64[:, :], 0.0))
    p.op("vector", lambda e: e.tensor_scalar(rsinv_sb[:, :], rsel_sb[:, :], -1.0, 1.0,
                                             op0=ALU.mult, op1=ALU.add))
    p.wait("vector", "GE", ge_mask)
    ve_c0 = vop(lambda e: e.tensor_scalar(triu[:, :], mask_i[:, :], 0.0, None, op0=ALU.is_ge))
    p.wait("scalar", "LDS", lds_c)
    se_A = sop(lambda e: e.activation(A_sb[:, :], alog_sb[:, :], AF.Exp))
    p.wait("vector", "SE", se_A)
    ve_A = vop(lambda e: e.tensor_scalar(A_sb[:, :], A_sb[:, :], -1.0, None, op0=ALU.mult))
    p.wait("tensor", "LDS", lds_c)
    pe_dbc = top(lambda e: e.matmul(psG[:, 0:NH], ones_f[0:1, :], dv_sb[0:1, :],
                                    start=True, stop=True))
    p.wait("vector", "PE", pe_dbc)
    p.spacer("vector")
    ve_dbc = vop(lambda e: e.tensor_copy(D_bc[:, :], psG[:, 0:NH]))
    p.gset("psG", "VE", ve_dbc)

    TCH = [(0, 512), (512, 512), (1024, 3)]

    def in_proj_group(cc, wslot, dst_fn, m=128, raw_name=None):
        if cc < NG:
            src = WXT[cc, :, :, :].rearrange("p k c -> p (k c)")
            wcols = 2048
        else:
            src = WDT[:, :, :].rearrange("p k c -> p (k c)")
            wcols = 16 * 64
        p.gwait("gpsimd", f"wst{wslot}")
        wv = p.dma("gpsimd", wst[:, wslot * 2048: wslot * 2048 + wcols], src, f"LW{wslot}")
        p.wait("tensor", f"LW{wslot}", wv)
        last_pe = 0
        for ti, (c0, w) in enumerate(TCH):
            ps = psA[ti % 2]
            p.gwait("tensor", f"psA{ti % 2}")
            for k in range(16):
                lhs = wst[:, wslot * 2048 + k * m: wslot * 2048 + (k + 1) * m]
                rhs = u_fm[:, k * TOKH + c0: k * TOKH + c0 + w]
                pe = p.op("tensor", lambda e, ps=ps, lhs=lhs, rhs=rhs, k=k, m=m, w=w:
                          e.matmul(ps[0:m, 0:w], lhs, rhs, start=(k == 0), stop=(k == 15)),
                          inc=("PE", 1) if k == 15 else None)
            last_pe = p.cnt["PE"]
            p.wait("vector", "PE", last_pe)
            if raw_name:
                p.gwait("vector", raw_name)
            dst = dst_fn(c0, w)
            vv = vop(lambda e, ps=ps, dst=dst, m=m, w=w: e.tensor_copy(dst, ps[0:m, 0:w]))
            p.gset(f"psA{ti % 2}", "VE", vv)
        p.gset(f"wst{wslot}", "PE", last_pe)
        if raw_name:
            p.bg[raw_name] = []
        return p.cnt["VE"]

    def conv_group(g, raw_slot, dst_fn, post_name=None, ve_ready=0):
        cs = g % 2
        p.gwait("gpsimd", f"cdst{cs}")
        cv = p.dma("gpsimd", cdst[:, cs * 512:(cs + 1) * 512],
                   CDIAG[g, :, :, :].rearrange("p k c -> p (k c)"), f"LC{cs}")
        p.wait("tensor", f"LC{cs}", cv)
        p.wait("tensor", "VE", ve_ready)
        last_pe = 0
        for ti in range(2):
            ps = psC[ti]
            p.gwait("tensor", f"psC{ti}")
            for k in range(4):
                lhs = cdst[:, cs * 512 + k * 128: cs * 512 + (k + 1) * 128]
                rhs = xraw[:, raw_slot * TOKH + ti * 512 + k: raw_slot * TOKH + ti * 512 + k + 512]
                p.op("tensor", lambda e, ps=ps, lhs=lhs, rhs=rhs, k=k:
                     e.matmul(ps[:, 0:512], lhs, rhs, start=(k == 0), stop=(k == 3)),
                     inc=("PE", 1) if k == 3 else None)
            last_pe = p.cnt["PE"]
            p.wait("scalar", "PE", last_pe)
            p.spacer("scalar", 256)
            if post_name:
                p.gwait("scalar", post_name)
            dst = dst_fn(ti * 512, 512)
            sv = sop(lambda e, ps=ps, dst=dst, g=g:
                     e.activation(dst, ps[:, 0:512], AF.Silu, bias=cb_sb[:, g:g + 1]))
            p.gset(f"psC{ti}", "SE", sv)
        p.gset(f"cdst{cs}", "PE", last_pe)
        p.gadd(f"xraw{raw_slot}", "PE", last_pe)
        if post_name:
            p.bg[post_name] = []
        return p.cnt["SE"]

    p.wait("tensor", "LDG", ldg_ufm)
    p.wait("tensor", "VE", ve_c0)

    ve_r = in_proj_group(32, 0, lambda c0, w: xraw[:, c0:c0 + w], raw_name="xraw0")
    conv_group(32, 0, lambda c0, w: BCp[:, c0:c0 + w], ve_ready=ve_r)
    ve_r = in_proj_group(33, 1, lambda c0, w: xraw[:, TOKH + c0:TOKH + c0 + w], raw_name="xraw1")
    conv_group(33, 1, lambda c0, w: BCp[:, TOK + c0:TOK + c0 + w], ve_ready=ve_r)
    in_proj_group(NG, 0, lambda c0, w: dtraw[:, c0:c0 + w], m=64, raw_name="xraw0")
    se_bc = p.cnt["SE"]
    ve_dt_ready = p.cnt["VE"]
    if "bc" in dbg:
        p.wait("gpsimd", "SE", se_bc)
        p.dma("gpsimd", dbg["bc"][:, :], BCp[:, :], "DBS")
        p.wait("tensor", "DBS", p.cnt["DBS"])

    p.wait("tensor", "SE", se_bc)
    p.wait("sync", "SE", se_bc)
    for ck in range(CK):
        p.emit("sync", lambda e, ck=ck, s=p.sems["XT"]:
               e.dma_start(out=B_tm[:, ck * 128:(ck + 1) * 128],
                           in_=BCp[:, ck * 128:(ck + 1) * 128], transpose=True).then_inc(s, 16))
        p.bump("XT", 16)
    btm_xt = p.cnt["XT"]
    for ck in range(CK):
        p.gwait("tensor", "psG")
        pe_g = top(lambda e, ck=ck: e.matmul(psG[:, 0:128], BCp[:, ck * 128:(ck + 1) * 128],
                                             BCp[:, TOK + ck * 128: TOK + (ck + 1) * 128],
                                             start=True, stop=True))
        p.wait("vector", "PE", pe_g)
        p.spacer("vector")
        vv = vop(lambda e, ck=ck: e.tensor_tensor(GmT[:, ck * 128:(ck + 1) * 128], psG[:, 0:128],
                                                  triu[:, :], op=ALU.mult))
        p.gset("psG", "VE", vv)

    dt_o = dtraw[:, 3:3 + TOK]
    p.wait("vector", "VE", ve_dt_ready)
    v2 = vop(lambda e: e.tensor_scalar(scr64[:, :], dt_o, dtb_sb[:, 0:1], None, op0=ALU.add))
    p.wait("scalar", "VE", v2)
    sab = sop(lambda e: e.activation(dts[:, :], scr64[:, :], AF.Abs))
    p.wait("scalar", "SE", sab)
    s1 = sop(lambda e: e.activation(evl[:, :], dts[:, :], AF.Exp, scale=-1.0))
    p.wait("scalar", "SE", s1)
    s2 = sop(lambda e: e.activation(ecm[:, :], evl[:, :], AF.Ln, bias=1.0))
    p.wait("vector", "SE", s2)
    vop(lambda e: e.tensor_scalar(evl[:, :], scr64[:, :], 0.0, None, op0=ALU.max))
    vsync()
    vop(lambda e: e.tensor_tensor(dts[:, :], ecm[:, :], evl[:, :], op=ALU.add))
    vsync()
    p.wait("vector", "VE", ve_A)
    vop(lambda e: e.tensor_scalar(scr64[:, :], dts[:, :], A_sb[:, 0:1], None, op0=ALU.mult))
    vsync()
    for ck in range(CK):
        vop(lambda e, ck=ck: e.tensor_tensor_scan(cum[:, ck * 128:(ck + 1) * 128],
                                                  scr64[:, ck * 128:(ck + 1) * 128],
                                                  z64[:, :], initial=0.0,
                                                  op0=ALU.add, op1=ALU.bypass))
    vsync()
    v4 = p.cnt["VE"]
    p.wait("scalar", "VE", v4)
    s3 = sop(lambda e: e.activation(evl[:, :], dts[:, :], AF.Ln))
    p.wait("vector", "SE", s3)
    v5 = vop(lambda e: e.tensor_tensor(evl[:, :], evl[:, :], cum[:, :], op=ALU.subtract))
    p.wait("scalar", "VE", v5)
    s4 = sop(lambda e: e.activation(ecm[:, :], cum[:, :], AF.Exp))
    lad = cum[:, 127::128]
    p.wait("vector", "SE", s4)
    vop(lambda e: e.tensor_tensor_scan(scr64[:, 0:CK], lad, z64[:, 0:CK], initial=0.0,
                                       op0=ALU.add, op1=ALU.bypass))
    vsync()
    vop(lambda e: e.tensor_tensor(scr64[:, 8:16], scr64[:, 0:CK], lad, op=ALU.subtract))
    vsync()
    vop(lambda e: e.tensor_copy(ab_h[:, 16:24], lad))
    vsync()
    v6 = p.cnt["VE"]
    p.wait("scalar", "VE", v6)
    sop(lambda e: e.activation(ab_h[:, 0:8], lad, AF.Exp))
    sop(lambda e: e.activation(ab_h[:, 8:16], scr64[:, 8:16], AF.Exp))
    s6 = sop(lambda e: e.activation(aseg[:, 0:1], scr64[:, 7:8], AF.Exp))

    p.wait("tensor", "SE", s6)
    tr_jobs = []
    for ck in range(CK):
        tr_jobs.append((cum[:, ck * 128:(ck + 1) * 128], cum_T, ck))
        tr_jobs.append((evl[:, ck * 128:(ck + 1) * 128], evec_T, ck))
        tr_jobs.append((ecm[:, ck * 128:(ck + 1) * 128], ecum_T, ck))
    for src, dstt, ck in tr_jobs:
        p.gwait("tensor", "psG")
        pe_t = top(lambda e, src=src: e.transpose(psG[0:128, 0:64], src[0:64, :],
                                                  id_f[0:64, 0:64]))
        p.wait("vector", "PE", pe_t)
        p.spacer("vector")
        vv = vop(lambda e, dstt=dstt, ck=ck: e.tensor_copy(dstt[:, ck * NH:(ck + 1) * NH],
                                                           psG[:, 0:64]))
        p.gset("psG", "VE", vv)
    p.gwait("tensor", "psG")
    pe_abt = top(lambda e: e.transpose(psG[0:24, 0:64], ab_h[0:64, :], id_f[0:64, 0:64]))
    p.wait("vector", "PE", pe_abt)
    p.spacer("vector")
    ve_abt = vop(lambda e: e.tensor_copy(ab_T[:, :], psG[0:24, 0:64]))
    p.gset("psG", "VE", ve_abt)
    for ck in range(CK):
        p.gwait("tensor", "psG")
        top(lambda e, ck=ck: e.matmul(psG[:, 0:NH], oh_sb[:, ck * 128:(ck + 1) * 128],
                                      ab_T[:, :], start=True, stop=True))
        pe_b2 = top(lambda e, ck=ck: e.matmul(psG[:, 64:64 + NH],
                                              oh_sb[:, (8 + ck) * 128:(9 + ck) * 128],
                                              ab_T[:, :], start=True, stop=True))
        pe_b3 = top(lambda e, ck=ck: e.matmul(psG[:, 128:128 + NH],
                                              oh_sb[:, (16 + ck) * 128:(17 + ck) * 128],
                                              ab_T[:, :], start=True, stop=True))
        p.wait("vector", "PE", pe_b3)
        p.spacer("vector")
        vop(lambda e, ck=ck: e.tensor_copy(adb[:, ck * NH:(ck + 1) * NH], psG[:, 0:NH]))
        vop(lambda e, ck=ck: e.tensor_tensor(s_T[:, ck * NH:(ck + 1) * NH],
                                             ecum_T[:, ck * NH:(ck + 1) * NH],
                                             psG[:, 64:64 + NH], op=ALU.mult))
        vv = vop(lambda e, ck=ck: e.tensor_tensor(w_T[:, ck * NH:(ck + 1) * NH],
                                                  evec_T[:, ck * NH:(ck + 1) * NH],
                                                  psG[:, 128:128 + NH], op=ALU.add))
        p.gset("psG", "VE", vv)
        p.wait("scalar", "VE", vv)
        sop(lambda e, ck=ck: e.activation(w_T[:, ck * NH:(ck + 1) * NH],
                                          w_T[:, ck * NH:(ck + 1) * NH], AF.Exp))

    p.wait("gpsimd", "VE", p.cnt["VE"])
    p.wait("gpsimd", "SE", p.cnt["SE"])
    cum_st = p.dma("gpsimd", cum_dram[:, :], cum[:, :], "CST")

    if "tri" in dbg:
        p.wait("sync", "VE", p.cnt["VE"])
        p.dma("sync", dbg["tri"][:, 0:128], triu[:, :], "STS")
        p.dma("sync", dbg["tri"][:, 128:9 * 128], GmT[:, :], "STS")
        p.wait("vector", "STS", p.cnt["STS"])
    if "dt" in dbg:
        p.wait("sync", "VE", p.cnt["VE"])
        p.wait("sync", "SE", p.cnt["SE"])
        p.dma("sync", dbg["dt"][:, :], dts[:, :], "STS")
        p.wait("vector", "STS", p.cnt["STS"])
    if "sc" in dbg:
        p.wait("sync", "VE", p.cnt["VE"])
        p.wait("sync", "SE", p.cnt["SE"])
        for i, tt in enumerate((cum_T, evec_T, ecum_T, w_T, adb, s_T)):
            p.dma("sync", dbg["sc"][:, i * 512:(i + 1) * 512], tt[:, :], "STS")
        p.wait("vector", "STS", p.cnt["STS"])
        p.wait("scalar", "STS", p.cnt["STS"])

    for nm in ("z64", "scr64", "evl", "cum", "dts", "dtraw"):
        p.free(nm)
    Mt = p.alloc("Mt", [128, 2 * 128], BF16)
    xtm = p.alloc("xtm", [128, 2 * PH], BF16)
    xwtm = p.alloc("xwtm", [128, 2 * PH], BF16)
    xdsc = p.alloc("xdsc", [128, 2 * PH], BF16)
    p.op("vector", lambda e: e.memset(T0[:, :], 0.0))
    vz = vop(lambda e: e.memset(T_bf[:, :], 0.0))

    dbg_mt_sb = p.alloc("dbg_mt_sb", [128, 5 * 128], F32) if "mt" in dbg else None
    tt_sb = p.alloc("tt_sb", [128, 8 * 144], F32) if "tt" in dbg else None
    CT_bc = p.alloc("CT_bc", [128, 2 * TOK], F32)
    xtm2 = p.alloc("xtm2", [128, 2 * 128], BF16)
    E_a = p.alloc("E_a", [128, 2 * 128], F32)
    E_b = p.alloc("E_b", [128, 2 * 128], F32)
    Ea_se = {0: 0, 1: 0}
    Eb_ve = {0: 0, 1: 0}
    rowC_pe = {0: 0, 1: 0}
    row_ld = 0
    xt_guard = {0: 0, 1: 0}
    for g in range(32):
        rs = g % 2
        ve_r = in_proj_group(g, rs, lambda c0, w, rs=rs:
                             xraw[:, rs * TOKH + c0: rs * TOKH + c0 + w],
                             raw_name=f"xraw{rs}")
        se_conv = conv_group(g, rs, lambda c0, w, rs=rs: xpost[:, rs * TOK + c0: rs * TOK + c0 + w],
                             post_name=f"xpost{rs}", ve_ready=ve_r)
        p.wait("gpsimd", "CST", cum_st)
        p.gwait("gpsimd", "rowAB")
        row_ld = p.dma("gpsimd", CT_bc[:, :],
                       cum_dram[2 * g:2 * g + 2, :].rearrange("h c -> (h c)").partition_broadcast(128),
                       "LR")
        pe_lastx = 0
        xt_val = {}
        p.wait("sync", "SE", se_conv)
        for cc0 in (0, 1):
            p.wait("sync", "PE", xt_guard[cc0 % 2])
            p.emit("sync", lambda e, rs=rs, cc0=cc0, s=p.sems["XT"]:
                   e.dma_start(out=xtm2[:, (cc0 % 2) * 128:(cc0 % 2) * 128 + 128],
                               in_=xpost[:, rs * TOK + cc0 * 128: rs * TOK + (cc0 + 1) * 128],
                               transpose=True).then_inc(s, 16))
            xt_val[cc0] = p.bump("XT", 16)
        for ck in range(CK):
            Tsrc = T0 if ck % 2 == 0 else T1
            Tdst = T1 if ck % 2 == 0 else T0
            xpar = ck % 2
            for hs in (0, 1):
                h = 2 * g + hs
                par = hs
                p.wait("vector", "XT", xt_val[ck])
                p.gwait("vector", f"xtm{hs}")
                ve_xt = vop(lambda e, hs=hs, xpar=xpar:
                            e.tensor_copy(xtm[:, hs * PH:(hs + 1) * PH],
                                          xtm2[:, xpar * 128 + hs * PH: xpar * 128 + (hs + 1) * PH]))
                ve_xw = vop(lambda e, hs=hs, xpar=xpar, ck=ck, h=h:
                            e.tensor_scalar(xwtm[:, hs * PH:(hs + 1) * PH],
                                            xtm2[:, xpar * 128 + hs * PH: xpar * 128 + (hs + 1) * PH],
                                            w_T[:, ck * NH + h: ck * NH + h + 1], None,
                                            op0=ALU.mult))
                p.wait("vector", "LR", row_ld)
                p.wait("vector", "SE", Ea_se[par])
                ve_ts = vop(lambda e, par=par, ck=ck, h=h, hs=hs:
                            e.tensor_scalar(E_a[:, par * 128:(par + 1) * 128],
                                            CT_bc[:, hs * TOK + ck * 128: hs * TOK + (ck + 1) * 128],
                                            evec_T[:, ck * NH + h: ck * NH + h + 1], 8.0,
                                            op0=ALU.add, op1=ALU.min))
                p.wait("scalar", "VE", ve_ts)
                p.wait("scalar", "VE", Eb_ve[par])
                se_e = sop(lambda e, par=par: e.activation(E_b[:, par * 128:(par + 1) * 128],
                                                           E_a[:, par * 128:(par + 1) * 128],
                                                           AF.Exp))
                Ea_se[par] = se_e
                p.wait("vector", "SE", se_e)
                ve_m = vop(lambda e, hs=hs, par=par, ck=ck:
                           e.tensor_tensor(Mt[:, hs * 128:(hs + 1) * 128],
                                           E_b[:, par * 128:(par + 1) * 128],
                                           GmT[:, ck * 128:(ck + 1) * 128], op=ALU.mult))
                Eb_ve[par] = ve_m
                p.wait("tensor", "VE", ve_m)
                p.gwait("tensor", "psY")
                p.op("tensor", lambda e, hs=hs: e.matmul(psY[:, 0:PH], Mt[:, hs * 128:(hs + 1) * 128],
                                                         xtm[:, hs * PH:(hs + 1) * PH],
                                                         start=True, stop=False))
                pe_y = top(lambda e, ck=ck, h=h:
                           e.matmul(psY[:, 0:PH], BCp[:, TOK + ck * 128: TOK + (ck + 1) * 128],
                                    T_bf[:, h * PH:(h + 1) * PH], start=False, stop=True))
                p.gwait("tensor", "psS")
                pe_s = top(lambda e, ck=ck, hs=hs:
                           e.matmul(psS[:, 0:PH], B_tm[:, ck * 128:(ck + 1) * 128],
                                    xwtm[:, hs * PH:(hs + 1) * PH], start=True, stop=True))
                p.gset(f"xtm{hs}", "PE", pe_s)
                if hs == 1:
                    xt_guard[xpar] = pe_s
                    if ck + 2 < CK:
                        p.wait("sync", "PE", pe_s)
                        p.emit("sync", lambda e, rs=rs, cc=ck + 2, xpar=xpar, s=p.sems["XT"]:
                               e.dma_start(out=xtm2[:, xpar * 128: xpar * 128 + 128],
                                           in_=xpost[:, rs * TOK + cc * 128: rs * TOK + (cc + 1) * 128],
                                           transpose=True).then_inc(s, 16))
                        xt_val[ck + 2] = p.bump("XT", 16)
                p.wait("vector", "PE", pe_s)
                p.spacer("vector", 256)
                if (g, ck, hs) == MT_TAP and "mt" in dbg:
                    vop(lambda e: e.tensor_copy(dbg_mt_sb[:, 0:64], psS[:, 0:PH]))
                    vop(lambda e, hs=hs: e.tensor_copy(dbg_mt_sb[:, 64:128],
                                                       xwtm[:, hs * PH:(hs + 1) * PH]))
                    vop(lambda e: e.tensor_copy(dbg_mt_sb[:, 128:192], psY[:, 0:PH]))
                    vop(lambda e, hs=hs: e.tensor_copy(dbg_mt_sb[:, 192:256],
                                                       xtm[:, hs * PH:(hs + 1) * PH]))
                    vop(lambda e, hs=hs: e.tensor_copy(dbg_mt_sb[:, 256:384],
                                                       Mt[:, hs * 128:(hs + 1) * 128]))
                    vop(lambda e, ck=ck: e.tensor_copy(dbg_mt_sb[:, 384:512],
                                                       B_tm[:, ck * 128:(ck + 1) * 128]))
                    vmt = vop(lambda e, hs=hs: e.tensor_copy(dbg_mt_sb[:, 512:576], xtm[:, hs * PH:(hs + 1) * PH]))
                    p.wait("sync", "VE", vmt)
                    p.dma("sync", dbg["mt"][:, :], dbg_mt_sb[:, :], "STS")
                    p.wait("vector", "STS", p.cnt["STS"])
                vop(lambda e, hs=hs, h=h: e.tensor_scalar(xdsc[:, hs * PH:(hs + 1) * PH],
                                                          xtm[:, hs * PH:(hs + 1) * PH],
                                                          D_bc[:, h:h + 1], None, op0=ALU.mult))
                vsync()
                vy = vop(lambda e, ck=ck, h=h, hs=hs:
                         e.scalar_tensor_tensor(y_sb[:, ck * DIN + h * PH: ck * DIN + (h + 1) * PH],
                                                psY[:, 0:PH],
                                                ecum_T[:, ck * NH + h: ck * NH + h + 1],
                                                xdsc[:, hs * PH:(hs + 1) * PH],
                                                op0=ALU.mult, op1=ALU.add))
                p.gset("psY", "VE", vy)
                vt = vop(lambda e, ck=ck, h=h, Tsrc=Tsrc, Tdst=Tdst:
                         e.scalar_tensor_tensor(Tdst[:, h * PH:(h + 1) * PH],
                                                Tsrc[:, h * PH:(h + 1) * PH],
                                                adb[:, ck * NH + h: ck * NH + h + 1],
                                                psS[:, 0:PH], op0=ALU.mult, op1=ALU.add))
                p.gset("psS", "VE", vt)
                vsync()
                vop(lambda e, h=h, Tdst=Tdst: e.tensor_copy(T_bf[:, h * PH:(h + 1) * PH],
                                                            Tdst[:, h * PH:(h + 1) * PH]))
                if "tt" in dbg and g == MT_TAP[0] and hs == MT_TAP[2]:
                    vsync()
                    vop(lambda e, ck=ck, h=h, Tdst=Tdst:
                        e.tensor_copy(tt_sb[:, ck * 144: ck * 144 + 64],
                                      Tdst[:, h * PH:(h + 1) * PH]))
                    vop(lambda e, ck=ck: e.tensor_copy(tt_sb[:, ck * 144 + 64: ck * 144 + 128],
                                                       psS[:, 0:PH]))
                    vop(lambda e, ck=ck, h=h:
                        e.tensor_copy(tt_sb[:, ck * 144 + 128: ck * 144 + 129],
                                      adb[:, ck * NH + h: ck * NH + h + 1]))
                    vop(lambda e, ck=ck, h=h:
                        e.tensor_copy(tt_sb[:, ck * 144 + 129: ck * 144 + 130],
                                      w_T[:, ck * NH + h: ck * NH + h + 1]))
                    vop(lambda e, ck=ck, h=h:
                        e.tensor_copy(tt_sb[:, ck * 144 + 130: ck * 144 + 131],
                                      ecum_T[:, ck * NH + h: ck * NH + h + 1]))
                    vtt = vop(lambda e, ck=ck, hs=hs:
                              e.tensor_copy(tt_sb[:, ck * 144 + 131: ck * 144 + 131 + 13],
                                            xwtm[:, hs * PH: hs * PH + 13]))
                    if ck == 7:
                        p.wait("sync", "VE", vtt)
                        p.dma("sync", dbg["tt"][:, :], tt_sb[:, :], "STS")
        p.gset("rowAB", "VE", p.cnt["VE"])
        pe_lastx = p.cnt["XT"]
        if g == XP_TAP and "xp" in dbg:
            p.wait("gpsimd", "SE", se_conv)
            xpv = p.dma("gpsimd", dbg["xp"][:, :], xpost[:, 0:TOK], "DBS")
            p.gadd("xpost0", "DBS", xpv)
        p.gadd(f"xpost{rs}", "XT", p.cnt["XT"])
    ve_scan = p.cnt["VE"]
    if "T" in dbg:
        p.wait("sync", "VE", ve_scan)
        p.dma("sync", dbg["T"][:, :], T0[:, :], "STS")
    if "y0" in dbg:
        p.wait("gpsimd", "VE", ve_scan)
        p.dma("gpsimd", dbg["y0"][:, :], y_sb[:, :], "DBS")
        p.wait("tensor", "DBS", p.cnt["DBS"])

    if NO_CC:
        ve_y = ve_scan
        for nm in (["E_b", "E_a", "xtm2", "CT_bc"] + (["tt_sb"] if "tt" in dbg else []) +
               (["dbg_mt_sb"] if "mt" in dbg else []) +
                   ["xdsc", "xwtm", "xtm", "Mt", "xraw",
                    "xpost", "cdst", "wst", "T_bf", "D_bc", "adb", "w_T",
                    "ecum_T", "evec_T", "cum_T", "B_tm", "GmT"]):
            p.free(nm)
        return finish_build(p, nc, locals())
    Tfin = T0
    p.wait("gpsimd", "VE", ve_scan)
    vz2 = vop(lambda e: e.memset(azp[:, :], 0.0))
    p.wait("vector", "VE", vz2)
    vz3 = vop(lambda e: e.tensor_copy(azp[0:64, 0:1], aseg[:, 0:1]))
    st_T = p.dma("gpsimd", cc_in[:, 0:4096], Tfin[:, :], "CCS")
    p.wait("gpsimd", "SE", p.cnt["SE"])
    p.wait("gpsimd", "VE", vz3)
    st_a = p.dma("gpsimd", cc_in[:, 4096:4104], azp[:, :], "CCS")
    p.wait("gpsimd", "CCS", p.cnt["CCS"])
    cc_v = p.op("gpsimd", lambda e: e.collective_compute(
        "AllGather", ALU.bypass,
        replica_groups=[[0, 1, 2, 3], [4, 5, 6, 7]],
        ins=[cc_in.ap().opt()], outs=[cc_out.ap().opt()]), inc=("CC", 1))
    p.wait("gpsimd", "CC", cc_v)
    for nm in (["E_b", "E_a", "xtm2", "CT_bc"] + (["tt_sb"] if "tt" in dbg else []) +
               (["dbg_mt_sb"] if "mt" in dbg else []) +
               ["xdsc", "xwtm", "xtm", "Mt", "xraw",
                "xpost", "cdst", "wst", "T_bf", "D_bc", "adb", "w_T",
                "ecum_T", "evec_T", "cum_T", "B_tm", "GmT"]):
        p.free(nm)
    avrow = p.alloc("avrow", [1, 4 * NH], F32)
    ga = p.alloc("ga", [128, NH], F32)
    rstg = p.alloc("rstg", [128, 4096], F32)
    relay_b = p.alloc("relay_b", [128, NH * PH], F32)
    relay_bf = p.alloc("relay_bf", [128, NH * PH], BF16)
    for r in range(4):
        p.dma("gpsimd", avrow[0:1, r * NH:(r + 1) * NH],
              cc_out[r * 128:r * 128 + 64, 4096:4097].rearrange("h o -> o h"), "LDG", slow=True)
    ldav = p.cnt["LDG"]
    p.wait("tensor", "LDG", ldav)
    relay_a = T1
    p.wait("vector", "VE", ve_scan)
    vop(lambda e: e.memset(relay_a[:, :], 0.0))
    for r in range(3):
        p.wait("gpsimd", "VE", p.cnt["VE"])
        ldT = p.dma("gpsimd", rstg[:, :], cc_out[r * 128:(r + 1) * 128, 0:4096], "LRS")
        p.gwait("tensor", "psG")
        pe_ab = top(lambda e, r=r: e.matmul(psG[:, 0:NH], ones_f[0:1, :],
                                            avrow[0:1, r * NH:(r + 1) * NH],
                                            start=True, stop=True))
        p.wait("vector", "PE", pe_ab)
        p.spacer("vector")
        vop(lambda e, r=r: e.tensor_scalar(ga[:, :], psG[:, 0:NH], rsel_sb[:, r:r + 1],
                                           rsinv_sb[:, r:r + 1], op0=ALU.mult, op1=ALU.add))
        p.gset("psG", "VE", p.cnt["VE"])
        p.wait("vector", "LRS", ldT)
        vop(lambda e: e.tensor_scalar(rstg[:, :], rstg[:, :], rsel_sb[:, r:r + 1], None,
                                      op0=ALU.mult))
        vsync()
        src, dst = (relay_a, relay_b) if r % 2 == 0 else (relay_b, relay_a)
        for h in range(NH):
            vop(lambda e, h=h, src=src, dst=dst:
                e.scalar_tensor_tensor(dst[:, h * PH:(h + 1) * PH], src[:, h * PH:(h + 1) * PH],
                                       ga[:, h:h + 1], rstg[:, h * PH:(h + 1) * PH],
                                       op0=ALU.mult, op1=ALU.add))
        vsync()
    vop(lambda e: e.tensor_copy(relay_bf[:, :], relay_b[:, :]))
    vsync()
    ve_rbf = p.cnt["VE"]
    if "rel" in dbg:
        p.wait("sync", "VE", ve_rbf)
        p.dma("sync", dbg["rel"][:, :], relay_b[:, :], "STS")
    p.wait("tensor", "VE", ve_rbf)
    for ck in range(CK):
        for h in range(NH):
            p.gwait("tensor", "psY")
            pe_y2 = top(lambda e, ck=ck, h=h:
                        e.matmul(psY[:, 0:PH], BCp[:, TOK + ck * 128: TOK + (ck + 1) * 128],
                                 relay_bf[:, h * PH:(h + 1) * PH], start=True, stop=True))
            p.wait("vector", "PE", pe_y2)
            p.spacer("vector")
            vy = vop(lambda e, ck=ck, h=h:
                     e.scalar_tensor_tensor(y_sb[:, ck * DIN + h * PH: ck * DIN + (h + 1) * PH],
                                            psY[:, 0:PH],
                                            s_T[:, ck * NH + h: ck * NH + h + 1],
                                            y_sb[:, ck * DIN + h * PH: ck * DIN + (h + 1) * PH],
                                            op0=ALU.mult, op1=ALU.add))
            p.gset("psY", "VE", vy)
    ve_y = p.cnt["VE"]
    return finish_build(p, nc, locals())


def finish_build(p, nc, env):
    dbg = env["dbg"]
    vop = env["vop"]
    y_sb = env["y_sb"]
    out = env["out"]
    ve_y = env["ve_y"]

    if "y" in dbg:
        ydbg = p.alloc("ydbg", [128, 1024], F32)
        for i in range(32):
            p.wait("vector", "VE", ve_y)
            p.wait("vector", "STS", p.cnt["STS"])
            vv = vop(lambda e, i=i: e.tensor_copy(ydbg[:, :], y_sb[:, i * 1024:(i + 1) * 1024]))
            p.wait("sync", "VE", vv)
            p.dma("sync", dbg["y"][:, i * 1024:(i + 1) * 1024], ydbg[:, :], "STS")

    if SKIP_TAIL:
        zz = p.alloc("zz", [128, DM], F32)
        p.wait("vector", "VE", ve_y)
        vz = vop(lambda e: e.memset(zz[:, :], 0.0))
        p.wait("sync", "VE", vz)
        for tb in range(8):
            p.dma("sync", out[tb * 128:(tb + 1) * 128, :], zz[:, :], "STS")
    else:
        build_tail(p, nc, env)

    p.wait("sync", "STS", p.cnt["STS"])

    with nc.Block() as block:
        @block.sync
        def _(e):
            for f in p.ops["sync"]:
                f(e)

        @block.tensor
        def _(e):
            for f in p.ops["tensor"]:
                f(e)

        @block.vector
        def _(e):
            for f in p.ops["vector"]:
                f(e)

        @block.scalar
        def _(e):
            for f in p.ops["scalar"]:
                f(e)

        @block.gpsimd
        def _(e):
            for f in p.ops["gpsimd"]:
                f(e)

    p.close()
    return nc


def build_tail(p, nc, env):
    out = env["out"]
    ve_y = env["ve_y"]
    vop, sop, top, vsync = env["vop"], env["sop"], env["top"], env["vsync"]
    psA, psC, psY, psS, psG = env["psA"], env["psC"], env["psY"], env["psS"], env["psG"]
    y_sb, u_fm = env["y_sb"], env["u_fm"]
    WZT, WOT, WG, WU, WD = env["WZT"], env["WOT"], env["WG"], env["WU"], env["WD"]
    GW, EB, UF, NWR = env["GW"], env["EB"], env["UF"], env["NWR"]
    x1_dram, comb_dram = env["x1_dram"], env["comb_dram"]
    yn_dram = nc.dram_tensor("yn_dram", [128, CK * DIN], BF16)
    xlo_dram = nc.dram_tensor("xlo_dram", [128, 8 * DM], BF16)
    nw_sb, dbg = env["nw_sb"], env["dbg"]
    id_f, ones_f, ones_c = env["id_f"], env["ones_f"], env["ones_c"]
    XTS = p.sems["XT"]

    for nm in ("relay_bf", "relay_b", "rstg", "ga", "avrow"):
        p.free(nm)

    sgz = p.alloc("sgz", [128, 2 * 512], BF16)
    sc512 = p.alloc("sc512", [128, 512], BF16)
    ssq = p.alloc("ssq", [128, 64], F32)
    nw_bc = p.alloc("nw_bc", [128, DIN], BF16)
    rstd8 = p.alloc("rstd8", [128, 8], F32)
    rs_scr = p.alloc("rs_scr", [128, 8], F32)
    zw = p.alloc("zw", [128, 2 * 8192], BF16)
    p.wait("gpsimd", "VE", ve_y)
    p.dma("gpsimd", nw_bc[:, :],
          NWR[0:1, :].rearrange("o c -> (o c)").partition_broadcast(128), "LCB")
    nwbc_ld = p.cnt["LCB"]
    banks = [psA[0], psA[1], psC[0], psC[1]]
    bnames = ["psA0", "psA1", "psC0", "psC1"]
    p.wait("vector", "VE", ve_y)
    zw_val = {}
    for s8 in range(8):
        sl = s8 % 2
        p.gwait("gpsimd", f"zw{sl}")
        zw_val[s8] = p.dma("gpsimd", zw[:, sl * 8192:(sl + 1) * 8192],
                           WZT[s8, :, :, :].rearrange("p k c -> p (k c)"), f"LZ{sl}")
        p.wait("tensor", f"LZ{sl}", zw_val[s8])
        for tb in range(8):
            bi = (s8 * 8 + tb) % 4
            ps = banks[bi]
            p.gwait("tensor", bnames[bi])
            for k in range(16):
                p.op("tensor", lambda e, ps=ps, k=k, sl=sl, tb=tb:
                     e.matmul(ps[:, 0:512],
                              u_fm[:, k * TOKH + 3 + tb * 128: k * TOKH + 3 + (tb + 1) * 128],
                              zw[:, sl * 8192 + k * 512: sl * 8192 + (k + 1) * 512],
                              start=(k == 0), stop=(k == 15)),
                     inc=("PE", 1) if k == 15 else None)
            pev = p.cnt["PE"]
            p.wait("scalar", "PE", pev)
            p.gwait("scalar", f"sgz{tb % 2}")
            se_z = sop(lambda e, ps=ps, tb=tb: e.activation(sgz[:, (tb % 2) * 512:(tb % 2 + 1) * 512],
                                                            ps[:, 0:512], AF.Silu))
            p.gset(bnames[bi], "SE", se_z)
            p.wait("vector", "SE", se_z)
            ysl = y_sb[:, tb * DIN + s8 * 512: tb * DIN + (s8 + 1) * 512]
            vg = vop(lambda e, ysl=ysl, tb=tb: e.tensor_tensor(ysl, ysl,
                                                               sgz[:, (tb % 2) * 512:(tb % 2 + 1) * 512],
                                                               op=ALU.mult))
            p.gset(f"sgz{tb % 2}", "VE", vg)
            vsync()
            vop(lambda e, ysl=ysl, tb=tb, s8=s8:
                e.scalar_tensor_tensor(sc512[:, :], ysl, 1.0, ysl, op0=ALU.mult, op1=ALU.mult,
                                       accum_out=ssq[:, tb * 8 + s8: tb * 8 + s8 + 1]))
        p.gset(f"zw{sl}", "PE", p.cnt["PE"])
    p.free("zw")
    vsync()
    for tb in range(8):
        vop(lambda e, tb=tb: e.reduce_sum(rs_scr[:, tb:tb + 1], ssq[:, tb * 8:(tb + 1) * 8],
                                          axis=mybir.AxisListType.X))
    vsync()
    vop(lambda e: e.tensor_scalar(rs_scr[:, :], rs_scr[:, :], 1.0 / DIN, EPS,
                                  op0=ALU.mult, op1=ALU.add))
    vsync()
    v_rs = p.cnt["VE"]
    p.wait("scalar", "VE", v_rs)
    s_l = sop(lambda e: e.activation(rstd8[:, :], rs_scr[:, :], AF.Ln))
    p.wait("scalar", "SE", s_l)
    s_r = sop(lambda e: e.activation(rstd8[:, :], rstd8[:, :], AF.Exp, scale=-0.5))
    p.wait("vector", "SE", s_r)
    p.wait("vector", "LCB", nwbc_ld)
    for ck in range(CK):
        ysl = y_sb[:, ck * DIN:(ck + 1) * DIN]
        vop(lambda e, ysl=ysl: e.tensor_tensor(ysl, ysl, nw_bc[:, :], op=ALU.mult))
        vsync()
        vv = vop(lambda e, ysl=ysl, ck=ck: e.tensor_scalar(ysl, ysl, rstd8[:, ck:ck + 1], None,
                                                           op0=ALU.mult))
        p.wait("gpsimd", "VE", vv)
        p.dma("gpsimd", yn_dram[:, ck * DIN:(ck + 1) * DIN], ysl, "CST")
    yn_st = p.cnt["CST"]

    if "yn" in dbg:
        p.wait("gpsimd", "CST", yn_st)
        p.dma("gpsimd", dbg["yn"][:, :], y_sb[:, :], "DBS")
        p.wait("vector", "DBS", p.cnt["DBS"])

    for nm in ("rs_scr", "rstd8", "nw_bc", "ssq", "sc512", "sgz",
               "T1", "T0", "s_T", "azp", "aseg", "ab_T", "ab_h", "BCp", "y_sb", "u_fm"):
        p.free(nm)

    x1_fm = p.alloc("x1_fm", [128, 16 * TOK], BF16)
    yn_fm = p.alloc("yn_fm", [128, CK * DIN], BF16)
    wo = p.alloc("wo", [128, 2 * 4096], BF16)
    ustg = p.alloc("ustg", [128, 2 * DM], F32)
    x1stg = p.alloc("x1stg", [128, 8 * 128], F32)
    x1t = p.alloc("x1t", [128, 2 * DM], F32)
    xbt = p.alloc("xbt", [128, 2 * DM], BF16)
    xlt = p.alloc("xlt", [128, 2 * DM], BF16)
    bnscr = p.alloc("bnscr", [128, 24], F32)
    mv2 = p.alloc("mv2", [128, 2], F32)
    mcol = p.alloc("mcol", [128, 2], F32)
    p.wait("sync", "CST", yn_st)
    ynfm_val = 0
    for cc in range(32):
        for ck in range(CK):
            p.emit("sync", lambda e, cc=cc, ck=ck, s=XTS:
                   e.dma_start(out=yn_fm[:, cc * TOK + ck * 128: cc * TOK + (ck + 1) * 128],
                               in_=yn_dram[:, ck * DIN + cc * 128: ck * DIN + (cc + 1) * 128],
                               transpose=True).then_inc(s, 16))
            ynfm_val = p.bump("XT", 16)
    p.wait("tensor", "XT", ynfm_val)
    for dc in range(16):
        sl = dc % 2
        p.gwait("gpsimd", f"wo{sl}")
        wo_v = p.dma("gpsimd", wo[:, sl * 4096:(sl + 1) * 4096],
                     WOT[dc, :, :, :].rearrange("p k c -> p (k c)"), f"LO{sl}")
        p.wait("tensor", f"LO{sl}", wo_v)
        for tb in range(8):
            bi = tb % 4
            ps = banks[bi]
            p.gwait("tensor", bnames[bi])
            for cc in range(32):
                p.op("tensor", lambda e, ps=ps, cc=cc, tb=tb, sl=sl:
                     e.matmul(ps[:, 0:128],
                              yn_fm[:, cc * TOK + tb * 128: cc * TOK + (tb + 1) * 128],
                              wo[:, sl * 4096 + cc * 128: sl * 4096 + (cc + 1) * 128],
                              start=(cc == 0), stop=(cc == 31)),
                     inc=("PE", 1) if cc == 31 else None)
            pev = p.cnt["PE"]
            p.wait("vector", "PE", pev)
            p.spacer("vector", 256)
            p.gwait("vector", f"x1stg{tb}")
            vv = vop(lambda e, ps=ps, tb=tb: e.tensor_copy(x1stg[:, tb * 128:(tb + 1) * 128],
                                                           ps[:, 0:128]))
            p.gset(bnames[bi], "VE", vv)
        vsync()
        vstg = p.cnt["VE"]
        p.wait("gpsimd", "VE", vstg)
        dst = x1_dram[:, :].rearrange("p (tb c) -> p tb c", c=DM)[:, :, dc * 128:(dc + 1) * 128]
        st_v = p.dma("gpsimd", dst, x1stg[:, 0:1024].rearrange("p (tb c) -> p tb c", c=128), "CST")
        for tb in range(8):
            p.gset(f"x1stg{tb}", "CST", st_v)
        p.gset(f"wo{sl}", "PE", p.cnt["PE"])
    x1pre_st = p.cnt["CST"]
    x1fm_val = 0
    for tb in range(8):
        sl = tb % 2
        p.gwait("gpsimd", f"x1t{sl}")
        p.wait("gpsimd", "CST", x1pre_st)
        lx = p.dma("gpsimd", x1t[:, sl * DM:(sl + 1) * DM], x1_dram[:, tb * DM:(tb + 1) * DM],
                   f"LX{sl}")
        lu = p.dma("gpsimd", ustg[:, sl * DM:(sl + 1) * DM], UF[tb * 128:(tb + 1) * 128, :],
                   f"LU{sl}")
        p.wait("vector", f"LX{sl}", lx)
        p.wait("vector", f"LU{sl}", lu)
        xs = x1t[:, sl * DM:(sl + 1) * DM]
        vop(lambda e, xs=xs, sl=sl: e.tensor_tensor(xs, xs, ustg[:, sl * DM:(sl + 1) * DM],
                                                    op=ALU.add))
        vsync()
        for q in range(4):
            vop(lambda e, xs=xs, q=q: e.bn_stats(bnscr[:, q * 6:(q + 1) * 6],
                                                 xs[:, q * 512:(q + 1) * 512]))
            vsync()
        vop(lambda e: e.bn_aggr(mv2[:, :], bnscr[:, :]))
        vsync()
        vop(lambda e: e.tensor_scalar(mcol[:, 0:1], mv2[:, 0:1], -1.0, None, op0=ALU.mult))
        vsync()
        vop(lambda e: e.tensor_scalar(mv2[:, 1:2], mv2[:, 1:2], 1.0, EPS,
                                      op0=ALU.mult, op1=ALU.add))
        vsync()
        v_mv = p.cnt["VE"]
        p.wait("scalar", "VE", v_mv)
        s1 = sop(lambda e: e.activation(mcol[:, 1:2], mv2[:, 1:2], AF.Ln))
        p.wait("scalar", "SE", s1)
        s2 = sop(lambda e: e.activation(mcol[:, 1:2], mcol[:, 1:2], AF.Exp, scale=-0.5))
        p.wait("vector", "SE", s2)
        vop(lambda e, xs=xs: e.tensor_scalar(xs, xs, mcol[:, 0:1], None, op0=ALU.add))
        vsync()
        vop(lambda e, xs=xs: e.tensor_scalar(xs, xs, mcol[:, 1:2], None, op0=ALU.mult))
        vsync()
        vx1 = p.cnt["VE"]
        p.wait("gpsimd", "VE", vx1)
        p.dma("gpsimd", x1_dram[:, tb * DM:(tb + 1) * DM], xs, "CST")
        vop(lambda e, xs=xs, sl=sl: e.tensor_copy(xbt[:, sl * DM:(sl + 1) * DM], xs))
        vsync()
        vlo = vop(lambda e, xs=xs, sl=sl: e.tensor_tensor(xlt[:, sl * DM:(sl + 1) * DM], xs,
                                                          xbt[:, sl * DM:(sl + 1) * DM],
                                                          op=ALU.subtract))
        p.wait("gpsimd", "VE", vlo)
        lo_st = p.dma("gpsimd", xlo_dram[:, tb * DM:(tb + 1) * DM],
                      xlt[:, sl * DM:(sl + 1) * DM], "CCS")
        p.wait("sync", "VE", vlo)
        for k in range(16):
            p.emit("sync", lambda e, k=k, tb=tb, sl=sl, s=XTS:
                   e.dma_start(out=x1_fm[:, k * TOK + tb * 128: k * TOK + (tb + 1) * 128],
                               in_=xbt[:, sl * DM + k * 128: sl * DM + (k + 1) * 128],
                               transpose=True).then_inc(s, 16))
            x1fm_val = p.bump("XT", 16)
        p.gset(f"x1t{sl}", "XT", x1fm_val)
        p.gadd(f"x1t{sl}", "CCS", lo_st)
    x1lo_st = p.cnt["CCS"]
    for nm in ("mcol", "mv2", "bnscr", "xlt", "xbt", "x1t", "x1stg", "ustg", "wo", "yn_fm"):
        p.free(nm)

    build_tail2(p, nc, env, locals())


def build_tail2(p, nc, env, env1):
    out = env["out"]
    vop, sop, top, vsync = env["vop"], env["sop"], env["top"], env["vsync"]
    psA, psC, psY, psS, psG = env["psA"], env["psC"], env["psY"], env["psS"], env["psG"]
    WG, WU, WD = env["WG"], env["WU"], env["WD"]
    GW, EB, UF = env["GW"], env["EB"], env["UF"]
    x1_dram, comb_dram = env["x1_dram"], env["comb_dram"]
    dbg = env["dbg"]
    xlo_dram = env1["xlo_dram"]
    x1_fm = env1["x1_fm"]
    x1lo_st = env1["x1lo_st"]
    x1fm_val = env1["x1fm_val"]
    XTS = p.sems["XT"]
    banks = [psA[0], psA[1], psC[0], psC[1]]
    bnames = ["psA0", "psA1", "psC0", "psC1"]
    dbanks = None

    xlo_fm = p.alloc("xlo_fm", [128, 16 * TOK], BF16)
    gw_sb = p.alloc("gw_sb", [EXP, DM], F32)
    gwp = p.alloc("gwp", [128, 2 * 16 * 128], BF16)
    gwt_f = p.alloc("gwt_f", [128, 16 * 16], F32)
    lgt = p.alloc("lgt", [128, 8 * 16], F32)
    bsd = p.alloc("bsd", [128, 8 * 16], F32)
    eb_bc = p.alloc("eb_bc", [128, 16], F32)
    m8 = p.alloc("m8", [128, 8], F32)
    rsc = p.alloc("rsc", [128, 16], F32)
    rcol = p.alloc("rcol", [128, 4], F32)
    comb = p.alloc("comb", [128, 2 * 128], BF16)
    p.wait("sync", "XT", x1fm_val)
    for tb in range(8):
        for k in range(16):
            p.emit("sync", lambda e, k=k, tb=tb, s=XTS:
                   e.dma_start(out=xlo_fm[:, k * TOK + tb * 128: k * TOK + (tb + 1) * 128],
                               in_=xlo_dram[:, tb * DM + k * 128: tb * DM + (k + 1) * 128],
                               transpose=True).then_inc(s, 16))
            p.bump("XT", 16)
    xlo_val = p.cnt["XT"]
    p.wait("gpsimd", "VE", p.cnt["VE"])
    lgw = p.dma("gpsimd", gw_sb[:, :], GW[:, :], "LCB")
    leb = p.dma("gpsimd", eb_bc[:, 0:16],
                EB[:, :].rearrange("e o -> (e o)").partition_broadcast(128), "LCB")
    gwh = p.alloc("gwh", [EXP, DM], BF16)
    gwl = p.alloc("gwl", [EXP, DM], BF16)
    p.wait("vector", "LCB", lgw)
    vop(lambda e: e.memset(gwp[:, :], 0.0))
    vop(lambda e: e.tensor_copy(gwh[:, :], gw_sb[:, :]))
    vsync()
    vgl = vop(lambda e: e.tensor_tensor(gwl[:, :], gw_sb[:, :], gwh[:, :], op=ALU.subtract))
    p.wait("sync", "VE", vgl)
    for k in range(16):
        for hi in range(2):
            srcb = gwh if hi == 0 else gwl
            p.emit("sync", lambda e, k=k, hi=hi, srcb=srcb, s=XTS:
                   e.dma_start(out=gwp[:, (hi * 16 + k) * 128: (hi * 16 + k) * 128 + 16],
                               in_=srcb[:, k * 128:(k + 1) * 128], transpose=True).then_inc(s, 16))
            p.bump("XT", 16)
    gw_val = p.cnt["XT"]
    p.wait("tensor", "XT", max(xlo_val, gw_val))
    p.wait("tensor", "LCB", leb)
    comb_st = 0
    for tb in range(8):
        bi = tb % 2
        ps = banks[bi]
        p.gwait("tensor", bnames[bi])
        nmm = 0
        for k in range(16):
            for (xa, wa) in ((0, 0), (0, 1), (1, 0)):
                xs = x1_fm if xa == 0 else xlo_fm
                p.op("tensor", lambda e, ps=ps, k=k, tb=tb, xs=xs, wa=wa, nmm=nmm:
                     e.matmul(ps[:, 0:16],
                              xs[:, k * TOK + tb * 128: k * TOK + (tb + 1) * 128],
                              gwp[:, (wa * 16 + k) * 128: (wa * 16 + k) * 128 + 16],
                              start=(nmm == 0), stop=(nmm == 47)),
                     inc=("PE", 1) if nmm == 47 else None)
                nmm += 1
        pev = p.cnt["PE"]
        p.wait("vector", "PE", pev)
        p.spacer("vector", 256)
        vop(lambda e, ps=ps, tb=tb: e.tensor_copy(lgt[:, tb * 16:(tb + 1) * 16], ps[:, 0:16]))
        vsync()
        vv = vop(lambda e, ps=ps, tb=tb: e.tensor_tensor(bsd[:, tb * 16:(tb + 1) * 16],
                                                         ps[:, 0:16], eb_bc[:, 0:16],
                                                         op=ALU.add))
        p.gset(bnames[bi], "VE", vv)
        vsync()
        vop(lambda e, tb=tb: e.max(m8[:, :], bsd[:, tb * 16:(tb + 1) * 16]))
        vsync()
        vop(lambda e, tb=tb: e.tensor_scalar(rsc[:, :], bsd[:, tb * 16:(tb + 1) * 16],
                                             m8[:, 1:2], None, op0=ALU.is_ge))
        vsync()
        vop(lambda e, tb=tb: e.reduce_max(rcol[:, 0:1], lgt[:, tb * 16:(tb + 1) * 16],
                                          axis=mybir.AxisListType.X))
        vsync()
        vop(lambda e: e.tensor_scalar(rcol[:, 1:2], rcol[:, 0:1], -1.0, None, op0=ALU.mult))
        vsync()
        vq = p.cnt["VE"]
        p.wait("scalar", "VE", vq)
        sq = sop(lambda e, tb=tb: e.activation(lgt[:, tb * 16:(tb + 1) * 16],
                                               lgt[:, tb * 16:(tb + 1) * 16],
                                               AF.Exp, bias=rcol[:, 1:2]))
        p.wait("vector", "SE", sq)
        vop(lambda e, tb=tb: e.tensor_tensor(lgt[:, tb * 16:(tb + 1) * 16],
                                             lgt[:, tb * 16:(tb + 1) * 16], rsc[:, 0:16],
                                             op=ALU.mult))
        vsync()
        vop(lambda e, tb=tb: e.reduce_sum(rcol[:, 2:3], lgt[:, tb * 16:(tb + 1) * 16],
                                          axis=mybir.AxisListType.X))
        vsync()
        vop(lambda e: e.reciprocal(rcol[:, 3:4], rcol[:, 2:3]))
        vsync()
        p.gwait("vector", "comb_w")
        vop(lambda e: e.memset(comb[:, 0:128], 0.0))
        vsync()
        vcb = vop(lambda e, tb=tb:
                  e.tensor_scalar(comb[:, 0:16],
                                  lgt[:, tb * 16:(tb + 1) * 16], rcol[:, 3:4], None,
                                  op0=ALU.mult))
        p.wait("sync", "VE", vcb)
        p.gwait("sync", "comb_t")
        p.emit("sync", lambda e, s=XTS:
               e.dma_start(out=comb[:, 128:256],
                           in_=comb[:, 0:128], transpose=True).then_inc(s, 16))
        ctv = p.bump("XT", 16)
        p.gset("comb_w", "XT", ctv)
        p.wait("gpsimd", "XT", ctv)
        comb_st = p.dma("gpsimd", comb_dram[:, tb * 128:(tb + 1) * 128],
                        comb[0:16, 128:256], "CCS")
        p.gset("comb_t", "CCS", comb_st)
    if "comb" in dbg:
        p.wait("gpsimd", "CCS", comb_st)
        p.dma("gpsimd", dbg["comb"][0:16, :].rearrange("e c -> e c"), comb_dram[:, :], "DBS")
        p.wait("vector", "DBS", p.cnt["DBS"])
    for nm in ("gwl", "gwh", "comb", "rcol", "rsc", "m8", "eb_bc", "bsd", "lgt",
               "gwt_f", "gwp", "gw_sb", "xlo_fm"):
        p.free(nm)

    p.free("psTR")
    psD = p.alloc("psD", [128, 512], F32, psum=True)
    dbanks = [psA[0], psA[1], psC[0], psC[1], psY, psS, psG, psD]
    dbnames = ["psA0", "psA1", "psC0", "psC1", "psY", "psS", "psG", "psD"]
    hbuf = p.alloc("hbuf", [128, 2 * FC * TOK], BF16)
    acc = p.alloc("acc", [128, 16 * TOK], F32)
    wgst = p.alloc("wgst", [128, 2 * DM], BF16)
    wust = p.alloc("wust", [128, 2 * DM], BF16)
    wdst = p.alloc("wdst", [128, 2 * FC * 512], BF16)
    cbc = p.alloc("cbc", [128, 2 * TOK], BF16)
    sg1 = p.alloc("sg1", [128, 2 * 512], BF16)
    sg2 = p.alloc("sg2", [128, 2 * 512], BF16)
    groups = [(0, 1), (2, 3), (4, 5), (6, 7), (8, 9), (10, 11), (12, 13), (14, 15), (16,)]
    p.wait("gpsimd", "CCS", max(comb_st, x1lo_st))
    for gi, grp in enumerate(groups):
        for ei, e_ in enumerate(grp):
            sl = e_ % 2
            if e_ < EXP:
                p.gwait("gpsimd", f"cbc{sl}")
                cb_v = p.dma("gpsimd", cbc[:, sl * TOK:(sl + 1) * TOK],
                             comb_dram[e_:e_ + 1, :].rearrange("o c -> (o c)").partition_broadcast(128),
                             "LCB")
            else:
                cb_v = None
            for fc in range(FC):
                wsl = fc % 2
                p.gwait("gpsimd", f"wg{wsl}")
                g_v = p.dma("gpsimd", wgst[:, wsl * DM:(wsl + 1) * DM],
                            WG[e_, fc, :, :, :].rearrange("p k c -> p (k c)"), f"LG{wsl}")
                u_v = p.dma("gpsimd", wust[:, wsl * DM:(wsl + 1) * DM],
                            WU[e_, fc, :, :, :].rearrange("p k c -> p (k c)"), f"LG{wsl}")
                p.wait("tensor", f"LG{wsl}", u_v)
                for tc in range(2):
                    big, biu = (tc, 2 + tc)
                    psg_, psu_ = banks[big], banks[biu]
                    p.gwait("tensor", bnames[big])
                    p.gwait("tensor", bnames[biu])
                    for k in range(16):
                        p.op("tensor", lambda e, ps=psg_, k=k, tc=tc, wsl=wsl:
                             e.matmul(ps[:, 0:512],
                                      wgst[:, wsl * DM + k * 128: wsl * DM + (k + 1) * 128],
                                      x1_fm[:, k * TOK + tc * 512: k * TOK + (tc + 1) * 512],
                                      start=(k == 0), stop=(k == 15)),
                             inc=("PE", 1) if k == 15 else None)
                    for k in range(16):
                        p.op("tensor", lambda e, ps=psu_, k=k, tc=tc, wsl=wsl:
                             e.matmul(ps[:, 0:512],
                                      wust[:, wsl * DM + k * 128: wsl * DM + (k + 1) * 128],
                                      x1_fm[:, k * TOK + tc * 512: k * TOK + (tc + 1) * 512],
                                      start=(k == 0), stop=(k == 15)),
                             inc=("PE", 1) if k == 15 else None)
                    pev = p.cnt["PE"]
                    p.wait("scalar", "PE", pev)
                    p.gwait("scalar", f"sg1{tc}")
                    sv = sop(lambda e, ps=psg_, tc=tc: e.activation(sg1[:, tc * 512:(tc + 1) * 512],
                                                                    ps[:, 0:512], AF.Silu))
                    p.gset(bnames[big], "SE", sv)
                    p.wait("vector", "SE", sv)
                    p.spacer("vector", 256)
                    if e_ < EXP:
                        p.wait("vector", "LCB", cb_v)
                        vu = vop(lambda e, ps=psu_, tc=tc, sl=sl:
                                 e.tensor_tensor(sg2[:, tc * 512:(tc + 1) * 512], ps[:, 0:512],
                                                 cbc[:, sl * TOK + tc * 512: sl * TOK + (tc + 1) * 512],
                                                 op=ALU.mult))
                        vsync()
                        vh = vop(lambda e, ei=ei, fc=fc, tc=tc:
                                 e.tensor_tensor(hbuf[:, (ei * FC + fc) * TOK + tc * 512:
                                                      (ei * FC + fc) * TOK + (tc + 1) * 512],
                                                 sg2[:, tc * 512:(tc + 1) * 512],
                                                 sg1[:, tc * 512:(tc + 1) * 512], op=ALU.mult))
                    else:
                        vh = vop(lambda e, ei=ei, fc=fc, tc=tc, ps=psu_:
                                 e.tensor_tensor(hbuf[:, (ei * FC + fc) * TOK + tc * 512:
                                                      (ei * FC + fc) * TOK + (tc + 1) * 512],
                                                 ps[:, 0:512],
                                                 sg1[:, tc * 512:(tc + 1) * 512], op=ALU.mult))
                    p.gset(bnames[biu], "VE", vh)
                    p.gset(f"sg1{tc}", "VE", vh)
                p.gset(f"wg{wsl}", "PE", p.cnt["PE"])
            if e_ < EXP:
                p.gset(f"cbc{sl}", "VE", p.cnt["VE"])
        ve_h = p.cnt["VE"]
        p.wait("tensor", "VE", ve_h)
        for dsl in range(4):
            for ei, e_ in enumerate(grp):
                wsl = ei
                p.gwait("gpsimd", f"wd{wsl}")
                d_v = p.dma("gpsimd", wdst[:, wsl * FC * 512:(wsl + 1) * FC * 512],
                            WD[e_, dsl, :, :, :, :].rearrange("p f d c -> p (f d c)"), f"LD{wsl}")
                p.wait("tensor", f"LD{wsl}", d_v)
                for tc in range(2):
                    for dj in range(4):
                        bi = tc * 4 + dj
                        ps = dbanks[bi]
                        first = (ei == 0)
                        last = (ei == len(grp) - 1)
                        if first:
                            p.gwait("tensor", dbnames[bi])
                        ng = len(grp)
                        for fc in range(FC):
                            p.op("tensor", lambda e, ps=ps, fc=fc, dj=dj, tc=tc, ei=ei, wsl=wsl, ng=ng:
                                 e.matmul(ps[:, 0:512],
                                          wdst[:, wsl * FC * 512 + (fc * 4 + dj) * 128:
                                               wsl * FC * 512 + (fc * 4 + dj) * 128 + 128],
                                          hbuf[:, (ei * FC + fc) * TOK + tc * 512:
                                               (ei * FC + fc) * TOK + (tc + 1) * 512],
                                          start=(ei == 0 and fc == 0),
                                          stop=(ei == ng - 1 and fc == FC - 1)),
                                 inc=("PE", 1) if fc == FC - 1 else None)
                        if last:
                            pev = p.cnt["PE"]
                            p.wait("vector", "PE", pev)
                            p.spacer("vector", 256)
                            col = (dsl * 4 + dj) * TOK + tc * 512
                            if gi == 0:
                                vv = vop(lambda e, ps=ps, col=col:
                                         e.tensor_copy(acc[:, col:col + 512], ps[:, 0:512]))
                            else:
                                vv = vop(lambda e, ps=ps, col=col:
                                         e.scalar_tensor_tensor(acc[:, col:col + 512], ps[:, 0:512],
                                                                1.0, acc[:, col:col + 512],
                                                                op0=ALU.mult, op1=ALU.add))
                            p.gset(dbnames[bi], "VE", vv)
                p.gset(f"wd{wsl}", "PE", p.cnt["PE"])
    ve_ffn = p.cnt["VE"]
    if "acc" in dbg:
        p.wait("gpsimd", "VE", ve_ffn)
        p.dma("gpsimd", dbg["acc"][:, :], acc[:, :], "DBS")
        p.wait("vector", "DBS", p.cnt["DBS"])

    for nm in ("sg2", "sg1", "cbc", "wdst", "wust", "wgst"):
        p.free(nm)
    accb = p.alloc("accb", [128, 16 * 128], BF16)
    accT = p.alloc("accT", [128, 2 * DM], BF16)
    x2t = p.alloc("x2t", [128, 2 * DM], F32)
    bnscr2 = p.alloc("bnscr2", [128, 24], F32)
    mv3 = p.alloc("mv3", [128, 2], F32)
    mc3 = p.alloc("mc3", [128, 2], F32)
    outs = p.alloc("outs", [128, 2 * DM], F32)
    p.wait("vector", "VE", ve_ffn)
    p.wait("gpsimd", "VE", ve_ffn)
    p.wait("sync", "VE", ve_ffn)
    for tb in range(8):
        sl = tb % 2
        p.gwait("vector", f"accb{sl}")
        for dc in range(16):
            vop(lambda e, dc=dc, tb=tb: e.tensor_copy(accb[:, dc * 128:(dc + 1) * 128],
                                                      acc[:, dc * TOK + tb * 128: dc * TOK + (tb + 1) * 128]))
        vsync()
        vab = p.cnt["VE"]
        p.wait("sync", "VE", vab)
        p.gwait("sync", f"accT{sl}")
        for dc in range(16):
            p.emit("sync", lambda e, dc=dc, sl=sl, s=XTS:
                   e.dma_start(out=accT[:, sl * DM + dc * 128: sl * DM + (dc + 1) * 128],
                               in_=accb[:, dc * 128:(dc + 1) * 128], transpose=True).then_inc(s, 16))
            p.bump("XT", 16)
        at_v = p.cnt["XT"]
        p.gset(f"accb{sl}", "XT", at_v)
        p.gwait("gpsimd", f"x2t{sl}")
        lx = p.dma("gpsimd", x2t[:, sl * DM:(sl + 1) * DM], x1_dram[:, tb * DM:(tb + 1) * DM],
                   f"LX{sl}")
        p.wait("vector", f"LX{sl}", lx)
        p.wait("vector", "XT", at_v)
        xs = x2t[:, sl * DM:(sl + 1) * DM]
        vv0 = vop(lambda e, xs=xs, sl=sl: e.tensor_tensor(xs, xs, accT[:, sl * DM:(sl + 1) * DM],
                                                          op=ALU.add))
        p.gset(f"accT{sl}", "VE", vv0)
        vsync()
        for q in range(4):
            vop(lambda e, xs=xs, q=q: e.bn_stats(bnscr2[:, q * 6:(q + 1) * 6],
                                                 xs[:, q * 512:(q + 1) * 512]))
            vsync()
        vop(lambda e: e.bn_aggr(mv3[:, :], bnscr2[:, :]))
        vsync()
        vop(lambda e: e.tensor_scalar(mc3[:, 0:1], mv3[:, 0:1], -1.0, None, op0=ALU.mult))
        vsync()
        vop(lambda e: e.tensor_scalar(mv3[:, 1:2], mv3[:, 1:2], 1.0, EPS,
                                      op0=ALU.mult, op1=ALU.add))
        vsync()
        v_mv = p.cnt["VE"]
        p.wait("scalar", "VE", v_mv)
        s1 = sop(lambda e: e.activation(mc3[:, 1:2], mv3[:, 1:2], AF.Ln))
        p.wait("scalar", "SE", s1)
        s2 = sop(lambda e: e.activation(mc3[:, 1:2], mc3[:, 1:2], AF.Exp, scale=-0.5))
        p.wait("vector", "SE", s2)
        p.gwait("vector", f"outs{sl}")
        vop(lambda e, xs=xs, sl=sl: e.scalar_tensor_tensor(outs[:, sl * DM:(sl + 1) * DM], xs,
                                                           1.0, xs, op0=ALU.bypass,
                                                           op1=ALU.bypass)) if False else None
        vop(lambda e, xs=xs: e.tensor_scalar(xs, xs, mc3[:, 0:1], None, op0=ALU.add))
        vsync()
        vo = vop(lambda e, xs=xs, sl=sl: e.tensor_scalar(outs[:, sl * DM:(sl + 1) * DM], xs,
                                                         mc3[:, 1:2], None, op0=ALU.mult))
        p.wait("sync", "VE", vo)
        ost = p.dma("sync", out[tb * 128:(tb + 1) * 128, :], outs[:, sl * DM:(sl + 1) * DM], "STS")
        p.gset(f"outs{sl}", "STS", ost)
        p.gset(f"x2t{sl}", "VE", vo)



def _host_prep(inputs):
    f32 = np.float32
    u = np.asarray(inputs["L0_attn_u"], f32)
    Win = np.asarray(inputs["L0_attn_W_in"], f32)
    convw = np.asarray(inputs["L0_attn_conv_w"], f32)[:, 0, :]
    convb = np.asarray(inputs["L0_attn_conv_b"], f32)
    dtb = np.asarray(inputs["L0_attn_dt_bias"], f32)
    alog = np.asarray(inputs["L0_attn_A_log"], f32)
    Dv = np.asarray(inputs["L0_attn_D"], f32)
    normw = np.asarray(inputs["L0_attn_norm_w"], f32)
    Wout = np.asarray(inputs["L0_attn_W_out"], f32)
    gw = np.asarray(inputs["L0_ffn_gate_w"], f32)
    wg = np.asarray(inputs["L0_ffn_w_gate"], f32)
    wu = np.asarray(inputs["L0_ffn_w_up"], f32)
    wd = np.asarray(inputs["L0_ffn_w_down"], f32)
    swg = np.asarray(inputs["L0_ffn_sw_gate"], f32)
    swu = np.asarray(inputs["L0_ffn_sw_up"], f32)
    swd = np.asarray(inputs["L0_ffn_sw_down"], f32)
    eb = np.asarray(inputs["L0_ffn_expert_bias"], f32)

    uf = np.ascontiguousarray(u.reshape(8192, DM))
    uT = np.ascontiguousarray(uf.T)

    WX = Win[DIN:DIN + 4352]
    WXT = np.ascontiguousarray(WX.reshape(NG, 128, 16, 128).transpose(0, 3, 2, 1))
    WDT = np.ascontiguousarray(Win[8448:8512].reshape(64, 16, 128).transpose(2, 1, 0))
    WZT = np.ascontiguousarray(Win[0:DIN].reshape(8, 512, 16, 128).transpose(0, 3, 2, 1))
    WOT = np.ascontiguousarray(Wout.T.reshape(32, 128, 16, 128).transpose(2, 1, 0, 3))
    CDIAG = np.zeros((NG, 128, 4, 128), f32)
    cwg = convw.reshape(NG, 128, 4)
    for rix in range(128):
        CDIAG[:, rix, :, rix] = cwg[:, rix, :]
    CB = np.ascontiguousarray(convb.reshape(NG, 128).T)
    NW = np.ascontiguousarray(normw.reshape(32, 128).T)
    WG17 = np.concatenate([wg, swg[None]], 0)
    WU17 = np.concatenate([wu, swu[None]], 0)
    WD17 = np.concatenate([wd, swd[None]], 0)
    WGt = np.ascontiguousarray(WG17.reshape(17, FC, 128, 16, 128).transpose(0, 1, 4, 3, 2))
    WUt = np.ascontiguousarray(WU17.reshape(17, FC, 128, 16, 128).transpose(0, 1, 4, 3, 2))
    WDt = np.ascontiguousarray(WD17.reshape(17, 4, 4, 128, FC, 128).transpose(0, 1, 5, 4, 2, 3))
    OH = np.zeros((24, 24 * 128), f32)
    for rix in range(24):
        OH[rix, rix * 128:(rix + 1) * 128] = 1.0
    common = dict(
        WXT=WXT, WDT=WDT, WZT=WZT, WOT=WOT, CDIAG=CDIAG, CB=CB, NW=NW,
        NWR=normw.reshape(1, DIN),
        DTB=dtb.reshape(NH, 1), ALOG=alog.reshape(NH, 1), DVEC=Dv.reshape(1, NH),
        GW=gw, EB=eb.reshape(EXP, 1), WG=WGt, WU=WUt, WD=WDt,
        IDENT=np.eye(128, dtype=f32), ONES1=np.ones((1, 128), f32), OH16=OH,
    )
    per_core = []
    for c in range(NCORES):
        g0 = c * TOK
        rank = c % 4
        if rank == 0:
            uTc = np.concatenate([np.zeros((DM, 3), f32), uT[:, g0:g0 + TOK]], axis=1)
        else:
            uTc = uT[:, g0 - 3:g0 + TOK]
        rsel = np.zeros((128, 4), f32)
        rsel[:, :rank] = 1.0
        m = dict(common)
        m["uT"] = np.ascontiguousarray(uTc)
        m["uTm"] = np.ascontiguousarray(uT[:, g0:g0 + TOK])
        m["UF"] = np.ascontiguousarray(uf[g0:g0 + TOK, :])
        m["RSEL"] = rsel
        per_core.append(m)
    return per_core


def kernel(**inputs):
    in_maps = _host_prep(inputs)
    nc = build()
    res = run_bass_kernel_spmd(nc, in_maps, core_ids=list(range(NCORES)))
    kernel.debug_results = res.results if DEBUG else None
    outs = [res.results[c]["out"] for c in range(NCORES)]
    return np.concatenate(outs, axis=0).reshape(2, 4096, DM).astype(np.float32)


kernel.debug_results = None


# version 03
# speedup vs baseline: 1.2047x; 1.2047x over previous
import numpy as np
import concourse.bass as bass
import concourse.mybir as mybir
from concourse.bass_utils import run_bass_kernel_spmd

F32 = mybir.dt.float32
BF16 = mybir.dt.bfloat16
I32 = mybir.dt.int32
AF = mybir.ActivationFunctionType
ALU = mybir.AluOpType

NCORES = 8
TOK = 1024
TOKH = TOK + 3
DM = 2048
DIN = 4096
NH = 64
PH = 64
NG = 34
CK = 8
EXP = 16
FF = 1408
FC = 11
EPS = 1e-5

MT_TAP = (16, 0, 1)
XP_TAP = 0
FLUSH = False
RACECHECK = True
NO_CC = False
DEBUG = []
SKIP_TAIL = False

ENGINES = ("sync", "tensor", "vector", "scalar", "gpsimd")


class P:
    def __init__(self, nc):
        self.nc = nc
        self.ops = {e: [] for e in ENGINES}
        self.sems = {}
        self.cnt = {}
        self._guards = {}
        self._gorder = []
        self.bg = {}

    def sem(self, name):
        g = self.nc.semaphore(name)
        h = g.__enter__()
        self._guards[name] = g
        self._gorder.append(name)
        self.sems[name] = h
        self.cnt[name] = 0
        return h

    def alloc(self, name, shape, dtype, psum=False):
        g = (self.nc.psum_tensor if psum else self.nc.sbuf_tensor)(name, shape, dtype)
        t = g.__enter__()
        self._guards[name] = g
        self._gorder.append(name)
        return t

    def free(self, name):
        self._guards.pop(name).__exit__(None, None, None)
        self._gorder.remove(name)

    def close(self):
        for n in reversed(self._gorder):
            self._guards[n].__exit__(None, None, None)
        self._guards = {}
        self._gorder = []

    def emit(self, eng, fn):
        self.ops[eng].append(fn)

    def bump(self, sem_name, k):
        self.cnt[sem_name] += k
        return self.cnt[sem_name]

    def dma(self, eng, out, in_, sem_name="LDG", slow=False):
        s = self.sems[sem_name]
        v = self.bump(sem_name, 16)
        if slow:
            self.emit(eng, lambda e, out=out, in_=in_, s=s:
                      e.dma_start(out=out, in_=in_, allow_slow_non_contiguous=True).then_inc(s, 16))
        else:
            self.emit(eng, lambda e, out=out, in_=in_, s=s:
                      e.dma_start(out=out, in_=in_).then_inc(s, 16))
        return v

    def wait(self, eng, sem_name, val):
        if val <= 0:
            return
        s = self.sems[sem_name]
        self.emit(eng, lambda e, s=s, val=val: e.wait_ge(s, val))

    def op(self, eng, fn, inc=None, flush=True):
        if inc is None:
            self.emit(eng, fn)
            return None
        s = self.sems[inc[0]]
        v = self.bump(inc[0], inc[1])
        if flush and FLUSH and eng in ("vector", "scalar", "tensor"):
            def emitfn(e, fn=fn, s=s, k=inc[1]):
                fn(e)
                e.drain().then_inc(s, k)
            self.emit(eng, emitfn)
        else:
            self.emit(eng, lambda e, fn=fn, s=s, k=inc[1]: fn(e).then_inc(s, k))
        return v

    def spacer(self, eng, cycles=512):
        if FLUSH:
            self.emit(eng, lambda e, c=cycles: e.nop(cycle_cnt=c, nofuse=True))

    def gwait(self, eng, name):
        for sem, val in self.bg.get(name, []):
            self.wait(eng, sem, val)

    def gset(self, name, sem, val):
        self.bg[name] = [(sem, val)]

    def gadd(self, name, sem, val):
        self.bg.setdefault(name, []).append((sem, val))


def build():
    nc = bass.Bass("TRN2", target_bir_lowering=False, dynamic_dma_scratch_size=6144,
                   detect_race_conditions=RACECHECK)
    p = P(nc)

    def din(name, shape):
        return nc.dram_tensor(name, shape, F32, kind="ExternalInput").ap()

    uT = din("uT", [DM, TOKH])
    uTm = din("uTm", [DM, TOK])
    UF = din("UF", [TOK, DM])
    NWR = din("NWR", [1, DIN])
    WXT = din("WXT", [NG, 128, 16, 128])
    WDT = din("WDT", [128, 16, 64])
    WZT = din("WZT", [8, 128, 16, 512])
    WOT = din("WOT", [16, 128, 32, 128])
    CDIAG = din("CDIAG", [NG, 128, 4, 128])
    CB = din("CB", [128, NG])
    NW = din("NW", [128, 32])
    DTB = din("DTB", [NH, 1])
    ALOG = din("ALOG", [NH, 1])
    DVEC = din("DVEC", [1, NH])
    GW = din("GW", [EXP, DM])
    EB = din("EB", [EXP, 1])
    WG = din("WG", [17, FC, 128, 16, 128])
    WU = din("WU", [17, FC, 128, 16, 128])
    WD = din("WD", [17, 4, 128, FC, 4, 128])
    IDENT = din("IDENT", [128, 128])
    ONES1 = din("ONES1", [1, 128])
    OH16 = din("OH16", [24, 24 * 128])
    RSEL = din("RSEL", [128, 4])

    out = nc.dram_tensor("out", [TOK, DM], F32, kind="ExternalOutput").ap()
    cc_in = nc.dram_tensor("cc_in", [128, 4104], F32)
    cc_out = nc.dram_tensor("cc_out", [512, 4104], F32)
    x1_dram = nc.dram_tensor("x1_dram", [128, 8 * DM], F32)
    comb_dram = nc.dram_tensor("comb_dram", [EXP, TOK], BF16)
    cum_dram = nc.dram_tensor("cum_dram", [NH, TOK], F32)

    dbg_shapes = {"xbc": [128, NG * TOK], "dt": [NH, TOK], "y": [128, CK * DIN],
                  "x1": [128, 16 * TOK], "comb": [128, 8 * 16], "acc": [128, 16 * TOK],
                  "sc": [128, 6 * 512], "bc": [128, 2 * TOK], "xp": [128, TOK],
                  "T": [128, NH * PH], "y0": [128, CK * DIN], "rel": [128, NH * PH],
                  "tri": [128, 9 * 128], "mt": [128, 5 * 128], "tt": [128, 8 * 144],
                  "yn": [128, CK * DIN]}
    dbg = {n: nc.dram_tensor("dbg_" + n, dbg_shapes[n], F32, kind="ExternalOutput").ap()
           for n in DEBUG}

    for s in ("LDG", "LDS", "PE", "VE", "SE", "GE", "CC", "STS",
              "LW0", "LW1", "LC0", "LC1", "LR", "LRS", "CST", "CCS", "DBS", "XT", "LZ0", "LZ1", "LO0", "LO1", "LU0", "LU1", "LG0", "LG1", "LD0", "LD1", "LCB", "LX0", "LX1"):
        p.sem(s)

    id_f = p.alloc("id_f", [128, 128], F32)
    id_b = p.alloc("id_b", [128, 128], BF16)
    triu = p.alloc("triu", [128, 128], F32)
    mask_i = p.alloc("mask_i", [128, 128], I32)
    ones_f = p.alloc("ones_f", [1, 128], F32)
    ones_c = p.alloc("ones_c", [128, 1], F32)
    ones_b = p.alloc("ones_b", [1, 128], BF16)
    oh_sb = p.alloc("oh_sb", [24, 24 * 128], F32)
    cb_sb = p.alloc("cb_sb", [128, NG], F32)
    nw_sb = p.alloc("nw_sb", [128, 32], F32)
    dtb_sb = p.alloc("dtb_sb", [NH, 1], F32)
    alog_sb = p.alloc("alog_sb", [NH, 1], F32)
    A_sb = p.alloc("A_sb", [NH, 1], F32)
    dv_sb = p.alloc("dv_sb", [1, NH], F32)
    rsel_sb = p.alloc("rsel_sb", [128, 4], F32)
    rsinv_sb = p.alloc("rsinv_sb", [128, 4], F32)
    u_fm = p.alloc("u_fm", [128, 16 * TOKH], BF16)
    y_sb = p.alloc("y_sb", [128, CK * DIN], BF16)
    BCp = p.alloc("BCp", [128, 2 * TOK], BF16)
    ab_h = p.alloc("ab_h", [NH, 24], F32)
    ab_T = p.alloc("ab_T", [24, NH], F32)
    aseg = p.alloc("aseg", [NH, 1], F32)
    azp = p.alloc("azp", [128, 8], F32)
    s_T = p.alloc("s_T", [128, CK * NH], F32)
    T0 = p.alloc("T0", [128, NH * PH], F32)
    T1 = p.alloc("T1", [128, NH * PH], F32)
    GmT = p.alloc("GmT", [128, CK * 128], F32)
    B_tm = p.alloc("B_tm", [128, CK * 128], BF16)
    cum_T = p.alloc("cum_T", [128, CK * NH], F32)
    evec_T = p.alloc("evec_T", [128, CK * NH], F32)
    ecum_T = p.alloc("ecum_T", [128, CK * NH], F32)
    w_T = p.alloc("w_T", [128, CK * NH], F32)
    adb = p.alloc("adb", [128, CK * NH], F32)
    D_bc = p.alloc("D_bc", [128, NH], F32)
    T_bf = p.alloc("T_bf", [128, NH * PH], BF16)
    wst = p.alloc("wst", [128, 2 * 2048], BF16)
    cdst = p.alloc("cdst", [128, 2 * 512], BF16)
    xpost = p.alloc("xpost", [128, 2 * TOK], BF16)
    xraw = p.alloc("xraw", [128, 2 * TOKH], BF16)
    dtraw = p.alloc("dtraw", [NH, TOKH], F32)
    dts = p.alloc("dts", [NH, TOK], F32)
    cum = p.alloc("cum", [NH, TOK], F32)
    evl = p.alloc("evl", [NH, TOK], F32)
    scr64 = p.alloc("scr64", [NH, TOK], F32)
    z64 = p.alloc("z64", [NH, 128], F32)
    ecm = dtraw[:, 0:TOK]

    psA = [p.alloc(f"psA{i}", [128, 512], F32, psum=True) for i in range(2)]
    psC = [p.alloc(f"psC{i}", [128, 512], F32, psum=True) for i in range(2)]
    psY = p.alloc("psY", [128, 512], F32, psum=True)
    psS = p.alloc("psS", [128, 512], F32, psum=True)
    psG = p.alloc("psG", [128, 512], F32, psum=True)
    psTR = p.alloc("psTR", [128, 128], F32, psum=True)

    def vop(fn):
        return p.op("vector", fn, inc=("VE", 1))

    def sop(fn):
        return p.op("scalar", fn, inc=("SE", 1))

    def top(fn):
        return p.op("tensor", fn, inc=("PE", 1))

    def vsync():
        p.wait("vector", "VE", p.cnt["VE"])

    for dst, src in ((id_f, IDENT), (ones_f, ONES1), (oh_sb, OH16), (cb_sb, CB),
                     (nw_sb, NW), (dtb_sb, DTB), (alog_sb, ALOG), (dv_sb, DVEC),
                     (rsel_sb, RSEL)):
        p.dma("sync", dst[:, :], src[:, :], "LDS")
    lds_c = p.cnt["LDS"]

    for k in range(16):
        p.dma("gpsimd", u_fm[:, k * TOKH:(k + 1) * TOKH], uT[k * 128:(k + 1) * 128, :], "LDG")
    ldg_ufm = p.cnt["LDG"]

    ge_mask = p.op("gpsimd", lambda e: e.iota(mask_i[:, :], pattern=[[1, 128]], base=0,
                                              channel_multiplier=-1), inc=("GE", 1))
    p.wait("vector", "LDS", lds_c)
    p.op("vector", lambda e: e.tensor_copy(id_b[:, :], id_f[:, :]))
    p.op("vector", lambda e: e.memset(ones_c[:, :], 1.0))
    p.op("vector", lambda e: e.tensor_copy(ones_b[:, :], ones_f[:, :]))
    p.op("vector", lambda e: e.memset(z64[:, :], 0.0))
    p.op("vector", lambda e: e.tensor_scalar(rsinv_sb[:, :], rsel_sb[:, :], -1.0, 1.0,
                                             op0=ALU.mult, op1=ALU.add))
    p.wait("vector", "GE", ge_mask)
    ve_c0 = vop(lambda e: e.tensor_scalar(triu[:, :], mask_i[:, :], 0.0, None, op0=ALU.is_ge))
    p.wait("scalar", "LDS", lds_c)
    se_A = sop(lambda e: e.activation(A_sb[:, :], alog_sb[:, :], AF.Exp))
    p.wait("vector", "SE", se_A)
    ve_A = vop(lambda e: e.tensor_scalar(A_sb[:, :], A_sb[:, :], -1.0, None, op0=ALU.mult))
    p.wait("tensor", "LDS", lds_c)
    pe_dbc = top(lambda e: e.matmul(psG[:, 0:NH], ones_f[0:1, :], dv_sb[0:1, :],
                                    start=True, stop=True))
    p.wait("vector", "PE", pe_dbc)
    p.spacer("vector")
    ve_dbc = vop(lambda e: e.tensor_copy(D_bc[:, :], psG[:, 0:NH]))
    p.gset("psG", "VE", ve_dbc)

    TCH = [(0, 512), (512, 512), (1024, 3)]

    def in_proj_group(cc, wslot, dst_fn, m=128, raw_name=None):
        if cc < NG:
            src = WXT[cc, :, :, :].rearrange("p k c -> p (k c)")
            wcols = 2048
        else:
            src = WDT[:, :, :].rearrange("p k c -> p (k c)")
            wcols = 16 * 64
        p.gwait("gpsimd", f"wst{wslot}")
        wv = p.dma("gpsimd", wst[:, wslot * 2048: wslot * 2048 + wcols], src, f"LW{wslot}")
        p.wait("tensor", f"LW{wslot}", wv)
        last_pe = 0
        for ti, (c0, w) in enumerate(TCH):
            ps = psA[ti % 2]
            p.gwait("tensor", f"psA{ti % 2}")
            for k in range(16):
                lhs = wst[:, wslot * 2048 + k * m: wslot * 2048 + (k + 1) * m]
                rhs = u_fm[:, k * TOKH + c0: k * TOKH + c0 + w]
                pe = p.op("tensor", lambda e, ps=ps, lhs=lhs, rhs=rhs, k=k, m=m, w=w:
                          e.matmul(ps[0:m, 0:w], lhs, rhs, start=(k == 0), stop=(k == 15)),
                          inc=("PE", 1) if k == 15 else None)
            last_pe = p.cnt["PE"]
            p.wait("vector", "PE", last_pe)
            if raw_name:
                p.gwait("vector", raw_name)
            dst = dst_fn(c0, w)
            vv = vop(lambda e, ps=ps, dst=dst, m=m, w=w: e.tensor_copy(dst, ps[0:m, 0:w]))
            p.gset(f"psA{ti % 2}", "VE", vv)
        p.gset(f"wst{wslot}", "PE", last_pe)
        if raw_name:
            p.bg[raw_name] = []
        return p.cnt["VE"]

    def conv_group(g, raw_slot, dst_fn, post_name=None, ve_ready=0):
        cs = g % 2
        p.gwait("gpsimd", f"cdst{cs}")
        cv = p.dma("gpsimd", cdst[:, cs * 512:(cs + 1) * 512],
                   CDIAG[g, :, :, :].rearrange("p k c -> p (k c)"), f"LC{cs}")
        p.wait("tensor", f"LC{cs}", cv)
        p.wait("tensor", "VE", ve_ready)
        last_pe = 0
        for ti in range(2):
            ps = psC[ti]
            p.gwait("tensor", f"psC{ti}")
            for k in range(4):
                lhs = cdst[:, cs * 512 + k * 128: cs * 512 + (k + 1) * 128]
                rhs = xraw[:, raw_slot * TOKH + ti * 512 + k: raw_slot * TOKH + ti * 512 + k + 512]
                p.op("tensor", lambda e, ps=ps, lhs=lhs, rhs=rhs, k=k:
                     e.matmul(ps[:, 0:512], lhs, rhs, start=(k == 0), stop=(k == 3)),
                     inc=("PE", 1) if k == 3 else None)
            last_pe = p.cnt["PE"]
            p.wait("scalar", "PE", last_pe)
            p.spacer("scalar", 256)
            if post_name:
                p.gwait("scalar", post_name)
            dst = dst_fn(ti * 512, 512)
            sv = sop(lambda e, ps=ps, dst=dst, g=g:
                     e.activation(dst, ps[:, 0:512], AF.Silu, bias=cb_sb[:, g:g + 1]))
            p.gset(f"psC{ti}", "SE", sv)
        p.gset(f"cdst{cs}", "PE", last_pe)
        p.gadd(f"xraw{raw_slot}", "PE", last_pe)
        if post_name:
            p.bg[post_name] = []
        return p.cnt["SE"]

    p.wait("tensor", "LDG", ldg_ufm)
    p.wait("tensor", "VE", ve_c0)

    ve_r = in_proj_group(32, 0, lambda c0, w: xraw[:, c0:c0 + w], raw_name="xraw0")
    conv_group(32, 0, lambda c0, w: BCp[:, c0:c0 + w], ve_ready=ve_r)
    ve_r = in_proj_group(33, 1, lambda c0, w: xraw[:, TOKH + c0:TOKH + c0 + w], raw_name="xraw1")
    conv_group(33, 1, lambda c0, w: BCp[:, TOK + c0:TOK + c0 + w], ve_ready=ve_r)
    in_proj_group(NG, 0, lambda c0, w: dtraw[:, c0:c0 + w], m=64, raw_name="xraw0")
    se_bc = p.cnt["SE"]
    ve_dt_ready = p.cnt["VE"]
    if "bc" in dbg:
        p.wait("gpsimd", "SE", se_bc)
        p.dma("gpsimd", dbg["bc"][:, :], BCp[:, :], "DBS")
        p.wait("tensor", "DBS", p.cnt["DBS"])

    p.wait("tensor", "SE", se_bc)
    p.wait("sync", "SE", se_bc)
    for ck in range(CK):
        p.emit("sync", lambda e, ck=ck, s=p.sems["XT"]:
               e.dma_start(out=B_tm[:, ck * 128:(ck + 1) * 128],
                           in_=BCp[:, ck * 128:(ck + 1) * 128], transpose=True).then_inc(s, 16))
        p.bump("XT", 16)
    btm_xt = p.cnt["XT"]
    for ck in range(CK):
        p.gwait("tensor", "psG")
        pe_g = top(lambda e, ck=ck: e.matmul(psG[:, 0:128], BCp[:, ck * 128:(ck + 1) * 128],
                                             BCp[:, TOK + ck * 128: TOK + (ck + 1) * 128],
                                             start=True, stop=True))
        p.wait("vector", "PE", pe_g)
        p.spacer("vector")
        vv = vop(lambda e, ck=ck: e.tensor_tensor(GmT[:, ck * 128:(ck + 1) * 128], psG[:, 0:128],
                                                  triu[:, :], op=ALU.mult))
        p.gset("psG", "VE", vv)

    dt_o = dtraw[:, 3:3 + TOK]
    p.wait("vector", "VE", ve_dt_ready)
    v2 = vop(lambda e: e.tensor_scalar(scr64[:, :], dt_o, dtb_sb[:, 0:1], None, op0=ALU.add))
    p.wait("scalar", "VE", v2)
    sab = sop(lambda e: e.activation(dts[:, :], scr64[:, :], AF.Abs))
    p.wait("scalar", "SE", sab)
    s1 = sop(lambda e: e.activation(evl[:, :], dts[:, :], AF.Exp, scale=-1.0))
    p.wait("scalar", "SE", s1)
    s2 = sop(lambda e: e.activation(ecm[:, :], evl[:, :], AF.Ln, bias=1.0))
    p.wait("vector", "SE", s2)
    vop(lambda e: e.tensor_scalar(evl[:, :], scr64[:, :], 0.0, None, op0=ALU.max))
    vsync()
    vop(lambda e: e.tensor_tensor(dts[:, :], ecm[:, :], evl[:, :], op=ALU.add))
    vsync()
    p.wait("vector", "VE", ve_A)
    vop(lambda e: e.tensor_scalar(scr64[:, :], dts[:, :], A_sb[:, 0:1], None, op0=ALU.mult))
    vsync()
    for ck in range(CK):
        vop(lambda e, ck=ck: e.tensor_tensor_scan(cum[:, ck * 128:(ck + 1) * 128],
                                                  scr64[:, ck * 128:(ck + 1) * 128],
                                                  z64[:, :], initial=0.0,
                                                  op0=ALU.add, op1=ALU.bypass))
    vsync()
    v4 = p.cnt["VE"]
    p.wait("scalar", "VE", v4)
    s3 = sop(lambda e: e.activation(evl[:, :], dts[:, :], AF.Ln))
    p.wait("vector", "SE", s3)
    v5 = vop(lambda e: e.tensor_tensor(evl[:, :], evl[:, :], cum[:, :], op=ALU.subtract))
    p.wait("scalar", "VE", v5)
    s4 = sop(lambda e: e.activation(ecm[:, :], cum[:, :], AF.Exp))
    lad = cum[:, 127::128]
    p.wait("vector", "SE", s4)
    vop(lambda e: e.tensor_tensor_scan(scr64[:, 0:CK], lad, z64[:, 0:CK], initial=0.0,
                                       op0=ALU.add, op1=ALU.bypass))
    vsync()
    vop(lambda e: e.tensor_tensor(scr64[:, 8:16], scr64[:, 0:CK], lad, op=ALU.subtract))
    vsync()
    vop(lambda e: e.tensor_copy(ab_h[:, 16:24], lad))
    vsync()
    v6 = p.cnt["VE"]
    p.wait("scalar", "VE", v6)
    sop(lambda e: e.activation(ab_h[:, 0:8], lad, AF.Exp))
    sop(lambda e: e.activation(ab_h[:, 8:16], scr64[:, 8:16], AF.Exp))
    s6 = sop(lambda e: e.activation(aseg[:, 0:1], scr64[:, 7:8], AF.Exp))

    p.wait("tensor", "SE", s6)
    tr_jobs = []
    for ck in range(CK):
        tr_jobs.append((cum[:, ck * 128:(ck + 1) * 128], cum_T, ck))
        tr_jobs.append((evl[:, ck * 128:(ck + 1) * 128], evec_T, ck))
        tr_jobs.append((ecm[:, ck * 128:(ck + 1) * 128], ecum_T, ck))
    for src, dstt, ck in tr_jobs:
        p.gwait("tensor", "psG")
        pe_t = top(lambda e, src=src: e.transpose(psG[0:128, 0:64], src[0:64, :],
                                                  id_f[0:64, 0:64]))
        p.wait("vector", "PE", pe_t)
        p.spacer("vector")
        vv = vop(lambda e, dstt=dstt, ck=ck: e.tensor_copy(dstt[:, ck * NH:(ck + 1) * NH],
                                                           psG[:, 0:64]))
        p.gset("psG", "VE", vv)
    p.gwait("tensor", "psG")
    pe_abt = top(lambda e: e.transpose(psG[0:24, 0:64], ab_h[0:64, :], id_f[0:64, 0:64]))
    p.wait("vector", "PE", pe_abt)
    p.spacer("vector")
    ve_abt = vop(lambda e: e.tensor_copy(ab_T[:, :], psG[0:24, 0:64]))
    p.gset("psG", "VE", ve_abt)
    for ck in range(CK):
        p.gwait("tensor", "psG")
        top(lambda e, ck=ck: e.matmul(psG[:, 0:NH], oh_sb[:, ck * 128:(ck + 1) * 128],
                                      ab_T[:, :], start=True, stop=True))
        pe_b2 = top(lambda e, ck=ck: e.matmul(psG[:, 64:64 + NH],
                                              oh_sb[:, (8 + ck) * 128:(9 + ck) * 128],
                                              ab_T[:, :], start=True, stop=True))
        pe_b3 = top(lambda e, ck=ck: e.matmul(psG[:, 128:128 + NH],
                                              oh_sb[:, (16 + ck) * 128:(17 + ck) * 128],
                                              ab_T[:, :], start=True, stop=True))
        p.wait("vector", "PE", pe_b3)
        p.spacer("vector")
        vop(lambda e, ck=ck: e.tensor_copy(adb[:, ck * NH:(ck + 1) * NH], psG[:, 0:NH]))
        vop(lambda e, ck=ck: e.tensor_tensor(s_T[:, ck * NH:(ck + 1) * NH],
                                             ecum_T[:, ck * NH:(ck + 1) * NH],
                                             psG[:, 64:64 + NH], op=ALU.mult))
        vv = vop(lambda e, ck=ck: e.tensor_tensor(w_T[:, ck * NH:(ck + 1) * NH],
                                                  evec_T[:, ck * NH:(ck + 1) * NH],
                                                  psG[:, 128:128 + NH], op=ALU.add))
        p.gset("psG", "VE", vv)
        p.wait("scalar", "VE", vv)
        sop(lambda e, ck=ck: e.activation(w_T[:, ck * NH:(ck + 1) * NH],
                                          w_T[:, ck * NH:(ck + 1) * NH], AF.Exp))

    p.wait("gpsimd", "VE", p.cnt["VE"])
    p.wait("gpsimd", "SE", p.cnt["SE"])
    cum_st = p.dma("gpsimd", cum_dram[:, :], cum[:, :], "CST")

    if "tri" in dbg:
        p.wait("sync", "VE", p.cnt["VE"])
        p.dma("sync", dbg["tri"][:, 0:128], triu[:, :], "STS")
        p.dma("sync", dbg["tri"][:, 128:9 * 128], GmT[:, :], "STS")
        p.wait("vector", "STS", p.cnt["STS"])
    if "dt" in dbg:
        p.wait("sync", "VE", p.cnt["VE"])
        p.wait("sync", "SE", p.cnt["SE"])
        p.dma("sync", dbg["dt"][:, :], dts[:, :], "STS")
        p.wait("vector", "STS", p.cnt["STS"])
    if "sc" in dbg:
        p.wait("sync", "VE", p.cnt["VE"])
        p.wait("sync", "SE", p.cnt["SE"])
        for i, tt in enumerate((cum_T, evec_T, ecum_T, w_T, adb, s_T)):
            p.dma("sync", dbg["sc"][:, i * 512:(i + 1) * 512], tt[:, :], "STS")
        p.wait("vector", "STS", p.cnt["STS"])
        p.wait("scalar", "STS", p.cnt["STS"])

    for nm in ("z64", "scr64", "evl", "cum", "dts", "dtraw"):
        p.free(nm)
    Mt = p.alloc("Mt", [128, 2 * 128], BF16)
    xtm = p.alloc("xtm", [128, 2 * PH], BF16)
    xwtm = p.alloc("xwtm", [128, 2 * PH], BF16)
    xdsc = p.alloc("xdsc", [128, 2 * PH], BF16)
    p.op("vector", lambda e: e.memset(T0[:, :], 0.0))
    vz = vop(lambda e: e.memset(T_bf[:, :], 0.0))

    dbg_mt_sb = p.alloc("dbg_mt_sb", [128, 5 * 128], F32) if "mt" in dbg else None
    tt_sb = p.alloc("tt_sb", [128, 8 * 144], F32) if "tt" in dbg else None
    CT_bc = p.alloc("CT_bc", [128, 2 * TOK], F32)
    xtm2 = p.alloc("xtm2", [128, 2 * 128], BF16)
    E_a = p.alloc("E_a", [128, 2 * 128], F32)
    E_b = p.alloc("E_b", [128, 2 * 128], F32)
    Ea_se = {0: 0, 1: 0}
    Eb_ve = {0: 0, 1: 0}
    rowC_pe = {0: 0, 1: 0}
    row_ld = 0
    xt_guard = {0: 0, 1: 0}
    se_conv_g = {}

    def prep_group(g):
        rs = g % 2
        ve_r = in_proj_group(g, rs, lambda c0, w, rs=rs:
                             xraw[:, rs * TOKH + c0: rs * TOKH + c0 + w],
                             raw_name=f"xraw{rs}")
        se_conv_g[g] = conv_group(g, rs,
                                  lambda c0, w, rs=rs: xpost[:, rs * TOK + c0: rs * TOK + c0 + w],
                                  post_name=f"xpost{rs}", ve_ready=ve_r)

    prep_group(0)
    for g in range(32):
        rs = g % 2
        if g + 1 < 32:
            prep_group(g + 1)
        se_conv = se_conv_g[g]
        p.wait("gpsimd", "CST", cum_st)
        p.gwait("gpsimd", "rowAB")
        row_ld = p.dma("gpsimd", CT_bc[:, :],
                       cum_dram[2 * g:2 * g + 2, :].rearrange("h c -> (h c)").partition_broadcast(128),
                       "LR")
        pe_lastx = 0
        xt_val = {}
        p.wait("sync", "SE", se_conv)
        for cc0 in (0, 1):
            p.wait("sync", "PE", xt_guard[cc0 % 2])
            p.emit("sync", lambda e, rs=rs, cc0=cc0, s=p.sems["XT"]:
                   e.dma_start(out=xtm2[:, (cc0 % 2) * 128:(cc0 % 2) * 128 + 128],
                               in_=xpost[:, rs * TOK + cc0 * 128: rs * TOK + (cc0 + 1) * 128],
                               transpose=True).then_inc(s, 16))
            xt_val[cc0] = p.bump("XT", 16)
        for ck in range(CK):
            Tsrc = T0 if ck % 2 == 0 else T1
            Tdst = T1 if ck % 2 == 0 else T0
            xpar = ck % 2
            for hs in (0, 1):
                h = 2 * g + hs
                par = hs
                p.wait("vector", "XT", xt_val[ck])
                p.gwait("vector", f"xtm{hs}")
                ve_xt = vop(lambda e, hs=hs, xpar=xpar:
                            e.tensor_copy(xtm[:, hs * PH:(hs + 1) * PH],
                                          xtm2[:, xpar * 128 + hs * PH: xpar * 128 + (hs + 1) * PH]))
                ve_xw = vop(lambda e, hs=hs, xpar=xpar, ck=ck, h=h:
                            e.tensor_scalar(xwtm[:, hs * PH:(hs + 1) * PH],
                                            xtm2[:, xpar * 128 + hs * PH: xpar * 128 + (hs + 1) * PH],
                                            w_T[:, ck * NH + h: ck * NH + h + 1], None,
                                            op0=ALU.mult))
                p.wait("vector", "LR", row_ld)
                p.wait("vector", "SE", Ea_se[par])
                ve_ts = vop(lambda e, par=par, ck=ck, h=h, hs=hs:
                            e.tensor_scalar(E_a[:, par * 128:(par + 1) * 128],
                                            CT_bc[:, hs * TOK + ck * 128: hs * TOK + (ck + 1) * 128],
                                            evec_T[:, ck * NH + h: ck * NH + h + 1], 8.0,
                                            op0=ALU.add, op1=ALU.min))
                p.wait("scalar", "VE", ve_ts)
                p.wait("scalar", "VE", Eb_ve[par])
                se_e = sop(lambda e, par=par: e.activation(E_b[:, par * 128:(par + 1) * 128],
                                                           E_a[:, par * 128:(par + 1) * 128],
                                                           AF.Exp))
                Ea_se[par] = se_e
                p.wait("vector", "SE", se_e)
                ve_m = vop(lambda e, hs=hs, par=par, ck=ck:
                           e.tensor_tensor(Mt[:, hs * 128:(hs + 1) * 128],
                                           E_b[:, par * 128:(par + 1) * 128],
                                           GmT[:, ck * 128:(ck + 1) * 128], op=ALU.mult))
                Eb_ve[par] = ve_m
                p.wait("tensor", "VE", ve_m)
                p.gwait("tensor", "psY")
                p.op("tensor", lambda e, hs=hs: e.matmul(psY[:, 0:PH], Mt[:, hs * 128:(hs + 1) * 128],
                                                         xtm[:, hs * PH:(hs + 1) * PH],
                                                         start=True, stop=False))
                pe_y = top(lambda e, ck=ck, h=h:
                           e.matmul(psY[:, 0:PH], BCp[:, TOK + ck * 128: TOK + (ck + 1) * 128],
                                    T_bf[:, h * PH:(h + 1) * PH], start=False, stop=True))
                p.gwait("tensor", "psS")
                pe_s = top(lambda e, ck=ck, hs=hs:
                           e.matmul(psS[:, 0:PH], B_tm[:, ck * 128:(ck + 1) * 128],
                                    xwtm[:, hs * PH:(hs + 1) * PH], start=True, stop=True))
                p.gset(f"xtm{hs}", "PE", pe_s)
                if hs == 1:
                    xt_guard[xpar] = pe_s
                    if ck + 2 < CK:
                        p.wait("sync", "PE", pe_s)
                        p.emit("sync", lambda e, rs=rs, cc=ck + 2, xpar=xpar, s=p.sems["XT"]:
                               e.dma_start(out=xtm2[:, xpar * 128: xpar * 128 + 128],
                                           in_=xpost[:, rs * TOK + cc * 128: rs * TOK + (cc + 1) * 128],
                                           transpose=True).then_inc(s, 16))
                        xt_val[ck + 2] = p.bump("XT", 16)
                p.wait("vector", "PE", pe_s)
                p.spacer("vector", 256)
                if (g, ck, hs) == MT_TAP and "mt" in dbg:
                    vop(lambda e: e.tensor_copy(dbg_mt_sb[:, 0:64], psS[:, 0:PH]))
                    vop(lambda e, hs=hs: e.tensor_copy(dbg_mt_sb[:, 64:128],
                                                       xwtm[:, hs * PH:(hs + 1) * PH]))
                    vop(lambda e: e.tensor_copy(dbg_mt_sb[:, 128:192], psY[:, 0:PH]))
                    vop(lambda e, hs=hs: e.tensor_copy(dbg_mt_sb[:, 192:256],
                                                       xtm[:, hs * PH:(hs + 1) * PH]))
                    vop(lambda e, hs=hs: e.tensor_copy(dbg_mt_sb[:, 256:384],
                                                       Mt[:, hs * 128:(hs + 1) * 128]))
                    vop(lambda e, ck=ck: e.tensor_copy(dbg_mt_sb[:, 384:512],
                                                       B_tm[:, ck * 128:(ck + 1) * 128]))
                    vmt = vop(lambda e, hs=hs: e.tensor_copy(dbg_mt_sb[:, 512:576], xtm[:, hs * PH:(hs + 1) * PH]))
                    p.wait("sync", "VE", vmt)
                    p.dma("sync", dbg["mt"][:, :], dbg_mt_sb[:, :], "STS")
                    p.wait("vector", "STS", p.cnt["STS"])
                vop(lambda e, hs=hs, h=h: e.tensor_scalar(xdsc[:, hs * PH:(hs + 1) * PH],
                                                          xtm[:, hs * PH:(hs + 1) * PH],
                                                          D_bc[:, h:h + 1], None, op0=ALU.mult))
                vsync()
                vy = vop(lambda e, ck=ck, h=h, hs=hs:
                         e.scalar_tensor_tensor(y_sb[:, ck * DIN + h * PH: ck * DIN + (h + 1) * PH],
                                                psY[:, 0:PH],
                                                ecum_T[:, ck * NH + h: ck * NH + h + 1],
                                                xdsc[:, hs * PH:(hs + 1) * PH],
                                                op0=ALU.mult, op1=ALU.add))
                p.gset("psY", "VE", vy)
                vt = vop(lambda e, ck=ck, h=h, Tsrc=Tsrc, Tdst=Tdst:
                         e.scalar_tensor_tensor(Tdst[:, h * PH:(h + 1) * PH],
                                                Tsrc[:, h * PH:(h + 1) * PH],
                                                adb[:, ck * NH + h: ck * NH + h + 1],
                                                psS[:, 0:PH], op0=ALU.mult, op1=ALU.add))
                p.gset("psS", "VE", vt)
                vsync()
                vop(lambda e, h=h, Tdst=Tdst: e.tensor_copy(T_bf[:, h * PH:(h + 1) * PH],
                                                            Tdst[:, h * PH:(h + 1) * PH]))
                if "tt" in dbg and g == MT_TAP[0] and hs == MT_TAP[2]:
                    vsync()
                    vop(lambda e, ck=ck, h=h, Tdst=Tdst:
                        e.tensor_copy(tt_sb[:, ck * 144: ck * 144 + 64],
                                      Tdst[:, h * PH:(h + 1) * PH]))
                    vop(lambda e, ck=ck: e.tensor_copy(tt_sb[:, ck * 144 + 64: ck * 144 + 128],
                                                       psS[:, 0:PH]))
                    vop(lambda e, ck=ck, h=h:
                        e.tensor_copy(tt_sb[:, ck * 144 + 128: ck * 144 + 129],
                                      adb[:, ck * NH + h: ck * NH + h + 1]))
                    vop(lambda e, ck=ck, h=h:
                        e.tensor_copy(tt_sb[:, ck * 144 + 129: ck * 144 + 130],
                                      w_T[:, ck * NH + h: ck * NH + h + 1]))
                    vop(lambda e, ck=ck, h=h:
                        e.tensor_copy(tt_sb[:, ck * 144 + 130: ck * 144 + 131],
                                      ecum_T[:, ck * NH + h: ck * NH + h + 1]))
                    vtt = vop(lambda e, ck=ck, hs=hs:
                              e.tensor_copy(tt_sb[:, ck * 144 + 131: ck * 144 + 131 + 13],
                                            xwtm[:, hs * PH: hs * PH + 13]))
                    if ck == 7:
                        p.wait("sync", "VE", vtt)
                        p.dma("sync", dbg["tt"][:, :], tt_sb[:, :], "STS")
        p.gset("rowAB", "VE", p.cnt["VE"])
        pe_lastx = p.cnt["XT"]
        if g == XP_TAP and "xp" in dbg:
            p.wait("gpsimd", "SE", se_conv)
            xpv = p.dma("gpsimd", dbg["xp"][:, :], xpost[:, 0:TOK], "DBS")
            p.gadd("xpost0", "DBS", xpv)
        p.gadd(f"xpost{rs}", "XT", p.cnt["XT"])
    ve_scan = p.cnt["VE"]
    if "T" in dbg:
        p.wait("sync", "VE", ve_scan)
        p.dma("sync", dbg["T"][:, :], T0[:, :], "STS")
    if "y0" in dbg:
        p.wait("gpsimd", "VE", ve_scan)
        p.dma("gpsimd", dbg["y0"][:, :], y_sb[:, :], "DBS")
        p.wait("tensor", "DBS", p.cnt["DBS"])

    if NO_CC:
        ve_y = ve_scan
        for nm in (["E_b", "E_a", "xtm2", "CT_bc"] + (["tt_sb"] if "tt" in dbg else []) +
               (["dbg_mt_sb"] if "mt" in dbg else []) +
                   ["xdsc", "xwtm", "xtm", "Mt", "xraw",
                    "xpost", "cdst", "wst", "T_bf", "D_bc", "adb", "w_T",
                    "ecum_T", "evec_T", "cum_T", "B_tm", "GmT"]):
            p.free(nm)
        return finish_build(p, nc, locals())
    Tfin = T0
    p.wait("gpsimd", "VE", ve_scan)
    vz2 = vop(lambda e: e.memset(azp[:, :], 0.0))
    p.wait("vector", "VE", vz2)
    vz3 = vop(lambda e: e.tensor_copy(azp[0:64, 0:1], aseg[:, 0:1]))
    st_T = p.dma("gpsimd", cc_in[:, 0:4096], Tfin[:, :], "CCS")
    p.wait("gpsimd", "SE", p.cnt["SE"])
    p.wait("gpsimd", "VE", vz3)
    st_a = p.dma("gpsimd", cc_in[:, 4096:4104], azp[:, :], "CCS")
    p.wait("gpsimd", "CCS", p.cnt["CCS"])
    cc_v = p.op("gpsimd", lambda e: e.collective_compute(
        "AllGather", ALU.bypass,
        replica_groups=[[0, 1, 2, 3], [4, 5, 6, 7]],
        ins=[cc_in.ap().opt()], outs=[cc_out.ap().opt()]), inc=("CC", 1))
    p.wait("gpsimd", "CC", cc_v)
    for nm in (["E_b", "E_a", "xtm2", "CT_bc"] + (["tt_sb"] if "tt" in dbg else []) +
               (["dbg_mt_sb"] if "mt" in dbg else []) +
               ["xdsc", "xwtm", "xtm", "Mt", "xraw",
                "xpost", "cdst", "wst", "T_bf", "D_bc", "adb", "w_T",
                "ecum_T", "evec_T", "cum_T", "B_tm", "GmT"]):
        p.free(nm)
    avrow = p.alloc("avrow", [1, 4 * NH], F32)
    ga = p.alloc("ga", [128, NH], F32)
    rstg = p.alloc("rstg", [128, 4096], F32)
    relay_b = p.alloc("relay_b", [128, NH * PH], F32)
    relay_bf = p.alloc("relay_bf", [128, NH * PH], BF16)
    for r in range(4):
        p.dma("gpsimd", avrow[0:1, r * NH:(r + 1) * NH],
              cc_out[r * 128:r * 128 + 64, 4096:4097].rearrange("h o -> o h"), "LDG", slow=True)
    ldav = p.cnt["LDG"]
    p.wait("tensor", "LDG", ldav)
    relay_a = T1
    p.wait("vector", "VE", ve_scan)
    vop(lambda e: e.memset(relay_a[:, :], 0.0))
    for r in range(3):
        p.wait("gpsimd", "VE", p.cnt["VE"])
        ldT = p.dma("gpsimd", rstg[:, :], cc_out[r * 128:(r + 1) * 128, 0:4096], "LRS")
        p.gwait("tensor", "psG")
        pe_ab = top(lambda e, r=r: e.matmul(psG[:, 0:NH], ones_f[0:1, :],
                                            avrow[0:1, r * NH:(r + 1) * NH],
                                            start=True, stop=True))
        p.wait("vector", "PE", pe_ab)
        p.spacer("vector")
        vop(lambda e, r=r: e.tensor_scalar(ga[:, :], psG[:, 0:NH], rsel_sb[:, r:r + 1],
                                           rsinv_sb[:, r:r + 1], op0=ALU.mult, op1=ALU.add))
        p.gset("psG", "VE", p.cnt["VE"])
        p.wait("vector", "LRS", ldT)
        vop(lambda e: e.tensor_scalar(rstg[:, :], rstg[:, :], rsel_sb[:, r:r + 1], None,
                                      op0=ALU.mult))
        vsync()
        src, dst = (relay_a, relay_b) if r % 2 == 0 else (relay_b, relay_a)
        for h in range(NH):
            vop(lambda e, h=h, src=src, dst=dst:
                e.scalar_tensor_tensor(dst[:, h * PH:(h + 1) * PH], src[:, h * PH:(h + 1) * PH],
                                       ga[:, h:h + 1], rstg[:, h * PH:(h + 1) * PH],
                                       op0=ALU.mult, op1=ALU.add))
        vsync()
    vop(lambda e: e.tensor_copy(relay_bf[:, :], relay_b[:, :]))
    vsync()
    ve_rbf = p.cnt["VE"]
    if "rel" in dbg:
        p.wait("sync", "VE", ve_rbf)
        p.dma("sync", dbg["rel"][:, :], relay_b[:, :], "STS")
    p.wait("tensor", "VE", ve_rbf)
    for ck in range(CK):
        for h in range(NH):
            p.gwait("tensor", "psY")
            pe_y2 = top(lambda e, ck=ck, h=h:
                        e.matmul(psY[:, 0:PH], BCp[:, TOK + ck * 128: TOK + (ck + 1) * 128],
                                 relay_bf[:, h * PH:(h + 1) * PH], start=True, stop=True))
            p.wait("vector", "PE", pe_y2)
            p.spacer("vector")
            vy = vop(lambda e, ck=ck, h=h:
                     e.scalar_tensor_tensor(y_sb[:, ck * DIN + h * PH: ck * DIN + (h + 1) * PH],
                                            psY[:, 0:PH],
                                            s_T[:, ck * NH + h: ck * NH + h + 1],
                                            y_sb[:, ck * DIN + h * PH: ck * DIN + (h + 1) * PH],
                                            op0=ALU.mult, op1=ALU.add))
            p.gset("psY", "VE", vy)
    ve_y = p.cnt["VE"]
    return finish_build(p, nc, locals())


def finish_build(p, nc, env):
    dbg = env["dbg"]
    vop = env["vop"]
    y_sb = env["y_sb"]
    out = env["out"]
    ve_y = env["ve_y"]

    if "y" in dbg:
        ydbg = p.alloc("ydbg", [128, 1024], F32)
        for i in range(32):
            p.wait("vector", "VE", ve_y)
            p.wait("vector", "STS", p.cnt["STS"])
            vv = vop(lambda e, i=i: e.tensor_copy(ydbg[:, :], y_sb[:, i * 1024:(i + 1) * 1024]))
            p.wait("sync", "VE", vv)
            p.dma("sync", dbg["y"][:, i * 1024:(i + 1) * 1024], ydbg[:, :], "STS")

    if SKIP_TAIL:
        zz = p.alloc("zz", [128, DM], F32)
        p.wait("vector", "VE", ve_y)
        vz = vop(lambda e: e.memset(zz[:, :], 0.0))
        p.wait("sync", "VE", vz)
        for tb in range(8):
            p.dma("sync", out[tb * 128:(tb + 1) * 128, :], zz[:, :], "STS")
    else:
        build_tail(p, nc, env)

    p.wait("sync", "STS", p.cnt["STS"])

    with nc.Block() as block:
        @block.sync
        def _(e):
            for f in p.ops["sync"]:
                f(e)

        @block.tensor
        def _(e):
            for f in p.ops["tensor"]:
                f(e)

        @block.vector
        def _(e):
            for f in p.ops["vector"]:
                f(e)

        @block.scalar
        def _(e):
            for f in p.ops["scalar"]:
                f(e)

        @block.gpsimd
        def _(e):
            for f in p.ops["gpsimd"]:
                f(e)

    p.close()
    return nc


def build_tail(p, nc, env):
    out = env["out"]
    ve_y = env["ve_y"]
    vop, sop, top, vsync = env["vop"], env["sop"], env["top"], env["vsync"]
    psA, psC, psY, psS, psG = env["psA"], env["psC"], env["psY"], env["psS"], env["psG"]
    y_sb, u_fm = env["y_sb"], env["u_fm"]
    WZT, WOT, WG, WU, WD = env["WZT"], env["WOT"], env["WG"], env["WU"], env["WD"]
    GW, EB, UF, NWR = env["GW"], env["EB"], env["UF"], env["NWR"]
    x1_dram, comb_dram = env["x1_dram"], env["comb_dram"]
    yn_dram = nc.dram_tensor("yn_dram", [128, CK * DIN], BF16)
    xlo_dram = nc.dram_tensor("xlo_dram", [128, 8 * DM], BF16)
    nw_sb, dbg = env["nw_sb"], env["dbg"]
    id_f, ones_f, ones_c = env["id_f"], env["ones_f"], env["ones_c"]
    XTS = p.sems["XT"]

    for nm in ("relay_bf", "relay_b", "rstg", "ga", "avrow"):
        p.free(nm)

    sgz = p.alloc("sgz", [128, 2 * 512], BF16)
    sc512 = p.alloc("sc512", [128, 512], BF16)
    ssq = p.alloc("ssq", [128, 64], F32)
    nw_bc = p.alloc("nw_bc", [128, DIN], BF16)
    rstd8 = p.alloc("rstd8", [128, 8], F32)
    rs_scr = p.alloc("rs_scr", [128, 8], F32)
    zw = p.alloc("zw", [128, 2 * 8192], BF16)
    p.wait("gpsimd", "VE", ve_y)
    p.dma("gpsimd", nw_bc[:, :],
          NWR[0:1, :].rearrange("o c -> (o c)").partition_broadcast(128), "LCB")
    nwbc_ld = p.cnt["LCB"]
    banks = [psA[0], psA[1], psC[0], psC[1]]
    bnames = ["psA0", "psA1", "psC0", "psC1"]
    p.wait("vector", "VE", ve_y)
    zw_val = {}
    for s8 in range(8):
        sl = s8 % 2
        p.gwait("gpsimd", f"zw{sl}")
        zw_val[s8] = p.dma("gpsimd", zw[:, sl * 8192:(sl + 1) * 8192],
                           WZT[s8, :, :, :].rearrange("p k c -> p (k c)"), f"LZ{sl}")
        p.wait("tensor", f"LZ{sl}", zw_val[s8])
        for tb in range(8):
            bi = (s8 * 8 + tb) % 4
            ps = banks[bi]
            p.gwait("tensor", bnames[bi])
            for k in range(16):
                p.op("tensor", lambda e, ps=ps, k=k, sl=sl, tb=tb:
                     e.matmul(ps[:, 0:512],
                              u_fm[:, k * TOKH + 3 + tb * 128: k * TOKH + 3 + (tb + 1) * 128],
                              zw[:, sl * 8192 + k * 512: sl * 8192 + (k + 1) * 512],
                              start=(k == 0), stop=(k == 15)),
                     inc=("PE", 1) if k == 15 else None)
            pev = p.cnt["PE"]
            p.wait("scalar", "PE", pev)
            p.gwait("scalar", f"sgz{tb % 2}")
            se_z = sop(lambda e, ps=ps, tb=tb: e.activation(sgz[:, (tb % 2) * 512:(tb % 2 + 1) * 512],
                                                            ps[:, 0:512], AF.Silu))
            p.gset(bnames[bi], "SE", se_z)
            p.wait("vector", "SE", se_z)
            ysl = y_sb[:, tb * DIN + s8 * 512: tb * DIN + (s8 + 1) * 512]
            vg = vop(lambda e, ysl=ysl, tb=tb: e.tensor_tensor(ysl, ysl,
                                                               sgz[:, (tb % 2) * 512:(tb % 2 + 1) * 512],
                                                               op=ALU.mult))
            p.gset(f"sgz{tb % 2}", "VE", vg)
            vsync()
            vop(lambda e, ysl=ysl, tb=tb, s8=s8:
                e.scalar_tensor_tensor(sc512[:, :], ysl, 1.0, ysl, op0=ALU.mult, op1=ALU.mult,
                                       accum_out=ssq[:, tb * 8 + s8: tb * 8 + s8 + 1]))
        p.gset(f"zw{sl}", "PE", p.cnt["PE"])
    p.free("zw")
    vsync()
    for tb in range(8):
        vop(lambda e, tb=tb: e.reduce_sum(rs_scr[:, tb:tb + 1], ssq[:, tb * 8:(tb + 1) * 8],
                                          axis=mybir.AxisListType.X))
    vsync()
    vop(lambda e: e.tensor_scalar(rs_scr[:, :], rs_scr[:, :], 1.0 / DIN, EPS,
                                  op0=ALU.mult, op1=ALU.add))
    vsync()
    v_rs = p.cnt["VE"]
    p.wait("scalar", "VE", v_rs)
    s_l = sop(lambda e: e.activation(rstd8[:, :], rs_scr[:, :], AF.Ln))
    p.wait("scalar", "SE", s_l)
    s_r = sop(lambda e: e.activation(rstd8[:, :], rstd8[:, :], AF.Exp, scale=-0.5))
    p.wait("vector", "SE", s_r)
    p.wait("vector", "LCB", nwbc_ld)
    for ck in range(CK):
        ysl = y_sb[:, ck * DIN:(ck + 1) * DIN]
        vop(lambda e, ysl=ysl: e.tensor_tensor(ysl, ysl, nw_bc[:, :], op=ALU.mult))
        vsync()
        vv = vop(lambda e, ysl=ysl, ck=ck: e.tensor_scalar(ysl, ysl, rstd8[:, ck:ck + 1], None,
                                                           op0=ALU.mult))
        p.wait("gpsimd", "VE", vv)
        p.dma("gpsimd", yn_dram[:, ck * DIN:(ck + 1) * DIN], ysl, "CST")
    yn_st = p.cnt["CST"]

    if "yn" in dbg:
        p.wait("gpsimd", "CST", yn_st)
        p.dma("gpsimd", dbg["yn"][:, :], y_sb[:, :], "DBS")
        p.wait("vector", "DBS", p.cnt["DBS"])

    for nm in ("rs_scr", "rstd8", "nw_bc", "ssq", "sc512", "sgz",
               "T1", "T0", "s_T", "azp", "aseg", "ab_T", "ab_h", "BCp", "y_sb", "u_fm"):
        p.free(nm)

    x1_fm = p.alloc("x1_fm", [128, 16 * TOK], BF16)
    yn_fm = p.alloc("yn_fm", [128, CK * DIN], BF16)
    wo = p.alloc("wo", [128, 2 * 4096], BF16)
    ustg = p.alloc("ustg", [128, 2 * DM], F32)
    x1stg = p.alloc("x1stg", [128, 8 * 128], F32)
    x1t = p.alloc("x1t", [128, 2 * DM], F32)
    xbt = p.alloc("xbt", [128, 2 * DM], BF16)
    xlt = p.alloc("xlt", [128, 2 * DM], BF16)
    bnscr = p.alloc("bnscr", [128, 24], F32)
    mv2 = p.alloc("mv2", [128, 2], F32)
    mcol = p.alloc("mcol", [128, 2], F32)
    p.wait("sync", "CST", yn_st)
    ynfm_val = 0
    for cc in range(32):
        for ck in range(CK):
            p.emit("sync", lambda e, cc=cc, ck=ck, s=XTS:
                   e.dma_start(out=yn_fm[:, cc * TOK + ck * 128: cc * TOK + (ck + 1) * 128],
                               in_=yn_dram[:, ck * DIN + cc * 128: ck * DIN + (cc + 1) * 128],
                               transpose=True).then_inc(s, 16))
            ynfm_val = p.bump("XT", 16)
    p.wait("tensor", "XT", ynfm_val)
    for dc in range(16):
        sl = dc % 2
        p.gwait("gpsimd", f"wo{sl}")
        wo_v = p.dma("gpsimd", wo[:, sl * 4096:(sl + 1) * 4096],
                     WOT[dc, :, :, :].rearrange("p k c -> p (k c)"), f"LO{sl}")
        p.wait("tensor", f"LO{sl}", wo_v)
        for tb in range(8):
            bi = tb % 4
            ps = banks[bi]
            p.gwait("tensor", bnames[bi])
            for cc in range(32):
                p.op("tensor", lambda e, ps=ps, cc=cc, tb=tb, sl=sl:
                     e.matmul(ps[:, 0:128],
                              yn_fm[:, cc * TOK + tb * 128: cc * TOK + (tb + 1) * 128],
                              wo[:, sl * 4096 + cc * 128: sl * 4096 + (cc + 1) * 128],
                              start=(cc == 0), stop=(cc == 31)),
                     inc=("PE", 1) if cc == 31 else None)
            pev = p.cnt["PE"]
            p.wait("vector", "PE", pev)
            p.spacer("vector", 256)
            p.gwait("vector", f"x1stg{tb}")
            vv = vop(lambda e, ps=ps, tb=tb: e.tensor_copy(x1stg[:, tb * 128:(tb + 1) * 128],
                                                           ps[:, 0:128]))
            p.gset(bnames[bi], "VE", vv)
        vsync()
        vstg = p.cnt["VE"]
        p.wait("gpsimd", "VE", vstg)
        dst = x1_dram[:, :].rearrange("p (tb c) -> p tb c", c=DM)[:, :, dc * 128:(dc + 1) * 128]
        st_v = p.dma("gpsimd", dst, x1stg[:, 0:1024].rearrange("p (tb c) -> p tb c", c=128), "CST")
        for tb in range(8):
            p.gset(f"x1stg{tb}", "CST", st_v)
        p.gset(f"wo{sl}", "PE", p.cnt["PE"])
    x1pre_st = p.cnt["CST"]
    x1fm_val = 0
    for tb in range(8):
        sl = tb % 2
        p.gwait("gpsimd", f"x1t{sl}")
        p.wait("gpsimd", "CST", x1pre_st)
        lx = p.dma("gpsimd", x1t[:, sl * DM:(sl + 1) * DM], x1_dram[:, tb * DM:(tb + 1) * DM],
                   f"LX{sl}")
        lu = p.dma("gpsimd", ustg[:, sl * DM:(sl + 1) * DM], UF[tb * 128:(tb + 1) * 128, :],
                   f"LU{sl}")
        p.wait("vector", f"LX{sl}", lx)
        p.wait("vector", f"LU{sl}", lu)
        xs = x1t[:, sl * DM:(sl + 1) * DM]
        vop(lambda e, xs=xs, sl=sl: e.tensor_tensor(xs, xs, ustg[:, sl * DM:(sl + 1) * DM],
                                                    op=ALU.add))
        vsync()
        for q in range(4):
            vop(lambda e, xs=xs, q=q: e.bn_stats(bnscr[:, q * 6:(q + 1) * 6],
                                                 xs[:, q * 512:(q + 1) * 512]))
            vsync()
        vop(lambda e: e.bn_aggr(mv2[:, :], bnscr[:, :]))
        vsync()
        vop(lambda e: e.tensor_scalar(mcol[:, 0:1], mv2[:, 0:1], -1.0, None, op0=ALU.mult))
        vsync()
        vop(lambda e: e.tensor_scalar(mv2[:, 1:2], mv2[:, 1:2], 1.0, EPS,
                                      op0=ALU.mult, op1=ALU.add))
        vsync()
        v_mv = p.cnt["VE"]
        p.wait("scalar", "VE", v_mv)
        s1 = sop(lambda e: e.activation(mcol[:, 1:2], mv2[:, 1:2], AF.Ln))
        p.wait("scalar", "SE", s1)
        s2 = sop(lambda e: e.activation(mcol[:, 1:2], mcol[:, 1:2], AF.Exp, scale=-0.5))
        p.wait("vector", "SE", s2)
        vop(lambda e, xs=xs: e.tensor_scalar(xs, xs, mcol[:, 0:1], None, op0=ALU.add))
        vsync()
        vop(lambda e, xs=xs: e.tensor_scalar(xs, xs, mcol[:, 1:2], None, op0=ALU.mult))
        vsync()
        vx1 = p.cnt["VE"]
        p.wait("gpsimd", "VE", vx1)
        p.dma("gpsimd", x1_dram[:, tb * DM:(tb + 1) * DM], xs, "CST")
        vop(lambda e, xs=xs, sl=sl: e.tensor_copy(xbt[:, sl * DM:(sl + 1) * DM], xs))
        vsync()
        vlo = vop(lambda e, xs=xs, sl=sl: e.tensor_tensor(xlt[:, sl * DM:(sl + 1) * DM], xs,
                                                          xbt[:, sl * DM:(sl + 1) * DM],
                                                          op=ALU.subtract))
        p.wait("gpsimd", "VE", vlo)
        lo_st = p.dma("gpsimd", xlo_dram[:, tb * DM:(tb + 1) * DM],
                      xlt[:, sl * DM:(sl + 1) * DM], "CCS")
        p.wait("sync", "VE", vlo)
        for k in range(16):
            p.emit("sync", lambda e, k=k, tb=tb, sl=sl, s=XTS:
                   e.dma_start(out=x1_fm[:, k * TOK + tb * 128: k * TOK + (tb + 1) * 128],
                               in_=xbt[:, sl * DM + k * 128: sl * DM + (k + 1) * 128],
                               transpose=True).then_inc(s, 16))
            x1fm_val = p.bump("XT", 16)
        p.gset(f"x1t{sl}", "XT", x1fm_val)
        p.gadd(f"x1t{sl}", "CCS", lo_st)
    x1lo_st = p.cnt["CCS"]
    for nm in ("mcol", "mv2", "bnscr", "xlt", "xbt", "x1t", "x1stg", "ustg", "wo", "yn_fm"):
        p.free(nm)

    build_tail2(p, nc, env, locals())


def build_tail2(p, nc, env, env1):
    out = env["out"]
    vop, sop, top, vsync = env["vop"], env["sop"], env["top"], env["vsync"]
    psA, psC, psY, psS, psG = env["psA"], env["psC"], env["psY"], env["psS"], env["psG"]
    WG, WU, WD = env["WG"], env["WU"], env["WD"]
    GW, EB, UF = env["GW"], env["EB"], env["UF"]
    x1_dram, comb_dram = env["x1_dram"], env["comb_dram"]
    dbg = env["dbg"]
    xlo_dram = env1["xlo_dram"]
    x1_fm = env1["x1_fm"]
    x1lo_st = env1["x1lo_st"]
    x1fm_val = env1["x1fm_val"]
    XTS = p.sems["XT"]
    banks = [psA[0], psA[1], psC[0], psC[1]]
    bnames = ["psA0", "psA1", "psC0", "psC1"]
    dbanks = None

    xlo_fm = p.alloc("xlo_fm", [128, 16 * TOK], BF16)
    gw_sb = p.alloc("gw_sb", [EXP, DM], F32)
    gwp = p.alloc("gwp", [128, 2 * 16 * 128], BF16)
    gwt_f = p.alloc("gwt_f", [128, 16 * 16], F32)
    lgt = p.alloc("lgt", [128, 8 * 16], F32)
    bsd = p.alloc("bsd", [128, 8 * 16], F32)
    eb_bc = p.alloc("eb_bc", [128, 16], F32)
    m8 = p.alloc("m8", [128, 8], F32)
    rsc = p.alloc("rsc", [128, 16], F32)
    rcol = p.alloc("rcol", [128, 4], F32)
    comb = p.alloc("comb", [128, 2 * 128], BF16)
    p.wait("sync", "XT", x1fm_val)
    for tb in range(8):
        for k in range(16):
            p.emit("sync", lambda e, k=k, tb=tb, s=XTS:
                   e.dma_start(out=xlo_fm[:, k * TOK + tb * 128: k * TOK + (tb + 1) * 128],
                               in_=xlo_dram[:, tb * DM + k * 128: tb * DM + (k + 1) * 128],
                               transpose=True).then_inc(s, 16))
            p.bump("XT", 16)
    xlo_val = p.cnt["XT"]
    p.wait("gpsimd", "VE", p.cnt["VE"])
    lgw = p.dma("gpsimd", gw_sb[:, :], GW[:, :], "LCB")
    leb = p.dma("gpsimd", eb_bc[:, 0:16],
                EB[:, :].rearrange("e o -> (e o)").partition_broadcast(128), "LCB")
    gwh = p.alloc("gwh", [EXP, DM], BF16)
    gwl = p.alloc("gwl", [EXP, DM], BF16)
    p.wait("vector", "LCB", lgw)
    vop(lambda e: e.memset(gwp[:, :], 0.0))
    vop(lambda e: e.tensor_copy(gwh[:, :], gw_sb[:, :]))
    vsync()
    vgl = vop(lambda e: e.tensor_tensor(gwl[:, :], gw_sb[:, :], gwh[:, :], op=ALU.subtract))
    p.wait("sync", "VE", vgl)
    for k in range(16):
        for hi in range(2):
            srcb = gwh if hi == 0 else gwl
            p.emit("sync", lambda e, k=k, hi=hi, srcb=srcb, s=XTS:
                   e.dma_start(out=gwp[:, (hi * 16 + k) * 128: (hi * 16 + k) * 128 + 16],
                               in_=srcb[:, k * 128:(k + 1) * 128], transpose=True).then_inc(s, 16))
            p.bump("XT", 16)
    gw_val = p.cnt["XT"]
    p.wait("tensor", "XT", max(xlo_val, gw_val))
    p.wait("tensor", "LCB", leb)
    comb_st = 0
    for tb in range(8):
        bi = tb % 2
        ps = banks[bi]
        p.gwait("tensor", bnames[bi])
        nmm = 0
        for k in range(16):
            for (xa, wa) in ((0, 0), (0, 1), (1, 0)):
                xs = x1_fm if xa == 0 else xlo_fm
                p.op("tensor", lambda e, ps=ps, k=k, tb=tb, xs=xs, wa=wa, nmm=nmm:
                     e.matmul(ps[:, 0:16],
                              xs[:, k * TOK + tb * 128: k * TOK + (tb + 1) * 128],
                              gwp[:, (wa * 16 + k) * 128: (wa * 16 + k) * 128 + 16],
                              start=(nmm == 0), stop=(nmm == 47)),
                     inc=("PE", 1) if nmm == 47 else None)
                nmm += 1
        pev = p.cnt["PE"]
        p.wait("vector", "PE", pev)
        p.spacer("vector", 256)
        vop(lambda e, ps=ps, tb=tb: e.tensor_copy(lgt[:, tb * 16:(tb + 1) * 16], ps[:, 0:16]))
        vsync()
        vv = vop(lambda e, ps=ps, tb=tb: e.tensor_tensor(bsd[:, tb * 16:(tb + 1) * 16],
                                                         ps[:, 0:16], eb_bc[:, 0:16],
                                                         op=ALU.add))
        p.gset(bnames[bi], "VE", vv)
        vsync()
        vop(lambda e, tb=tb: e.max(m8[:, :], bsd[:, tb * 16:(tb + 1) * 16]))
        vsync()
        vop(lambda e, tb=tb: e.tensor_scalar(rsc[:, :], bsd[:, tb * 16:(tb + 1) * 16],
                                             m8[:, 1:2], None, op0=ALU.is_ge))
        vsync()
        vop(lambda e, tb=tb: e.reduce_max(rcol[:, 0:1], lgt[:, tb * 16:(tb + 1) * 16],
                                          axis=mybir.AxisListType.X))
        vsync()
        vop(lambda e: e.tensor_scalar(rcol[:, 1:2], rcol[:, 0:1], -1.0, None, op0=ALU.mult))
        vsync()
        vq = p.cnt["VE"]
        p.wait("scalar", "VE", vq)
        sq = sop(lambda e, tb=tb: e.activation(lgt[:, tb * 16:(tb + 1) * 16],
                                               lgt[:, tb * 16:(tb + 1) * 16],
                                               AF.Exp, bias=rcol[:, 1:2]))
        p.wait("vector", "SE", sq)
        vop(lambda e, tb=tb: e.tensor_tensor(lgt[:, tb * 16:(tb + 1) * 16],
                                             lgt[:, tb * 16:(tb + 1) * 16], rsc[:, 0:16],
                                             op=ALU.mult))
        vsync()
        vop(lambda e, tb=tb: e.reduce_sum(rcol[:, 2:3], lgt[:, tb * 16:(tb + 1) * 16],
                                          axis=mybir.AxisListType.X))
        vsync()
        vop(lambda e: e.reciprocal(rcol[:, 3:4], rcol[:, 2:3]))
        vsync()
        p.gwait("vector", "comb_w")
        vop(lambda e: e.memset(comb[:, 0:128], 0.0))
        vsync()
        vcb = vop(lambda e, tb=tb:
                  e.tensor_scalar(comb[:, 0:16],
                                  lgt[:, tb * 16:(tb + 1) * 16], rcol[:, 3:4], None,
                                  op0=ALU.mult))
        p.wait("sync", "VE", vcb)
        p.gwait("sync", "comb_t")
        p.emit("sync", lambda e, s=XTS:
               e.dma_start(out=comb[:, 128:256],
                           in_=comb[:, 0:128], transpose=True).then_inc(s, 16))
        ctv = p.bump("XT", 16)
        p.gset("comb_w", "XT", ctv)
        p.wait("gpsimd", "XT", ctv)
        comb_st = p.dma("gpsimd", comb_dram[:, tb * 128:(tb + 1) * 128],
                        comb[0:16, 128:256], "CCS")
        p.gset("comb_t", "CCS", comb_st)
    if "comb" in dbg:
        p.wait("gpsimd", "CCS", comb_st)
        p.dma("gpsimd", dbg["comb"][0:16, :].rearrange("e c -> e c"), comb_dram[:, :], "DBS")
        p.wait("vector", "DBS", p.cnt["DBS"])
    for nm in ("gwl", "gwh", "comb", "rcol", "rsc", "m8", "eb_bc", "bsd", "lgt",
               "gwt_f", "gwp", "gw_sb", "xlo_fm"):
        p.free(nm)

    p.free("psTR")
    psD = p.alloc("psD", [128, 512], F32, psum=True)
    dbanks = [psA[0], psA[1], psC[0], psC[1], psY, psS, psG, psD]
    dbnames = ["psA0", "psA1", "psC0", "psC1", "psY", "psS", "psG", "psD"]
    hbuf = p.alloc("hbuf", [128, 2 * FC * TOK], BF16)
    acc = p.alloc("acc", [128, 16 * TOK], F32)
    wgst = p.alloc("wgst", [128, 2 * DM], BF16)
    wust = p.alloc("wust", [128, 2 * DM], BF16)
    wdst = p.alloc("wdst", [128, 2 * FC * 512], BF16)
    cbc = p.alloc("cbc", [128, 2 * TOK], BF16)
    sg1 = p.alloc("sg1", [128, 2 * 512], BF16)
    sg2 = p.alloc("sg2", [128, 2 * 512], BF16)
    groups = [(0, 1), (2, 3), (4, 5), (6, 7), (8, 9), (10, 11), (12, 13), (14, 15), (16,)]
    p.wait("gpsimd", "CCS", max(comb_st, x1lo_st))
    for gi, grp in enumerate(groups):
        for ei, e_ in enumerate(grp):
            sl = e_ % 2
            if e_ < EXP:
                p.gwait("gpsimd", f"cbc{sl}")
                cb_v = p.dma("gpsimd", cbc[:, sl * TOK:(sl + 1) * TOK],
                             comb_dram[e_:e_ + 1, :].rearrange("o c -> (o c)").partition_broadcast(128),
                             "LCB")
            else:
                cb_v = None
            for fc in range(FC):
                wsl = fc % 2
                p.gwait("gpsimd", f"wg{wsl}")
                g_v = p.dma("gpsimd", wgst[:, wsl * DM:(wsl + 1) * DM],
                            WG[e_, fc, :, :, :].rearrange("p k c -> p (k c)"), f"LG{wsl}")
                u_v = p.dma("gpsimd", wust[:, wsl * DM:(wsl + 1) * DM],
                            WU[e_, fc, :, :, :].rearrange("p k c -> p (k c)"), f"LG{wsl}")
                p.wait("tensor", f"LG{wsl}", u_v)
                for tc in range(2):
                    big, biu = (tc, 2 + tc)
                    psg_, psu_ = banks[big], banks[biu]
                    p.gwait("tensor", bnames[big])
                    p.gwait("tensor", bnames[biu])
                    for k in range(16):
                        p.op("tensor", lambda e, ps=psg_, k=k, tc=tc, wsl=wsl:
                             e.matmul(ps[:, 0:512],
                                      wgst[:, wsl * DM + k * 128: wsl * DM + (k + 1) * 128],
                                      x1_fm[:, k * TOK + tc * 512: k * TOK + (tc + 1) * 512],
                                      start=(k == 0), stop=(k == 15)),
                             inc=("PE", 1) if k == 15 else None)
                    for k in range(16):
                        p.op("tensor", lambda e, ps=psu_, k=k, tc=tc, wsl=wsl:
                             e.matmul(ps[:, 0:512],
                                      wust[:, wsl * DM + k * 128: wsl * DM + (k + 1) * 128],
                                      x1_fm[:, k * TOK + tc * 512: k * TOK + (tc + 1) * 512],
                                      start=(k == 0), stop=(k == 15)),
                             inc=("PE", 1) if k == 15 else None)
                    pev = p.cnt["PE"]
                    p.wait("scalar", "PE", pev)
                    p.gwait("scalar", f"sg1{tc}")
                    sv = sop(lambda e, ps=psg_, tc=tc: e.activation(sg1[:, tc * 512:(tc + 1) * 512],
                                                                    ps[:, 0:512], AF.Silu))
                    p.gset(bnames[big], "SE", sv)
                    p.wait("vector", "SE", sv)
                    p.spacer("vector", 256)
                    if e_ < EXP:
                        p.wait("vector", "LCB", cb_v)
                        vu = vop(lambda e, ps=psu_, tc=tc, sl=sl:
                                 e.tensor_tensor(sg2[:, tc * 512:(tc + 1) * 512], ps[:, 0:512],
                                                 cbc[:, sl * TOK + tc * 512: sl * TOK + (tc + 1) * 512],
                                                 op=ALU.mult))
                        vsync()
                        vh = vop(lambda e, ei=ei, fc=fc, tc=tc:
                                 e.tensor_tensor(hbuf[:, (ei * FC + fc) * TOK + tc * 512:
                                                      (ei * FC + fc) * TOK + (tc + 1) * 512],
                                                 sg2[:, tc * 512:(tc + 1) * 512],
                                                 sg1[:, tc * 512:(tc + 1) * 512], op=ALU.mult))
                    else:
                        vh = vop(lambda e, ei=ei, fc=fc, tc=tc, ps=psu_:
                                 e.tensor_tensor(hbuf[:, (ei * FC + fc) * TOK + tc * 512:
                                                      (ei * FC + fc) * TOK + (tc + 1) * 512],
                                                 ps[:, 0:512],
                                                 sg1[:, tc * 512:(tc + 1) * 512], op=ALU.mult))
                    p.gset(bnames[biu], "VE", vh)
                    p.gset(f"sg1{tc}", "VE", vh)
                p.gset(f"wg{wsl}", "PE", p.cnt["PE"])
            if e_ < EXP:
                p.gset(f"cbc{sl}", "VE", p.cnt["VE"])
        ve_h = p.cnt["VE"]
        p.wait("tensor", "VE", ve_h)
        for dsl in range(4):
            for ei, e_ in enumerate(grp):
                wsl = ei
                p.gwait("gpsimd", f"wd{wsl}")
                d_v = p.dma("gpsimd", wdst[:, wsl * FC * 512:(wsl + 1) * FC * 512],
                            WD[e_, dsl, :, :, :, :].rearrange("p f d c -> p (f d c)"), f"LD{wsl}")
                p.wait("tensor", f"LD{wsl}", d_v)
                for tc in range(2):
                    for dj in range(4):
                        bi = tc * 4 + dj
                        ps = dbanks[bi]
                        first = (ei == 0)
                        last = (ei == len(grp) - 1)
                        if first:
                            p.gwait("tensor", dbnames[bi])
                        ng = len(grp)
                        for fc in range(FC):
                            p.op("tensor", lambda e, ps=ps, fc=fc, dj=dj, tc=tc, ei=ei, wsl=wsl, ng=ng:
                                 e.matmul(ps[:, 0:512],
                                          wdst[:, wsl * FC * 512 + (fc * 4 + dj) * 128:
                                               wsl * FC * 512 + (fc * 4 + dj) * 128 + 128],
                                          hbuf[:, (ei * FC + fc) * TOK + tc * 512:
                                               (ei * FC + fc) * TOK + (tc + 1) * 512],
                                          start=(ei == 0 and fc == 0),
                                          stop=(ei == ng - 1 and fc == FC - 1)),
                                 inc=("PE", 1) if fc == FC - 1 else None)
                        if last:
                            pev = p.cnt["PE"]
                            p.wait("vector", "PE", pev)
                            p.spacer("vector", 256)
                            col = (dsl * 4 + dj) * TOK + tc * 512
                            if gi == 0:
                                vv = vop(lambda e, ps=ps, col=col:
                                         e.tensor_copy(acc[:, col:col + 512], ps[:, 0:512]))
                            else:
                                vv = vop(lambda e, ps=ps, col=col:
                                         e.scalar_tensor_tensor(acc[:, col:col + 512], ps[:, 0:512],
                                                                1.0, acc[:, col:col + 512],
                                                                op0=ALU.mult, op1=ALU.add))
                            p.gset(dbnames[bi], "VE", vv)
                p.gset(f"wd{wsl}", "PE", p.cnt["PE"])
    ve_ffn = p.cnt["VE"]
    if "acc" in dbg:
        p.wait("gpsimd", "VE", ve_ffn)
        p.dma("gpsimd", dbg["acc"][:, :], acc[:, :], "DBS")
        p.wait("vector", "DBS", p.cnt["DBS"])

    for nm in ("sg2", "sg1", "cbc", "wdst", "wust", "wgst"):
        p.free(nm)
    accb = p.alloc("accb", [128, 16 * 128], BF16)
    accT = p.alloc("accT", [128, 2 * DM], BF16)
    x2t = p.alloc("x2t", [128, 2 * DM], F32)
    bnscr2 = p.alloc("bnscr2", [128, 24], F32)
    mv3 = p.alloc("mv3", [128, 2], F32)
    mc3 = p.alloc("mc3", [128, 2], F32)
    outs = p.alloc("outs", [128, 2 * DM], F32)
    p.wait("vector", "VE", ve_ffn)
    p.wait("gpsimd", "VE", ve_ffn)
    p.wait("sync", "VE", ve_ffn)
    for tb in range(8):
        sl = tb % 2
        p.gwait("vector", f"accb{sl}")
        for dc in range(16):
            vop(lambda e, dc=dc, tb=tb: e.tensor_copy(accb[:, dc * 128:(dc + 1) * 128],
                                                      acc[:, dc * TOK + tb * 128: dc * TOK + (tb + 1) * 128]))
        vsync()
        vab = p.cnt["VE"]
        p.wait("sync", "VE", vab)
        p.gwait("sync", f"accT{sl}")
        for dc in range(16):
            p.emit("sync", lambda e, dc=dc, sl=sl, s=XTS:
                   e.dma_start(out=accT[:, sl * DM + dc * 128: sl * DM + (dc + 1) * 128],
                               in_=accb[:, dc * 128:(dc + 1) * 128], transpose=True).then_inc(s, 16))
            p.bump("XT", 16)
        at_v = p.cnt["XT"]
        p.gset(f"accb{sl}", "XT", at_v)
        p.gwait("gpsimd", f"x2t{sl}")
        lx = p.dma("gpsimd", x2t[:, sl * DM:(sl + 1) * DM], x1_dram[:, tb * DM:(tb + 1) * DM],
                   f"LX{sl}")
        p.wait("vector", f"LX{sl}", lx)
        p.wait("vector", "XT", at_v)
        xs = x2t[:, sl * DM:(sl + 1) * DM]
        vv0 = vop(lambda e, xs=xs, sl=sl: e.tensor_tensor(xs, xs, accT[:, sl * DM:(sl + 1) * DM],
                                                          op=ALU.add))
        p.gset(f"accT{sl}", "VE", vv0)
        vsync()
        for q in range(4):
            vop(lambda e, xs=xs, q=q: e.bn_stats(bnscr2[:, q * 6:(q + 1) * 6],
                                                 xs[:, q * 512:(q + 1) * 512]))
            vsync()
        vop(lambda e: e.bn_aggr(mv3[:, :], bnscr2[:, :]))
        vsync()
        vop(lambda e: e.tensor_scalar(mc3[:, 0:1], mv3[:, 0:1], -1.0, None, op0=ALU.mult))
        vsync()
        vop(lambda e: e.tensor_scalar(mv3[:, 1:2], mv3[:, 1:2], 1.0, EPS,
                                      op0=ALU.mult, op1=ALU.add))
        vsync()
        v_mv = p.cnt["VE"]
        p.wait("scalar", "VE", v_mv)
        s1 = sop(lambda e: e.activation(mc3[:, 1:2], mv3[:, 1:2], AF.Ln))
        p.wait("scalar", "SE", s1)
        s2 = sop(lambda e: e.activation(mc3[:, 1:2], mc3[:, 1:2], AF.Exp, scale=-0.5))
        p.wait("vector", "SE", s2)
        p.gwait("vector", f"outs{sl}")
        vop(lambda e, xs=xs, sl=sl: e.scalar_tensor_tensor(outs[:, sl * DM:(sl + 1) * DM], xs,
                                                           1.0, xs, op0=ALU.bypass,
                                                           op1=ALU.bypass)) if False else None
        vop(lambda e, xs=xs: e.tensor_scalar(xs, xs, mc3[:, 0:1], None, op0=ALU.add))
        vsync()
        vo = vop(lambda e, xs=xs, sl=sl: e.tensor_scalar(outs[:, sl * DM:(sl + 1) * DM], xs,
                                                         mc3[:, 1:2], None, op0=ALU.mult))
        p.wait("sync", "VE", vo)
        ost = p.dma("sync", out[tb * 128:(tb + 1) * 128, :], outs[:, sl * DM:(sl + 1) * DM], "STS")
        p.gset(f"outs{sl}", "STS", ost)
        p.gset(f"x2t{sl}", "VE", vo)



def _host_prep(inputs):
    f32 = np.float32
    u = np.asarray(inputs["L0_attn_u"], f32)
    Win = np.asarray(inputs["L0_attn_W_in"], f32)
    convw = np.asarray(inputs["L0_attn_conv_w"], f32)[:, 0, :]
    convb = np.asarray(inputs["L0_attn_conv_b"], f32)
    dtb = np.asarray(inputs["L0_attn_dt_bias"], f32)
    alog = np.asarray(inputs["L0_attn_A_log"], f32)
    Dv = np.asarray(inputs["L0_attn_D"], f32)
    normw = np.asarray(inputs["L0_attn_norm_w"], f32)
    Wout = np.asarray(inputs["L0_attn_W_out"], f32)
    gw = np.asarray(inputs["L0_ffn_gate_w"], f32)
    wg = np.asarray(inputs["L0_ffn_w_gate"], f32)
    wu = np.asarray(inputs["L0_ffn_w_up"], f32)
    wd = np.asarray(inputs["L0_ffn_w_down"], f32)
    swg = np.asarray(inputs["L0_ffn_sw_gate"], f32)
    swu = np.asarray(inputs["L0_ffn_sw_up"], f32)
    swd = np.asarray(inputs["L0_ffn_sw_down"], f32)
    eb = np.asarray(inputs["L0_ffn_expert_bias"], f32)

    uf = np.ascontiguousarray(u.reshape(8192, DM))
    uT = np.ascontiguousarray(uf.T)

    WX = Win[DIN:DIN + 4352]
    WXT = np.ascontiguousarray(WX.reshape(NG, 128, 16, 128).transpose(0, 3, 2, 1))
    WDT = np.ascontiguousarray(Win[8448:8512].reshape(64, 16, 128).transpose(2, 1, 0))
    WZT = np.ascontiguousarray(Win[0:DIN].reshape(8, 512, 16, 128).transpose(0, 3, 2, 1))
    WOT = np.ascontiguousarray(Wout.T.reshape(32, 128, 16, 128).transpose(2, 1, 0, 3))
    CDIAG = np.zeros((NG, 128, 4, 128), f32)
    cwg = convw.reshape(NG, 128, 4)
    for rix in range(128):
        CDIAG[:, rix, :, rix] = cwg[:, rix, :]
    CB = np.ascontiguousarray(convb.reshape(NG, 128).T)
    NW = np.ascontiguousarray(normw.reshape(32, 128).T)
    WG17 = np.concatenate([wg, swg[None]], 0)
    WU17 = np.concatenate([wu, swu[None]], 0)
    WD17 = np.concatenate([wd, swd[None]], 0)
    WGt = np.ascontiguousarray(WG17.reshape(17, FC, 128, 16, 128).transpose(0, 1, 4, 3, 2))
    WUt = np.ascontiguousarray(WU17.reshape(17, FC, 128, 16, 128).transpose(0, 1, 4, 3, 2))
    WDt = np.ascontiguousarray(WD17.reshape(17, 4, 4, 128, FC, 128).transpose(0, 1, 5, 4, 2, 3))
    OH = np.zeros((24, 24 * 128), f32)
    for rix in range(24):
        OH[rix, rix * 128:(rix + 1) * 128] = 1.0
    common = dict(
        WXT=WXT, WDT=WDT, WZT=WZT, WOT=WOT, CDIAG=CDIAG, CB=CB, NW=NW,
        NWR=normw.reshape(1, DIN),
        DTB=dtb.reshape(NH, 1), ALOG=alog.reshape(NH, 1), DVEC=Dv.reshape(1, NH),
        GW=gw, EB=eb.reshape(EXP, 1), WG=WGt, WU=WUt, WD=WDt,
        IDENT=np.eye(128, dtype=f32), ONES1=np.ones((1, 128), f32), OH16=OH,
    )
    per_core = []
    for c in range(NCORES):
        g0 = c * TOK
        rank = c % 4
        if rank == 0:
            uTc = np.concatenate([np.zeros((DM, 3), f32), uT[:, g0:g0 + TOK]], axis=1)
        else:
            uTc = uT[:, g0 - 3:g0 + TOK]
        rsel = np.zeros((128, 4), f32)
        rsel[:, :rank] = 1.0
        m = dict(common)
        m["uT"] = np.ascontiguousarray(uTc)
        m["uTm"] = np.ascontiguousarray(uT[:, g0:g0 + TOK])
        m["UF"] = np.ascontiguousarray(uf[g0:g0 + TOK, :])
        m["RSEL"] = rsel
        per_core.append(m)
    return per_core


def kernel(**inputs):
    in_maps = _host_prep(inputs)
    nc = build()
    res = run_bass_kernel_spmd(nc, in_maps, core_ids=list(range(NCORES)))
    kernel.debug_results = res.results if DEBUG else None
    outs = [res.results[c]["out"] for c in range(NCORES)]
    return np.concatenate(outs, axis=0).reshape(2, 4096, DM).astype(np.float32)


kernel.debug_results = None


# version 04 (final)
# speedup vs baseline: 1.2218x; 1.0142x over previous
import numpy as np
import concourse.bass as bass
import concourse.mybir as mybir
from concourse.bass_utils import run_bass_kernel_spmd

F32 = mybir.dt.float32
BF16 = mybir.dt.bfloat16
I32 = mybir.dt.int32
AF = mybir.ActivationFunctionType
ALU = mybir.AluOpType

NCORES = 8
TOK = 1024
TOKH = TOK + 3
DM = 2048
DIN = 4096
NH = 64
PH = 64
NG = 34
CK = 8
EXP = 16
FF = 1408
FC = 11
EPS = 1e-5

MT_TAP = (16, 0, 1)
XP_TAP = 0
FLUSH = False
RACECHECK = True
NO_CC = False
DEBUG = []
SKIP_TAIL = False

ENGINES = ("sync", "tensor", "vector", "scalar", "gpsimd")


class P:
    def __init__(self, nc):
        self.nc = nc
        self.ops = {e: [] for e in ENGINES}
        self.sems = {}
        self.cnt = {}
        self._guards = {}
        self._gorder = []
        self.bg = {}

    def sem(self, name):
        g = self.nc.semaphore(name)
        h = g.__enter__()
        self._guards[name] = g
        self._gorder.append(name)
        self.sems[name] = h
        self.cnt[name] = 0
        return h

    def alloc(self, name, shape, dtype, psum=False):
        g = (self.nc.psum_tensor if psum else self.nc.sbuf_tensor)(name, shape, dtype)
        t = g.__enter__()
        self._guards[name] = g
        self._gorder.append(name)
        return t

    def free(self, name):
        self._guards.pop(name).__exit__(None, None, None)
        self._gorder.remove(name)

    def close(self):
        for n in reversed(self._gorder):
            self._guards[n].__exit__(None, None, None)
        self._guards = {}
        self._gorder = []

    def emit(self, eng, fn):
        self.ops[eng].append(fn)

    def bump(self, sem_name, k):
        self.cnt[sem_name] += k
        return self.cnt[sem_name]

    def dma(self, eng, out, in_, sem_name="LDG", slow=False):
        s = self.sems[sem_name]
        v = self.bump(sem_name, 16)
        if slow:
            self.emit(eng, lambda e, out=out, in_=in_, s=s:
                      e.dma_start(out=out, in_=in_, allow_slow_non_contiguous=True).then_inc(s, 16))
        else:
            self.emit(eng, lambda e, out=out, in_=in_, s=s:
                      e.dma_start(out=out, in_=in_).then_inc(s, 16))
        return v

    def wait(self, eng, sem_name, val):
        if val <= 0:
            return
        s = self.sems[sem_name]
        self.emit(eng, lambda e, s=s, val=val: e.wait_ge(s, val))

    def op(self, eng, fn, inc=None, flush=True):
        if inc is None:
            self.emit(eng, fn)
            return None
        s = self.sems[inc[0]]
        v = self.bump(inc[0], inc[1])
        if flush and FLUSH and eng in ("vector", "scalar", "tensor"):
            def emitfn(e, fn=fn, s=s, k=inc[1]):
                fn(e)
                e.drain().then_inc(s, k)
            self.emit(eng, emitfn)
        else:
            self.emit(eng, lambda e, fn=fn, s=s, k=inc[1]: fn(e).then_inc(s, k))
        return v

    def spacer(self, eng, cycles=512):
        if FLUSH:
            self.emit(eng, lambda e, c=cycles: e.nop(cycle_cnt=c, nofuse=True))

    def gwait(self, eng, name):
        for sem, val in self.bg.get(name, []):
            self.wait(eng, sem, val)

    def gset(self, name, sem, val):
        self.bg[name] = [(sem, val)]

    def gadd(self, name, sem, val):
        self.bg.setdefault(name, []).append((sem, val))


def build():
    nc = bass.Bass("TRN2", target_bir_lowering=False, dynamic_dma_scratch_size=6144,
                   detect_race_conditions=RACECHECK)
    p = P(nc)

    def din(name, shape):
        return nc.dram_tensor(name, shape, F32, kind="ExternalInput").ap()

    uT = din("uT", [DM, TOKH])
    uTm = din("uTm", [DM, TOK])
    UF = din("UF", [TOK, DM])
    NWR = din("NWR", [1, DIN])
    WXT = din("WXT", [NG, 128, 16, 128])
    WDT = din("WDT", [128, 16, 64])
    WZT = din("WZT", [8, 128, 16, 512])
    WOT = din("WOT", [16, 128, 32, 128])
    CDIAG = din("CDIAG", [NG, 128, 4, 128])
    CB = din("CB", [128, NG])
    NW = din("NW", [128, 32])
    DTB = din("DTB", [NH, 1])
    ALOG = din("ALOG", [NH, 1])
    DVEC = din("DVEC", [1, NH])
    GW = din("GW", [EXP, DM])
    EB = din("EB", [EXP, 1])
    WG = din("WG", [17, FC, 128, 16, 128])
    WU = din("WU", [17, FC, 128, 16, 128])
    WD = din("WD", [17, 4, 128, FC, 4, 128])
    IDENT = din("IDENT", [128, 128])
    ONES1 = din("ONES1", [1, 128])
    OH16 = din("OH16", [24, 24 * 128])
    RSEL = din("RSEL", [128, 4])

    out = nc.dram_tensor("out", [TOK, DM], F32, kind="ExternalOutput").ap()
    cc_in = nc.dram_tensor("cc_in", [128, 4104], F32)
    cc_out = nc.dram_tensor("cc_out", [512, 4104], F32)
    x1_dram = nc.dram_tensor("x1_dram", [128, 8 * DM], F32)
    comb_dram = nc.dram_tensor("comb_dram", [EXP, TOK], BF16)
    cum_dram = nc.dram_tensor("cum_dram", [NH, TOK], F32)

    dbg_shapes = {"xbc": [128, NG * TOK], "dt": [NH, TOK], "y": [128, CK * DIN],
                  "x1": [128, 16 * TOK], "comb": [128, 8 * 16], "acc": [128, 16 * TOK],
                  "sc": [128, 6 * 512], "bc": [128, 2 * TOK], "xp": [128, TOK],
                  "T": [128, NH * PH], "y0": [128, CK * DIN], "rel": [128, NH * PH],
                  "tri": [128, 9 * 128], "mt": [128, 5 * 128], "tt": [128, 8 * 144],
                  "yn": [128, CK * DIN]}
    dbg = {n: nc.dram_tensor("dbg_" + n, dbg_shapes[n], F32, kind="ExternalOutput").ap()
           for n in DEBUG}

    for s in ("LDG", "LDS", "PE", "VE", "SE", "GE", "CC", "STS",
              "LW0", "LW1", "LC0", "LC1", "LR", "LRS", "CST", "CCS", "DBS", "XT", "LZ0", "LZ1", "LO0", "LO1", "LU0", "LU1", "LG0", "LG1", "LD0", "LD1", "LCB", "LX0", "LX1"):
        p.sem(s)

    id_f = p.alloc("id_f", [128, 128], F32)
    id_b = p.alloc("id_b", [128, 128], BF16)
    triu = p.alloc("triu", [128, 128], F32)
    mask_i = p.alloc("mask_i", [128, 128], I32)
    ones_f = p.alloc("ones_f", [1, 128], F32)
    ones_c = p.alloc("ones_c", [128, 1], F32)
    ones_b = p.alloc("ones_b", [1, 128], BF16)
    oh_sb = p.alloc("oh_sb", [24, 24 * 128], F32)
    cb_sb = p.alloc("cb_sb", [128, NG], F32)
    nw_sb = p.alloc("nw_sb", [128, 32], F32)
    dtb_sb = p.alloc("dtb_sb", [NH, 1], F32)
    alog_sb = p.alloc("alog_sb", [NH, 1], F32)
    A_sb = p.alloc("A_sb", [NH, 1], F32)
    dv_sb = p.alloc("dv_sb", [1, NH], F32)
    rsel_sb = p.alloc("rsel_sb", [128, 4], F32)
    rsinv_sb = p.alloc("rsinv_sb", [128, 4], F32)
    u_fm = p.alloc("u_fm", [128, 16 * TOKH], BF16)
    y_sb = p.alloc("y_sb", [128, CK * DIN], BF16)
    BCp = p.alloc("BCp", [128, 2 * TOK], BF16)
    ab_h = p.alloc("ab_h", [NH, 24], F32)
    ab_T = p.alloc("ab_T", [24, NH], F32)
    aseg = p.alloc("aseg", [NH, 1], F32)
    azp = p.alloc("azp", [128, 8], F32)
    s_T = p.alloc("s_T", [128, CK * NH], F32)
    T0 = p.alloc("T0", [128, NH * PH], F32)
    T1 = p.alloc("T1", [128, NH * PH], F32)
    GmT = p.alloc("GmT", [128, CK * 128], F32)
    B_tm = p.alloc("B_tm", [128, CK * 128], BF16)
    cum_T = p.alloc("cum_T", [128, CK * NH], F32)
    evec_T = p.alloc("evec_T", [128, CK * NH], F32)
    ecum_T = p.alloc("ecum_T", [128, CK * NH], F32)
    w_T = p.alloc("w_T", [128, CK * NH], F32)
    adb = p.alloc("adb", [128, CK * NH], F32)
    D_bc = p.alloc("D_bc", [128, NH], F32)
    T_bf = p.alloc("T_bf", [128, NH * PH], BF16)
    wst = p.alloc("wst", [128, 2 * 2048], BF16)
    cdst = p.alloc("cdst", [128, 2 * 512], BF16)
    xpost = p.alloc("xpost", [128, 2 * TOK], BF16)
    xraw = p.alloc("xraw", [128, 2 * TOKH], BF16)
    dtraw = p.alloc("dtraw", [NH, TOKH], F32)
    dts = p.alloc("dts", [NH, TOK], F32)
    cum = p.alloc("cum", [NH, TOK], F32)
    evl = p.alloc("evl", [NH, TOK], F32)
    scr64 = p.alloc("scr64", [NH, TOK], F32)
    z64 = p.alloc("z64", [NH, 128], F32)
    ecm = dtraw[:, 0:TOK]

    psA = [p.alloc(f"psA{i}", [128, 512], F32, psum=True) for i in range(2)]
    psC = [p.alloc(f"psC{i}", [128, 512], F32, psum=True) for i in range(2)]
    psY = p.alloc("psY", [128, 512], F32, psum=True)
    psS = p.alloc("psS", [128, 512], F32, psum=True)
    psG = p.alloc("psG", [128, 512], F32, psum=True)
    psTR = p.alloc("psTR", [128, 128], F32, psum=True)

    def vop(fn):
        return p.op("vector", fn, inc=("VE", 1))

    def sop(fn):
        return p.op("scalar", fn, inc=("SE", 1))

    def top(fn):
        return p.op("tensor", fn, inc=("PE", 1))

    def vsync():
        p.wait("vector", "VE", p.cnt["VE"])

    for dst, src in ((id_f, IDENT), (ones_f, ONES1), (oh_sb, OH16), (cb_sb, CB),
                     (nw_sb, NW), (dtb_sb, DTB), (alog_sb, ALOG), (dv_sb, DVEC),
                     (rsel_sb, RSEL)):
        p.dma("sync", dst[:, :], src[:, :], "LDS")
    lds_c = p.cnt["LDS"]

    for k in range(16):
        p.dma("gpsimd", u_fm[:, k * TOKH:(k + 1) * TOKH], uT[k * 128:(k + 1) * 128, :], "LDG")
    ldg_ufm = p.cnt["LDG"]

    ge_mask = p.op("gpsimd", lambda e: e.iota(mask_i[:, :], pattern=[[1, 128]], base=0,
                                              channel_multiplier=-1), inc=("GE", 1))
    p.wait("vector", "LDS", lds_c)
    p.op("vector", lambda e: e.tensor_copy(id_b[:, :], id_f[:, :]))
    p.op("vector", lambda e: e.memset(ones_c[:, :], 1.0))
    p.op("vector", lambda e: e.tensor_copy(ones_b[:, :], ones_f[:, :]))
    p.op("vector", lambda e: e.memset(z64[:, :], 0.0))
    p.op("vector", lambda e: e.tensor_scalar(rsinv_sb[:, :], rsel_sb[:, :], -1.0, 1.0,
                                             op0=ALU.mult, op1=ALU.add))
    p.wait("vector", "GE", ge_mask)
    ve_c0 = vop(lambda e: e.tensor_scalar(triu[:, :], mask_i[:, :], 0.0, None, op0=ALU.is_ge))
    p.wait("scalar", "LDS", lds_c)
    se_A = sop(lambda e: e.activation(A_sb[:, :], alog_sb[:, :], AF.Exp))
    p.wait("vector", "SE", se_A)
    ve_A = vop(lambda e: e.tensor_scalar(A_sb[:, :], A_sb[:, :], -1.0, None, op0=ALU.mult))
    p.wait("tensor", "LDS", lds_c)
    pe_dbc = top(lambda e: e.matmul(psG[:, 0:NH], ones_f[0:1, :], dv_sb[0:1, :],
                                    start=True, stop=True))
    p.wait("vector", "PE", pe_dbc)
    p.spacer("vector")
    ve_dbc = vop(lambda e: e.tensor_copy(D_bc[:, :], psG[:, 0:NH]))
    p.gset("psG", "VE", ve_dbc)

    TCH = [(0, 512), (512, 512), (1024, 3)]

    def in_proj_group(cc, wslot, dst_fn, m=128, raw_name=None):
        if cc < NG:
            src = WXT[cc, :, :, :].rearrange("p k c -> p (k c)")
            wcols = 2048
        else:
            src = WDT[:, :, :].rearrange("p k c -> p (k c)")
            wcols = 16 * 64
        p.gwait("gpsimd", f"wst{wslot}")
        wv = p.dma("gpsimd", wst[:, wslot * 2048: wslot * 2048 + wcols], src, f"LW{wslot}")
        p.wait("tensor", f"LW{wslot}", wv)
        last_pe = 0
        for ti, (c0, w) in enumerate(TCH):
            ps = psA[ti % 2]
            p.gwait("tensor", f"psA{ti % 2}")
            for k in range(16):
                lhs = wst[:, wslot * 2048 + k * m: wslot * 2048 + (k + 1) * m]
                rhs = u_fm[:, k * TOKH + c0: k * TOKH + c0 + w]
                pe = p.op("tensor", lambda e, ps=ps, lhs=lhs, rhs=rhs, k=k, m=m, w=w:
                          e.matmul(ps[0:m, 0:w], lhs, rhs, start=(k == 0), stop=(k == 15)),
                          inc=("PE", 1) if k == 15 else None)
            last_pe = p.cnt["PE"]
            p.wait("vector", "PE", last_pe)
            if raw_name:
                p.gwait("vector", raw_name)
            dst = dst_fn(c0, w)
            vv = vop(lambda e, ps=ps, dst=dst, m=m, w=w: e.tensor_copy(dst, ps[0:m, 0:w]))
            p.gset(f"psA{ti % 2}", "VE", vv)
        p.gset(f"wst{wslot}", "PE", last_pe)
        if raw_name:
            p.bg[raw_name] = []
        return p.cnt["VE"]

    def conv_group(g, raw_slot, dst_fn, post_name=None, ve_ready=0):
        cs = g % 2
        p.gwait("gpsimd", f"cdst{cs}")
        cv = p.dma("gpsimd", cdst[:, cs * 512:(cs + 1) * 512],
                   CDIAG[g, :, :, :].rearrange("p k c -> p (k c)"), f"LC{cs}")
        p.wait("tensor", f"LC{cs}", cv)
        p.wait("tensor", "VE", ve_ready)
        last_pe = 0
        for ti in range(2):
            ps = psC[ti]
            p.gwait("tensor", f"psC{ti}")
            for k in range(4):
                lhs = cdst[:, cs * 512 + k * 128: cs * 512 + (k + 1) * 128]
                rhs = xraw[:, raw_slot * TOKH + ti * 512 + k: raw_slot * TOKH + ti * 512 + k + 512]
                p.op("tensor", lambda e, ps=ps, lhs=lhs, rhs=rhs, k=k:
                     e.matmul(ps[:, 0:512], lhs, rhs, start=(k == 0), stop=(k == 3)),
                     inc=("PE", 1) if k == 3 else None)
            last_pe = p.cnt["PE"]
            p.wait("scalar", "PE", last_pe)
            p.spacer("scalar", 256)
            if post_name:
                p.gwait("scalar", post_name)
            dst = dst_fn(ti * 512, 512)
            sv = sop(lambda e, ps=ps, dst=dst, g=g:
                     e.activation(dst, ps[:, 0:512], AF.Silu, bias=cb_sb[:, g:g + 1]))
            p.gset(f"psC{ti}", "SE", sv)
        p.gset(f"cdst{cs}", "PE", last_pe)
        p.gadd(f"xraw{raw_slot}", "PE", last_pe)
        if post_name:
            p.bg[post_name] = []
        return p.cnt["SE"]

    p.wait("tensor", "LDG", ldg_ufm)
    p.wait("tensor", "VE", ve_c0)

    ve_r = in_proj_group(32, 0, lambda c0, w: xraw[:, c0:c0 + w], raw_name="xraw0")
    conv_group(32, 0, lambda c0, w: BCp[:, c0:c0 + w], ve_ready=ve_r)
    ve_r = in_proj_group(33, 1, lambda c0, w: xraw[:, TOKH + c0:TOKH + c0 + w], raw_name="xraw1")
    conv_group(33, 1, lambda c0, w: BCp[:, TOK + c0:TOK + c0 + w], ve_ready=ve_r)
    in_proj_group(NG, 0, lambda c0, w: dtraw[:, c0:c0 + w], m=64, raw_name="xraw0")
    se_bc = p.cnt["SE"]
    ve_dt_ready = p.cnt["VE"]
    if "bc" in dbg:
        p.wait("gpsimd", "SE", se_bc)
        p.dma("gpsimd", dbg["bc"][:, :], BCp[:, :], "DBS")
        p.wait("tensor", "DBS", p.cnt["DBS"])

    p.wait("tensor", "SE", se_bc)
    p.wait("sync", "SE", se_bc)
    for ck in range(CK):
        p.emit("sync", lambda e, ck=ck, s=p.sems["XT"]:
               e.dma_start(out=B_tm[:, ck * 128:(ck + 1) * 128],
                           in_=BCp[:, ck * 128:(ck + 1) * 128], transpose=True).then_inc(s, 16))
        p.bump("XT", 16)
    btm_xt = p.cnt["XT"]
    for ck in range(CK):
        p.gwait("tensor", "psG")
        pe_g = top(lambda e, ck=ck: e.matmul(psG[:, 0:128], BCp[:, ck * 128:(ck + 1) * 128],
                                             BCp[:, TOK + ck * 128: TOK + (ck + 1) * 128],
                                             start=True, stop=True))
        p.wait("vector", "PE", pe_g)
        p.spacer("vector")
        vv = vop(lambda e, ck=ck: e.tensor_tensor(GmT[:, ck * 128:(ck + 1) * 128], psG[:, 0:128],
                                                  triu[:, :], op=ALU.mult))
        p.gset("psG", "VE", vv)

    dt_o = dtraw[:, 3:3 + TOK]
    p.wait("vector", "VE", ve_dt_ready)
    v2 = vop(lambda e: e.tensor_scalar(scr64[:, :], dt_o, dtb_sb[:, 0:1], None, op0=ALU.add))
    p.wait("scalar", "VE", v2)
    sab = sop(lambda e: e.activation(dts[:, :], scr64[:, :], AF.Abs))
    p.wait("scalar", "SE", sab)
    s1 = sop(lambda e: e.activation(evl[:, :], dts[:, :], AF.Exp, scale=-1.0))
    p.wait("scalar", "SE", s1)
    s2 = sop(lambda e: e.activation(ecm[:, :], evl[:, :], AF.Ln, bias=1.0))
    p.wait("vector", "SE", s2)
    vop(lambda e: e.tensor_scalar(evl[:, :], scr64[:, :], 0.0, None, op0=ALU.max))
    vsync()
    vop(lambda e: e.tensor_tensor(dts[:, :], ecm[:, :], evl[:, :], op=ALU.add))
    vsync()
    p.wait("vector", "VE", ve_A)
    vop(lambda e: e.tensor_scalar(scr64[:, :], dts[:, :], A_sb[:, 0:1], None, op0=ALU.mult))
    vsync()
    for ck in range(CK):
        vop(lambda e, ck=ck: e.tensor_tensor_scan(cum[:, ck * 128:(ck + 1) * 128],
                                                  scr64[:, ck * 128:(ck + 1) * 128],
                                                  z64[:, :], initial=0.0,
                                                  op0=ALU.add, op1=ALU.bypass))
    vsync()
    v4 = p.cnt["VE"]
    p.wait("scalar", "VE", v4)
    s3 = sop(lambda e: e.activation(evl[:, :], dts[:, :], AF.Ln))
    p.wait("vector", "SE", s3)
    v5 = vop(lambda e: e.tensor_tensor(evl[:, :], evl[:, :], cum[:, :], op=ALU.subtract))
    p.wait("scalar", "VE", v5)
    s4 = sop(lambda e: e.activation(ecm[:, :], cum[:, :], AF.Exp))
    lad = cum[:, 127::128]
    p.wait("vector", "SE", s4)
    vop(lambda e: e.tensor_tensor_scan(scr64[:, 0:CK], lad, z64[:, 0:CK], initial=0.0,
                                       op0=ALU.add, op1=ALU.bypass))
    vsync()
    vop(lambda e: e.tensor_tensor(scr64[:, 8:16], scr64[:, 0:CK], lad, op=ALU.subtract))
    vsync()
    vop(lambda e: e.tensor_copy(ab_h[:, 16:24], lad))
    vsync()
    v6 = p.cnt["VE"]
    p.wait("scalar", "VE", v6)
    sop(lambda e: e.activation(ab_h[:, 0:8], lad, AF.Exp))
    sop(lambda e: e.activation(ab_h[:, 8:16], scr64[:, 8:16], AF.Exp))
    s6 = sop(lambda e: e.activation(aseg[:, 0:1], scr64[:, 7:8], AF.Exp))

    p.wait("tensor", "SE", s6)
    tr_jobs = []
    for ck in range(CK):
        tr_jobs.append((cum[:, ck * 128:(ck + 1) * 128], cum_T, ck))
        tr_jobs.append((evl[:, ck * 128:(ck + 1) * 128], evec_T, ck))
        tr_jobs.append((ecm[:, ck * 128:(ck + 1) * 128], ecum_T, ck))
    for src, dstt, ck in tr_jobs:
        p.gwait("tensor", "psG")
        pe_t = top(lambda e, src=src: e.transpose(psG[0:128, 0:64], src[0:64, :],
                                                  id_f[0:64, 0:64]))
        p.wait("vector", "PE", pe_t)
        p.spacer("vector")
        vv = vop(lambda e, dstt=dstt, ck=ck: e.tensor_copy(dstt[:, ck * NH:(ck + 1) * NH],
                                                           psG[:, 0:64]))
        p.gset("psG", "VE", vv)
    p.gwait("tensor", "psG")
    pe_abt = top(lambda e: e.transpose(psG[0:24, 0:64], ab_h[0:64, :], id_f[0:64, 0:64]))
    p.wait("vector", "PE", pe_abt)
    p.spacer("vector")
    ve_abt = vop(lambda e: e.tensor_copy(ab_T[:, :], psG[0:24, 0:64]))
    p.gset("psG", "VE", ve_abt)
    for ck in range(CK):
        p.gwait("tensor", "psG")
        top(lambda e, ck=ck: e.matmul(psG[:, 0:NH], oh_sb[:, ck * 128:(ck + 1) * 128],
                                      ab_T[:, :], start=True, stop=True))
        pe_b2 = top(lambda e, ck=ck: e.matmul(psG[:, 64:64 + NH],
                                              oh_sb[:, (8 + ck) * 128:(9 + ck) * 128],
                                              ab_T[:, :], start=True, stop=True))
        pe_b3 = top(lambda e, ck=ck: e.matmul(psG[:, 128:128 + NH],
                                              oh_sb[:, (16 + ck) * 128:(17 + ck) * 128],
                                              ab_T[:, :], start=True, stop=True))
        p.wait("vector", "PE", pe_b3)
        p.spacer("vector")
        vop(lambda e, ck=ck: e.tensor_copy(adb[:, ck * NH:(ck + 1) * NH], psG[:, 0:NH]))
        vop(lambda e, ck=ck: e.tensor_tensor(s_T[:, ck * NH:(ck + 1) * NH],
                                             ecum_T[:, ck * NH:(ck + 1) * NH],
                                             psG[:, 64:64 + NH], op=ALU.mult))
        vv = vop(lambda e, ck=ck: e.tensor_tensor(w_T[:, ck * NH:(ck + 1) * NH],
                                                  evec_T[:, ck * NH:(ck + 1) * NH],
                                                  psG[:, 128:128 + NH], op=ALU.add))
        p.gset("psG", "VE", vv)
        p.wait("scalar", "VE", vv)
        sop(lambda e, ck=ck: e.activation(w_T[:, ck * NH:(ck + 1) * NH],
                                          w_T[:, ck * NH:(ck + 1) * NH], AF.Exp))

    p.wait("gpsimd", "VE", p.cnt["VE"])
    p.wait("gpsimd", "SE", p.cnt["SE"])
    cum_st = p.dma("gpsimd", cum_dram[:, :], cum[:, :], "CST")

    if "tri" in dbg:
        p.wait("sync", "VE", p.cnt["VE"])
        p.dma("sync", dbg["tri"][:, 0:128], triu[:, :], "STS")
        p.dma("sync", dbg["tri"][:, 128:9 * 128], GmT[:, :], "STS")
        p.wait("vector", "STS", p.cnt["STS"])
    if "dt" in dbg:
        p.wait("sync", "VE", p.cnt["VE"])
        p.wait("sync", "SE", p.cnt["SE"])
        p.dma("sync", dbg["dt"][:, :], dts[:, :], "STS")
        p.wait("vector", "STS", p.cnt["STS"])
    if "sc" in dbg:
        p.wait("sync", "VE", p.cnt["VE"])
        p.wait("sync", "SE", p.cnt["SE"])
        for i, tt in enumerate((cum_T, evec_T, ecum_T, w_T, adb, s_T)):
            p.dma("sync", dbg["sc"][:, i * 512:(i + 1) * 512], tt[:, :], "STS")
        p.wait("vector", "STS", p.cnt["STS"])
        p.wait("scalar", "STS", p.cnt["STS"])

    for nm in ("z64", "scr64", "evl", "cum", "dts", "dtraw"):
        p.free(nm)
    Mt = p.alloc("Mt", [128, 2 * 128], BF16)
    xtm = p.alloc("xtm", [128, 2 * PH], BF16)
    xwtm = p.alloc("xwtm", [128, 2 * PH], BF16)
    xdsc = p.alloc("xdsc", [128, 2 * PH], BF16)
    p.op("vector", lambda e: e.memset(T0[:, :], 0.0))
    vz = vop(lambda e: e.memset(T_bf[:, :], 0.0))

    dbg_mt_sb = p.alloc("dbg_mt_sb", [128, 5 * 128], F32) if "mt" in dbg else None
    tt_sb = p.alloc("tt_sb", [128, 8 * 144], F32) if "tt" in dbg else None
    CT_bc = p.alloc("CT_bc", [128, 2 * TOK], F32)
    xtm2 = p.alloc("xtm2", [128, 2 * 128], BF16)
    E_a = p.alloc("E_a", [128, 2 * 128], F32)
    E_b = p.alloc("E_b", [128, 2 * 128], F32)
    Ea_se = {0: 0, 1: 0}
    Eb_ve = {0: 0, 1: 0}
    rowC_pe = {0: 0, 1: 0}
    row_ld = 0
    xt_guard = {0: 0, 1: 0}
    se_conv_g = {}

    def prep_group(g):
        rs = g % 2
        ve_r = in_proj_group(g, rs, lambda c0, w, rs=rs:
                             xraw[:, rs * TOKH + c0: rs * TOKH + c0 + w],
                             raw_name=f"xraw{rs}")
        se_conv_g[g] = conv_group(g, rs,
                                  lambda c0, w, rs=rs: xpost[:, rs * TOK + c0: rs * TOK + c0 + w],
                                  post_name=f"xpost{rs}", ve_ready=ve_r)

    prep_group(0)
    for g in range(32):
        rs = g % 2
        if g + 1 < 32:
            prep_group(g + 1)
        se_conv = se_conv_g[g]
        p.wait("gpsimd", "CST", cum_st)
        p.gwait("gpsimd", "rowAB")
        row_ld = p.dma("gpsimd", CT_bc[:, :],
                       cum_dram[2 * g:2 * g + 2, :].rearrange("h c -> (h c)").partition_broadcast(128),
                       "LR")
        pe_lastx = 0
        xt_val = {}
        p.wait("sync", "SE", se_conv)
        p.wait("sync", "CST", cum_st)
        for cc0 in (0, 1):
            p.wait("sync", "PE", xt_guard[cc0 % 2])
            p.emit("sync", lambda e, rs=rs, cc0=cc0, s=p.sems["XT"]:
                   e.dma_start(out=xtm2[:, (cc0 % 2) * 128:(cc0 % 2) * 128 + 128],
                               in_=xpost[:, rs * TOK + cc0 * 128: rs * TOK + (cc0 + 1) * 128],
                               transpose=True).then_inc(s, 16))
            xt_val[cc0] = p.bump("XT", 16)
        for ck in range(CK):
            Tsrc = T0 if ck % 2 == 0 else T1
            Tdst = T1 if ck % 2 == 0 else T0
            xpar = ck % 2
            for hs in (0, 1):
                h = 2 * g + hs
                par = hs
                p.wait("vector", "XT", xt_val[ck])
                p.gwait("vector", f"xtm{hs}")
                ve_xt = vop(lambda e, hs=hs, xpar=xpar:
                            e.tensor_copy(xtm[:, hs * PH:(hs + 1) * PH],
                                          xtm2[:, xpar * 128 + hs * PH: xpar * 128 + (hs + 1) * PH]))
                ve_xw = vop(lambda e, hs=hs, xpar=xpar, ck=ck, h=h:
                            e.tensor_scalar(xwtm[:, hs * PH:(hs + 1) * PH],
                                            xtm2[:, xpar * 128 + hs * PH: xpar * 128 + (hs + 1) * PH],
                                            w_T[:, ck * NH + h: ck * NH + h + 1], None,
                                            op0=ALU.mult))
                p.wait("vector", "LR", row_ld)
                p.wait("vector", "SE", Ea_se[par])
                ve_ts = vop(lambda e, par=par, ck=ck, h=h, hs=hs:
                            e.tensor_scalar(E_a[:, par * 128:(par + 1) * 128],
                                            CT_bc[:, hs * TOK + ck * 128: hs * TOK + (ck + 1) * 128],
                                            evec_T[:, ck * NH + h: ck * NH + h + 1], 8.0,
                                            op0=ALU.add, op1=ALU.min))
                p.wait("scalar", "VE", ve_ts)
                p.wait("scalar", "VE", Eb_ve[par])
                se_e = sop(lambda e, par=par: e.activation(E_b[:, par * 128:(par + 1) * 128],
                                                           E_a[:, par * 128:(par + 1) * 128],
                                                           AF.Exp))
                Ea_se[par] = se_e
                p.wait("vector", "SE", se_e)
                ve_m = vop(lambda e, hs=hs, par=par, ck=ck:
                           e.tensor_tensor(Mt[:, hs * 128:(hs + 1) * 128],
                                           E_b[:, par * 128:(par + 1) * 128],
                                           GmT[:, ck * 128:(ck + 1) * 128], op=ALU.mult))
                Eb_ve[par] = ve_m
                p.wait("tensor", "VE", ve_m)
                p.gwait("tensor", "psY")
                p.op("tensor", lambda e, hs=hs: e.matmul(psY[:, 0:PH], Mt[:, hs * 128:(hs + 1) * 128],
                                                         xtm[:, hs * PH:(hs + 1) * PH],
                                                         start=True, stop=False))
                pe_y = top(lambda e, ck=ck, h=h:
                           e.matmul(psY[:, 0:PH], BCp[:, TOK + ck * 128: TOK + (ck + 1) * 128],
                                    T_bf[:, h * PH:(h + 1) * PH], start=False, stop=True))
                p.gwait("tensor", "psS")
                pe_s = top(lambda e, ck=ck, hs=hs:
                           e.matmul(psS[:, 0:PH], B_tm[:, ck * 128:(ck + 1) * 128],
                                    xwtm[:, hs * PH:(hs + 1) * PH], start=True, stop=True))
                p.gset(f"xtm{hs}", "PE", pe_s)
                if hs == 1:
                    xt_guard[xpar] = pe_s
                    if ck + 2 < CK:
                        p.wait("sync", "PE", pe_s)
                        p.emit("sync", lambda e, rs=rs, cc=ck + 2, xpar=xpar, s=p.sems["XT"]:
                               e.dma_start(out=xtm2[:, xpar * 128: xpar * 128 + 128],
                                           in_=xpost[:, rs * TOK + cc * 128: rs * TOK + (cc + 1) * 128],
                                           transpose=True).then_inc(s, 16))
                        xt_val[ck + 2] = p.bump("XT", 16)
                p.wait("vector", "PE", pe_s)
                p.spacer("vector", 256)
                if (g, ck, hs) == MT_TAP and "mt" in dbg:
                    vop(lambda e: e.tensor_copy(dbg_mt_sb[:, 0:64], psS[:, 0:PH]))
                    vop(lambda e, hs=hs: e.tensor_copy(dbg_mt_sb[:, 64:128],
                                                       xwtm[:, hs * PH:(hs + 1) * PH]))
                    vop(lambda e: e.tensor_copy(dbg_mt_sb[:, 128:192], psY[:, 0:PH]))
                    vop(lambda e, hs=hs: e.tensor_copy(dbg_mt_sb[:, 192:256],
                                                       xtm[:, hs * PH:(hs + 1) * PH]))
                    vop(lambda e, hs=hs: e.tensor_copy(dbg_mt_sb[:, 256:384],
                                                       Mt[:, hs * 128:(hs + 1) * 128]))
                    vop(lambda e, ck=ck: e.tensor_copy(dbg_mt_sb[:, 384:512],
                                                       B_tm[:, ck * 128:(ck + 1) * 128]))
                    vmt = vop(lambda e, hs=hs: e.tensor_copy(dbg_mt_sb[:, 512:576], xtm[:, hs * PH:(hs + 1) * PH]))
                    p.wait("sync", "VE", vmt)
                    p.dma("sync", dbg["mt"][:, :], dbg_mt_sb[:, :], "STS")
                    p.wait("vector", "STS", p.cnt["STS"])
                vop(lambda e, hs=hs, h=h: e.tensor_scalar(xdsc[:, hs * PH:(hs + 1) * PH],
                                                          xtm[:, hs * PH:(hs + 1) * PH],
                                                          D_bc[:, h:h + 1], None, op0=ALU.mult))
                vsync()
                vy = vop(lambda e, ck=ck, h=h, hs=hs:
                         e.scalar_tensor_tensor(y_sb[:, ck * DIN + h * PH: ck * DIN + (h + 1) * PH],
                                                psY[:, 0:PH],
                                                ecum_T[:, ck * NH + h: ck * NH + h + 1],
                                                xdsc[:, hs * PH:(hs + 1) * PH],
                                                op0=ALU.mult, op1=ALU.add))
                p.gset("psY", "VE", vy)
                vt = vop(lambda e, ck=ck, h=h, Tsrc=Tsrc, Tdst=Tdst:
                         e.scalar_tensor_tensor(Tdst[:, h * PH:(h + 1) * PH],
                                                Tsrc[:, h * PH:(h + 1) * PH],
                                                adb[:, ck * NH + h: ck * NH + h + 1],
                                                psS[:, 0:PH], op0=ALU.mult, op1=ALU.add))
                p.gset("psS", "VE", vt)
                vsync()
                vop(lambda e, h=h, Tdst=Tdst: e.tensor_copy(T_bf[:, h * PH:(h + 1) * PH],
                                                            Tdst[:, h * PH:(h + 1) * PH]))
                if "tt" in dbg and g == MT_TAP[0] and hs == MT_TAP[2]:
                    vsync()
                    vop(lambda e, ck=ck, h=h, Tdst=Tdst:
                        e.tensor_copy(tt_sb[:, ck * 144: ck * 144 + 64],
                                      Tdst[:, h * PH:(h + 1) * PH]))
                    vop(lambda e, ck=ck: e.tensor_copy(tt_sb[:, ck * 144 + 64: ck * 144 + 128],
                                                       psS[:, 0:PH]))
                    vop(lambda e, ck=ck, h=h:
                        e.tensor_copy(tt_sb[:, ck * 144 + 128: ck * 144 + 129],
                                      adb[:, ck * NH + h: ck * NH + h + 1]))
                    vop(lambda e, ck=ck, h=h:
                        e.tensor_copy(tt_sb[:, ck * 144 + 129: ck * 144 + 130],
                                      w_T[:, ck * NH + h: ck * NH + h + 1]))
                    vop(lambda e, ck=ck, h=h:
                        e.tensor_copy(tt_sb[:, ck * 144 + 130: ck * 144 + 131],
                                      ecum_T[:, ck * NH + h: ck * NH + h + 1]))
                    vtt = vop(lambda e, ck=ck, hs=hs:
                              e.tensor_copy(tt_sb[:, ck * 144 + 131: ck * 144 + 131 + 13],
                                            xwtm[:, hs * PH: hs * PH + 13]))
                    if ck == 7:
                        p.wait("sync", "VE", vtt)
                        p.dma("sync", dbg["tt"][:, :], tt_sb[:, :], "STS")
        p.gset("rowAB", "VE", p.cnt["VE"])
        pe_lastx = p.cnt["XT"]
        if g == XP_TAP and "xp" in dbg:
            p.wait("gpsimd", "SE", se_conv)
            xpv = p.dma("gpsimd", dbg["xp"][:, :], xpost[:, 0:TOK], "DBS")
            p.gadd("xpost0", "DBS", xpv)
        p.gadd(f"xpost{rs}", "XT", p.cnt["XT"])
    ve_scan = p.cnt["VE"]
    if "T" in dbg:
        p.wait("sync", "VE", ve_scan)
        p.dma("sync", dbg["T"][:, :], T0[:, :], "STS")
    if "y0" in dbg:
        p.wait("gpsimd", "VE", ve_scan)
        p.dma("gpsimd", dbg["y0"][:, :], y_sb[:, :], "DBS")
        p.wait("tensor", "DBS", p.cnt["DBS"])

    if NO_CC:
        ve_y = ve_scan
        for nm in (["E_b", "E_a", "xtm2", "CT_bc"] + (["tt_sb"] if "tt" in dbg else []) +
               (["dbg_mt_sb"] if "mt" in dbg else []) +
                   ["xdsc", "xwtm", "xtm", "Mt", "xraw",
                    "xpost", "cdst", "wst", "T_bf", "D_bc", "adb", "w_T",
                    "ecum_T", "evec_T", "cum_T", "B_tm", "GmT"]):
            p.free(nm)
        return finish_build(p, nc, locals())
    Tfin = T0
    p.wait("gpsimd", "VE", ve_scan)
    vz2 = vop(lambda e: e.memset(azp[:, :], 0.0))
    p.wait("vector", "VE", vz2)
    vz3 = vop(lambda e: e.tensor_copy(azp[0:64, 0:1], aseg[:, 0:1]))
    st_T = p.dma("gpsimd", cc_in[:, 0:4096], Tfin[:, :], "CCS")
    p.wait("gpsimd", "SE", p.cnt["SE"])
    p.wait("gpsimd", "VE", vz3)
    st_a = p.dma("gpsimd", cc_in[:, 4096:4104], azp[:, :], "CCS")
    p.wait("gpsimd", "CCS", p.cnt["CCS"])
    cc_v = p.op("gpsimd", lambda e: e.collective_compute(
        "AllGather", ALU.bypass,
        replica_groups=[[0, 1, 2, 3], [4, 5, 6, 7]],
        ins=[cc_in.ap().opt()], outs=[cc_out.ap().opt()]), inc=("CC", 1))
    p.wait("gpsimd", "CC", cc_v)
    for nm in (["E_b", "E_a", "xtm2", "CT_bc"] + (["tt_sb"] if "tt" in dbg else []) +
               (["dbg_mt_sb"] if "mt" in dbg else []) +
               ["xdsc", "xwtm", "xtm", "Mt", "xraw",
                "xpost", "cdst", "wst", "T_bf", "D_bc", "adb", "w_T",
                "ecum_T", "evec_T", "cum_T", "B_tm", "GmT"]):
        p.free(nm)
    avrow = p.alloc("avrow", [1, 4 * NH], F32)
    ga = p.alloc("ga", [128, NH], F32)
    rstg = p.alloc("rstg", [128, 4096], F32)
    relay_b = p.alloc("relay_b", [128, NH * PH], F32)
    relay_bf = p.alloc("relay_bf", [128, NH * PH], BF16)
    for r in range(4):
        p.dma("gpsimd", avrow[0:1, r * NH:(r + 1) * NH],
              cc_out[r * 128:r * 128 + 64, 4096:4097].rearrange("h o -> o h"), "LDG", slow=True)
    ldav = p.cnt["LDG"]
    p.wait("tensor", "LDG", ldav)
    relay_a = T1
    p.wait("vector", "VE", ve_scan)
    vop(lambda e: e.memset(relay_a[:, :], 0.0))
    for r in range(3):
        p.wait("gpsimd", "VE", p.cnt["VE"])
        ldT = p.dma("gpsimd", rstg[:, :], cc_out[r * 128:(r + 1) * 128, 0:4096], "LRS")
        p.gwait("tensor", "psG")
        pe_ab = top(lambda e, r=r: e.matmul(psG[:, 0:NH], ones_f[0:1, :],
                                            avrow[0:1, r * NH:(r + 1) * NH],
                                            start=True, stop=True))
        p.wait("vector", "PE", pe_ab)
        p.spacer("vector")
        vop(lambda e, r=r: e.tensor_scalar(ga[:, :], psG[:, 0:NH], rsel_sb[:, r:r + 1],
                                           rsinv_sb[:, r:r + 1], op0=ALU.mult, op1=ALU.add))
        p.gset("psG", "VE", p.cnt["VE"])
        p.wait("vector", "LRS", ldT)
        vop(lambda e: e.tensor_scalar(rstg[:, :], rstg[:, :], rsel_sb[:, r:r + 1], None,
                                      op0=ALU.mult))
        vsync()
        src, dst = (relay_a, relay_b) if r % 2 == 0 else (relay_b, relay_a)
        for h in range(NH):
            vop(lambda e, h=h, src=src, dst=dst:
                e.scalar_tensor_tensor(dst[:, h * PH:(h + 1) * PH], src[:, h * PH:(h + 1) * PH],
                                       ga[:, h:h + 1], rstg[:, h * PH:(h + 1) * PH],
                                       op0=ALU.mult, op1=ALU.add))
        vsync()
    vop(lambda e: e.tensor_copy(relay_bf[:, :], relay_b[:, :]))
    vsync()
    ve_rbf = p.cnt["VE"]
    if "rel" in dbg:
        p.wait("sync", "VE", ve_rbf)
        p.dma("sync", dbg["rel"][:, :], relay_b[:, :], "STS")
    p.wait("tensor", "VE", ve_rbf)
    for ck in range(CK):
        for h in range(NH):
            p.gwait("tensor", "psY")
            pe_y2 = top(lambda e, ck=ck, h=h:
                        e.matmul(psY[:, 0:PH], BCp[:, TOK + ck * 128: TOK + (ck + 1) * 128],
                                 relay_bf[:, h * PH:(h + 1) * PH], start=True, stop=True))
            p.wait("vector", "PE", pe_y2)
            p.spacer("vector")
            vy = vop(lambda e, ck=ck, h=h:
                     e.scalar_tensor_tensor(y_sb[:, ck * DIN + h * PH: ck * DIN + (h + 1) * PH],
                                            psY[:, 0:PH],
                                            s_T[:, ck * NH + h: ck * NH + h + 1],
                                            y_sb[:, ck * DIN + h * PH: ck * DIN + (h + 1) * PH],
                                            op0=ALU.mult, op1=ALU.add))
            p.gset("psY", "VE", vy)
    ve_y = p.cnt["VE"]
    return finish_build(p, nc, locals())


def finish_build(p, nc, env):
    dbg = env["dbg"]
    vop = env["vop"]
    y_sb = env["y_sb"]
    out = env["out"]
    ve_y = env["ve_y"]

    if "y" in dbg:
        ydbg = p.alloc("ydbg", [128, 1024], F32)
        for i in range(32):
            p.wait("vector", "VE", ve_y)
            p.wait("vector", "STS", p.cnt["STS"])
            vv = vop(lambda e, i=i: e.tensor_copy(ydbg[:, :], y_sb[:, i * 1024:(i + 1) * 1024]))
            p.wait("sync", "VE", vv)
            p.dma("sync", dbg["y"][:, i * 1024:(i + 1) * 1024], ydbg[:, :], "STS")

    if SKIP_TAIL:
        zz = p.alloc("zz", [128, DM], F32)
        p.wait("vector", "VE", ve_y)
        vz = vop(lambda e: e.memset(zz[:, :], 0.0))
        p.wait("sync", "VE", vz)
        for tb in range(8):
            p.dma("sync", out[tb * 128:(tb + 1) * 128, :], zz[:, :], "STS")
    else:
        build_tail(p, nc, env)

    p.wait("sync", "STS", p.cnt["STS"])

    with nc.Block() as block:
        @block.sync
        def _(e):
            for f in p.ops["sync"]:
                f(e)

        @block.tensor
        def _(e):
            for f in p.ops["tensor"]:
                f(e)

        @block.vector
        def _(e):
            for f in p.ops["vector"]:
                f(e)

        @block.scalar
        def _(e):
            for f in p.ops["scalar"]:
                f(e)

        @block.gpsimd
        def _(e):
            for f in p.ops["gpsimd"]:
                f(e)

    p.close()
    return nc


def build_tail(p, nc, env):
    out = env["out"]
    ve_y = env["ve_y"]
    vop, sop, top, vsync = env["vop"], env["sop"], env["top"], env["vsync"]
    psA, psC, psY, psS, psG = env["psA"], env["psC"], env["psY"], env["psS"], env["psG"]
    y_sb, u_fm = env["y_sb"], env["u_fm"]
    WZT, WOT, WG, WU, WD = env["WZT"], env["WOT"], env["WG"], env["WU"], env["WD"]
    GW, EB, UF, NWR = env["GW"], env["EB"], env["UF"], env["NWR"]
    x1_dram, comb_dram = env["x1_dram"], env["comb_dram"]
    yn_dram = nc.dram_tensor("yn_dram", [128, CK * DIN], BF16)
    xlo_dram = nc.dram_tensor("xlo_dram", [128, 8 * DM], BF16)
    nw_sb, dbg = env["nw_sb"], env["dbg"]
    id_f, ones_f, ones_c = env["id_f"], env["ones_f"], env["ones_c"]
    XTS = p.sems["XT"]

    for nm in ("relay_bf", "relay_b", "rstg", "ga", "avrow"):
        p.free(nm)

    sgz = p.alloc("sgz", [128, 2 * 512], BF16)
    sc512 = p.alloc("sc512", [128, 512], BF16)
    ssq = p.alloc("ssq", [128, 64], F32)
    nw_bc = p.alloc("nw_bc", [128, DIN], BF16)
    rstd8 = p.alloc("rstd8", [128, 8], F32)
    rs_scr = p.alloc("rs_scr", [128, 8], F32)
    zw = p.alloc("zw", [128, 2 * 8192], BF16)
    p.wait("gpsimd", "VE", ve_y)
    p.dma("gpsimd", nw_bc[:, :],
          NWR[0:1, :].rearrange("o c -> (o c)").partition_broadcast(128), "LCB")
    nwbc_ld = p.cnt["LCB"]
    banks = [psA[0], psA[1], psC[0], psC[1]]
    bnames = ["psA0", "psA1", "psC0", "psC1"]
    p.wait("vector", "VE", ve_y)
    zw_val = {}
    for s8 in range(8):
        sl = s8 % 2
        p.gwait("gpsimd", f"zw{sl}")
        zw_val[s8] = p.dma("gpsimd", zw[:, sl * 8192:(sl + 1) * 8192],
                           WZT[s8, :, :, :].rearrange("p k c -> p (k c)"), f"LZ{sl}")
        p.wait("tensor", f"LZ{sl}", zw_val[s8])
        for tb in range(8):
            bi = (s8 * 8 + tb) % 4
            ps = banks[bi]
            p.gwait("tensor", bnames[bi])
            for k in range(16):
                p.op("tensor", lambda e, ps=ps, k=k, sl=sl, tb=tb:
                     e.matmul(ps[:, 0:512],
                              u_fm[:, k * TOKH + 3 + tb * 128: k * TOKH + 3 + (tb + 1) * 128],
                              zw[:, sl * 8192 + k * 512: sl * 8192 + (k + 1) * 512],
                              start=(k == 0), stop=(k == 15)),
                     inc=("PE", 1) if k == 15 else None)
            pev = p.cnt["PE"]
            p.wait("scalar", "PE", pev)
            p.gwait("scalar", f"sgz{tb % 2}")
            se_z = sop(lambda e, ps=ps, tb=tb: e.activation(sgz[:, (tb % 2) * 512:(tb % 2 + 1) * 512],
                                                            ps[:, 0:512], AF.Silu))
            p.gset(bnames[bi], "SE", se_z)
            p.wait("vector", "SE", se_z)
            ysl = y_sb[:, tb * DIN + s8 * 512: tb * DIN + (s8 + 1) * 512]
            vg = vop(lambda e, ysl=ysl, tb=tb: e.tensor_tensor(ysl, ysl,
                                                               sgz[:, (tb % 2) * 512:(tb % 2 + 1) * 512],
                                                               op=ALU.mult))
            p.gset(f"sgz{tb % 2}", "VE", vg)
            vsync()
            vop(lambda e, ysl=ysl, tb=tb, s8=s8:
                e.scalar_tensor_tensor(sc512[:, :], ysl, 1.0, ysl, op0=ALU.mult, op1=ALU.mult,
                                       accum_out=ssq[:, tb * 8 + s8: tb * 8 + s8 + 1]))
        p.gset(f"zw{sl}", "PE", p.cnt["PE"])
    p.free("zw")
    vsync()
    for tb in range(8):
        vop(lambda e, tb=tb: e.reduce_sum(rs_scr[:, tb:tb + 1], ssq[:, tb * 8:(tb + 1) * 8],
                                          axis=mybir.AxisListType.X))
    vsync()
    vop(lambda e: e.tensor_scalar(rs_scr[:, :], rs_scr[:, :], 1.0 / DIN, EPS,
                                  op0=ALU.mult, op1=ALU.add))
    vsync()
    v_rs = p.cnt["VE"]
    p.wait("scalar", "VE", v_rs)
    s_l = sop(lambda e: e.activation(rstd8[:, :], rs_scr[:, :], AF.Ln))
    p.wait("scalar", "SE", s_l)
    s_r = sop(lambda e: e.activation(rstd8[:, :], rstd8[:, :], AF.Exp, scale=-0.5))
    p.wait("vector", "SE", s_r)
    p.wait("vector", "LCB", nwbc_ld)
    for ck in range(CK):
        ysl = y_sb[:, ck * DIN:(ck + 1) * DIN]
        vop(lambda e, ysl=ysl: e.tensor_tensor(ysl, ysl, nw_bc[:, :], op=ALU.mult))
        vsync()
        vv = vop(lambda e, ysl=ysl, ck=ck: e.tensor_scalar(ysl, ysl, rstd8[:, ck:ck + 1], None,
                                                           op0=ALU.mult))
        p.wait("gpsimd", "VE", vv)
        p.dma("gpsimd", yn_dram[:, ck * DIN:(ck + 1) * DIN], ysl, "CST")
    yn_st = p.cnt["CST"]

    if "yn" in dbg:
        p.wait("gpsimd", "CST", yn_st)
        p.dma("gpsimd", dbg["yn"][:, :], y_sb[:, :], "DBS")
        p.wait("vector", "DBS", p.cnt["DBS"])

    for nm in ("rs_scr", "rstd8", "nw_bc", "ssq", "sc512", "sgz",
               "T1", "T0", "s_T", "azp", "aseg", "ab_T", "ab_h", "BCp", "y_sb", "u_fm"):
        p.free(nm)

    x1_fm = p.alloc("x1_fm", [128, 16 * TOK], BF16)
    yn_fm = p.alloc("yn_fm", [128, CK * DIN], BF16)
    wo = p.alloc("wo", [128, 2 * 4096], BF16)
    ustg = p.alloc("ustg", [128, 2 * DM], F32)
    x1stg = p.alloc("x1stg", [128, 8 * 128], F32)
    x1t = p.alloc("x1t", [128, 2 * DM], F32)
    xbt = p.alloc("xbt", [128, 2 * DM], BF16)
    xlt = p.alloc("xlt", [128, 2 * DM], BF16)
    bnscr = p.alloc("bnscr", [128, 24], F32)
    mv2 = p.alloc("mv2", [128, 2], F32)
    mcol = p.alloc("mcol", [128, 2], F32)
    p.wait("sync", "CST", yn_st)
    ynfm_val = 0
    for cc in range(32):
        for ck in range(CK):
            p.emit("sync", lambda e, cc=cc, ck=ck, s=XTS:
                   e.dma_start(out=yn_fm[:, cc * TOK + ck * 128: cc * TOK + (ck + 1) * 128],
                               in_=yn_dram[:, ck * DIN + cc * 128: ck * DIN + (cc + 1) * 128],
                               transpose=True).then_inc(s, 16))
            ynfm_val = p.bump("XT", 16)
    p.wait("tensor", "XT", ynfm_val)
    for dc in range(16):
        sl = dc % 2
        p.gwait("gpsimd", f"wo{sl}")
        wo_v = p.dma("gpsimd", wo[:, sl * 4096:(sl + 1) * 4096],
                     WOT[dc, :, :, :].rearrange("p k c -> p (k c)"), f"LO{sl}")
        p.wait("tensor", f"LO{sl}", wo_v)
        for tb in range(8):
            bi = tb % 4
            ps = banks[bi]
            p.gwait("tensor", bnames[bi])
            for cc in range(32):
                p.op("tensor", lambda e, ps=ps, cc=cc, tb=tb, sl=sl:
                     e.matmul(ps[:, 0:128],
                              yn_fm[:, cc * TOK + tb * 128: cc * TOK + (tb + 1) * 128],
                              wo[:, sl * 4096 + cc * 128: sl * 4096 + (cc + 1) * 128],
                              start=(cc == 0), stop=(cc == 31)),
                     inc=("PE", 1) if cc == 31 else None)
            pev = p.cnt["PE"]
            p.wait("vector", "PE", pev)
            p.spacer("vector", 256)
            p.gwait("vector", f"x1stg{tb}")
            vv = vop(lambda e, ps=ps, tb=tb: e.tensor_copy(x1stg[:, tb * 128:(tb + 1) * 128],
                                                           ps[:, 0:128]))
            p.gset(bnames[bi], "VE", vv)
        vsync()
        vstg = p.cnt["VE"]
        p.wait("gpsimd", "VE", vstg)
        dst = x1_dram[:, :].rearrange("p (tb c) -> p tb c", c=DM)[:, :, dc * 128:(dc + 1) * 128]
        st_v = p.dma("gpsimd", dst, x1stg[:, 0:1024].rearrange("p (tb c) -> p tb c", c=128), "CST")
        for tb in range(8):
            p.gset(f"x1stg{tb}", "CST", st_v)
        p.gset(f"wo{sl}", "PE", p.cnt["PE"])
    x1pre_st = p.cnt["CST"]
    x1fm_val = 0
    for tb in range(8):
        sl = tb % 2
        p.gwait("gpsimd", f"x1t{sl}")
        p.wait("gpsimd", "CST", x1pre_st)
        lx = p.dma("gpsimd", x1t[:, sl * DM:(sl + 1) * DM], x1_dram[:, tb * DM:(tb + 1) * DM],
                   f"LX{sl}")
        lu = p.dma("gpsimd", ustg[:, sl * DM:(sl + 1) * DM], UF[tb * 128:(tb + 1) * 128, :],
                   f"LU{sl}")
        p.wait("vector", f"LX{sl}", lx)
        p.wait("vector", f"LU{sl}", lu)
        xs = x1t[:, sl * DM:(sl + 1) * DM]
        vop(lambda e, xs=xs, sl=sl: e.tensor_tensor(xs, xs, ustg[:, sl * DM:(sl + 1) * DM],
                                                    op=ALU.add))
        vsync()
        for q in range(4):
            vop(lambda e, xs=xs, q=q: e.bn_stats(bnscr[:, q * 6:(q + 1) * 6],
                                                 xs[:, q * 512:(q + 1) * 512]))
            vsync()
        vop(lambda e: e.bn_aggr(mv2[:, :], bnscr[:, :]))
        vsync()
        vop(lambda e: e.tensor_scalar(mcol[:, 0:1], mv2[:, 0:1], -1.0, None, op0=ALU.mult))
        vsync()
        vop(lambda e: e.tensor_scalar(mv2[:, 1:2], mv2[:, 1:2], 1.0, EPS,
                                      op0=ALU.mult, op1=ALU.add))
        vsync()
        v_mv = p.cnt["VE"]
        p.wait("scalar", "VE", v_mv)
        s1 = sop(lambda e: e.activation(mcol[:, 1:2], mv2[:, 1:2], AF.Ln))
        p.wait("scalar", "SE", s1)
        s2 = sop(lambda e: e.activation(mcol[:, 1:2], mcol[:, 1:2], AF.Exp, scale=-0.5))
        p.wait("vector", "SE", s2)
        vop(lambda e, xs=xs: e.tensor_scalar(xs, xs, mcol[:, 0:1], None, op0=ALU.add))
        vsync()
        vop(lambda e, xs=xs: e.tensor_scalar(xs, xs, mcol[:, 1:2], None, op0=ALU.mult))
        vsync()
        vx1 = p.cnt["VE"]
        p.wait("gpsimd", "VE", vx1)
        p.dma("gpsimd", x1_dram[:, tb * DM:(tb + 1) * DM], xs, "CST")
        vop(lambda e, xs=xs, sl=sl: e.tensor_copy(xbt[:, sl * DM:(sl + 1) * DM], xs))
        vsync()
        vlo = vop(lambda e, xs=xs, sl=sl: e.tensor_tensor(xlt[:, sl * DM:(sl + 1) * DM], xs,
                                                          xbt[:, sl * DM:(sl + 1) * DM],
                                                          op=ALU.subtract))
        p.wait("gpsimd", "VE", vlo)
        lo_st = p.dma("gpsimd", xlo_dram[:, tb * DM:(tb + 1) * DM],
                      xlt[:, sl * DM:(sl + 1) * DM], "CCS")
        p.wait("sync", "VE", vlo)
        for k in range(16):
            p.emit("sync", lambda e, k=k, tb=tb, sl=sl, s=XTS:
                   e.dma_start(out=x1_fm[:, k * TOK + tb * 128: k * TOK + (tb + 1) * 128],
                               in_=xbt[:, sl * DM + k * 128: sl * DM + (k + 1) * 128],
                               transpose=True).then_inc(s, 16))
            x1fm_val = p.bump("XT", 16)
        p.gset(f"x1t{sl}", "XT", x1fm_val)
        p.gadd(f"x1t{sl}", "CCS", lo_st)
    x1lo_st = p.cnt["CCS"]
    for nm in ("mcol", "mv2", "bnscr", "xlt", "xbt", "x1t", "x1stg", "ustg", "wo", "yn_fm"):
        p.free(nm)

    build_tail2(p, nc, env, locals())


def build_tail2(p, nc, env, env1):
    out = env["out"]
    vop, sop, top, vsync = env["vop"], env["sop"], env["top"], env["vsync"]
    psA, psC, psY, psS, psG = env["psA"], env["psC"], env["psY"], env["psS"], env["psG"]
    WG, WU, WD = env["WG"], env["WU"], env["WD"]
    GW, EB, UF = env["GW"], env["EB"], env["UF"]
    x1_dram, comb_dram = env["x1_dram"], env["comb_dram"]
    dbg = env["dbg"]
    xlo_dram = env1["xlo_dram"]
    x1_fm = env1["x1_fm"]
    x1lo_st = env1["x1lo_st"]
    x1fm_val = env1["x1fm_val"]
    XTS = p.sems["XT"]
    banks = [psA[0], psA[1], psC[0], psC[1]]
    bnames = ["psA0", "psA1", "psC0", "psC1"]
    dbanks = None

    xlo_fm = p.alloc("xlo_fm", [128, 16 * TOK], BF16)
    gw_sb = p.alloc("gw_sb", [EXP, DM], F32)
    gwp = p.alloc("gwp", [128, 2 * 16 * 128], BF16)
    gwt_f = p.alloc("gwt_f", [128, 16 * 16], F32)
    lgt = p.alloc("lgt", [128, 8 * 16], F32)
    bsd = p.alloc("bsd", [128, 8 * 16], F32)
    eb_bc = p.alloc("eb_bc", [128, 16], F32)
    m8 = p.alloc("m8", [128, 8], F32)
    rsc = p.alloc("rsc", [128, 16], F32)
    rcol = p.alloc("rcol", [128, 4], F32)
    comb = p.alloc("comb", [128, 2 * 128], BF16)
    p.wait("sync", "XT", x1fm_val)
    for tb in range(8):
        for k in range(16):
            p.emit("sync", lambda e, k=k, tb=tb, s=XTS:
                   e.dma_start(out=xlo_fm[:, k * TOK + tb * 128: k * TOK + (tb + 1) * 128],
                               in_=xlo_dram[:, tb * DM + k * 128: tb * DM + (k + 1) * 128],
                               transpose=True).then_inc(s, 16))
            p.bump("XT", 16)
    xlo_val = p.cnt["XT"]
    p.wait("gpsimd", "VE", p.cnt["VE"])
    lgw = p.dma("gpsimd", gw_sb[:, :], GW[:, :], "LCB")
    leb = p.dma("gpsimd", eb_bc[:, 0:16],
                EB[:, :].rearrange("e o -> (e o)").partition_broadcast(128), "LCB")
    gwh = p.alloc("gwh", [EXP, DM], BF16)
    gwl = p.alloc("gwl", [EXP, DM], BF16)
    p.wait("vector", "LCB", lgw)
    vop(lambda e: e.memset(gwp[:, :], 0.0))
    vop(lambda e: e.tensor_copy(gwh[:, :], gw_sb[:, :]))
    vsync()
    vgl = vop(lambda e: e.tensor_tensor(gwl[:, :], gw_sb[:, :], gwh[:, :], op=ALU.subtract))
    p.wait("sync", "VE", vgl)
    for k in range(16):
        for hi in range(2):
            srcb = gwh if hi == 0 else gwl
            p.emit("sync", lambda e, k=k, hi=hi, srcb=srcb, s=XTS:
                   e.dma_start(out=gwp[:, (hi * 16 + k) * 128: (hi * 16 + k) * 128 + 16],
                               in_=srcb[:, k * 128:(k + 1) * 128], transpose=True).then_inc(s, 16))
            p.bump("XT", 16)
    gw_val = p.cnt["XT"]
    p.wait("tensor", "XT", max(xlo_val, gw_val))
    p.wait("tensor", "LCB", leb)
    comb_st = 0
    for tb in range(8):
        bi = tb % 2
        ps = banks[bi]
        p.gwait("tensor", bnames[bi])
        nmm = 0
        for k in range(16):
            for (xa, wa) in ((0, 0), (0, 1), (1, 0)):
                xs = x1_fm if xa == 0 else xlo_fm
                p.op("tensor", lambda e, ps=ps, k=k, tb=tb, xs=xs, wa=wa, nmm=nmm:
                     e.matmul(ps[:, 0:16],
                              xs[:, k * TOK + tb * 128: k * TOK + (tb + 1) * 128],
                              gwp[:, (wa * 16 + k) * 128: (wa * 16 + k) * 128 + 16],
                              start=(nmm == 0), stop=(nmm == 47)),
                     inc=("PE", 1) if nmm == 47 else None)
                nmm += 1
        pev = p.cnt["PE"]
        p.wait("vector", "PE", pev)
        p.spacer("vector", 256)
        vop(lambda e, ps=ps, tb=tb: e.tensor_copy(lgt[:, tb * 16:(tb + 1) * 16], ps[:, 0:16]))
        vsync()
        vv = vop(lambda e, ps=ps, tb=tb: e.tensor_tensor(bsd[:, tb * 16:(tb + 1) * 16],
                                                         ps[:, 0:16], eb_bc[:, 0:16],
                                                         op=ALU.add))
        p.gset(bnames[bi], "VE", vv)
        vsync()
        vop(lambda e, tb=tb: e.max(m8[:, :], bsd[:, tb * 16:(tb + 1) * 16]))
        vsync()
        vop(lambda e, tb=tb: e.tensor_scalar(rsc[:, :], bsd[:, tb * 16:(tb + 1) * 16],
                                             m8[:, 1:2], None, op0=ALU.is_ge))
        vsync()
        vop(lambda e, tb=tb: e.reduce_max(rcol[:, 0:1], lgt[:, tb * 16:(tb + 1) * 16],
                                          axis=mybir.AxisListType.X))
        vsync()
        vop(lambda e: e.tensor_scalar(rcol[:, 1:2], rcol[:, 0:1], -1.0, None, op0=ALU.mult))
        vsync()
        vq = p.cnt["VE"]
        p.wait("scalar", "VE", vq)
        sq = sop(lambda e, tb=tb: e.activation(lgt[:, tb * 16:(tb + 1) * 16],
                                               lgt[:, tb * 16:(tb + 1) * 16],
                                               AF.Exp, bias=rcol[:, 1:2]))
        p.wait("vector", "SE", sq)
        vop(lambda e, tb=tb: e.tensor_tensor(lgt[:, tb * 16:(tb + 1) * 16],
                                             lgt[:, tb * 16:(tb + 1) * 16], rsc[:, 0:16],
                                             op=ALU.mult))
        vsync()
        vop(lambda e, tb=tb: e.reduce_sum(rcol[:, 2:3], lgt[:, tb * 16:(tb + 1) * 16],
                                          axis=mybir.AxisListType.X))
        vsync()
        vop(lambda e: e.reciprocal(rcol[:, 3:4], rcol[:, 2:3]))
        vsync()
        p.gwait("vector", "comb_w")
        vop(lambda e: e.memset(comb[:, 0:128], 0.0))
        vsync()
        vcb = vop(lambda e, tb=tb:
                  e.tensor_scalar(comb[:, 0:16],
                                  lgt[:, tb * 16:(tb + 1) * 16], rcol[:, 3:4], None,
                                  op0=ALU.mult))
        p.wait("sync", "VE", vcb)
        p.gwait("sync", "comb_t")
        p.emit("sync", lambda e, s=XTS:
               e.dma_start(out=comb[:, 128:256],
                           in_=comb[:, 0:128], transpose=True).then_inc(s, 16))
        ctv = p.bump("XT", 16)
        p.gset("comb_w", "XT", ctv)
        p.wait("gpsimd", "XT", ctv)
        comb_st = p.dma("gpsimd", comb_dram[:, tb * 128:(tb + 1) * 128],
                        comb[0:16, 128:256], "CCS")
        p.gset("comb_t", "CCS", comb_st)
    if "comb" in dbg:
        p.wait("gpsimd", "CCS", comb_st)
        p.dma("gpsimd", dbg["comb"][0:16, :].rearrange("e c -> e c"), comb_dram[:, :], "DBS")
        p.wait("vector", "DBS", p.cnt["DBS"])
    for nm in ("gwl", "gwh", "comb", "rcol", "rsc", "m8", "eb_bc", "bsd", "lgt",
               "gwt_f", "gwp", "gw_sb", "xlo_fm"):
        p.free(nm)

    p.free("psTR")
    psD = p.alloc("psD", [128, 512], F32, psum=True)
    dbanks = [psA[0], psA[1], psC[0], psC[1], psY, psS, psG, psD]
    dbnames = ["psA0", "psA1", "psC0", "psC1", "psY", "psS", "psG", "psD"]
    hbuf = p.alloc("hbuf", [128, 2 * FC * TOK], BF16)
    acc = p.alloc("acc", [128, 16 * TOK], F32)
    wgst = p.alloc("wgst", [128, 2 * DM], BF16)
    wust = p.alloc("wust", [128, 2 * DM], BF16)
    wdst = p.alloc("wdst", [128, 2 * FC * 512], BF16)
    cbc = p.alloc("cbc", [128, 2 * TOK], BF16)
    sg1 = p.alloc("sg1", [128, 2 * 512], BF16)
    sg2 = p.alloc("sg2", [128, 2 * 512], BF16)
    groups = [(0, 1), (2, 3), (4, 5), (6, 7), (8, 9), (10, 11), (12, 13), (14, 15), (16,)]
    p.wait("gpsimd", "CCS", max(comb_st, x1lo_st))
    for gi, grp in enumerate(groups):
        for ei, e_ in enumerate(grp):
            sl = e_ % 2
            if e_ < EXP:
                p.gwait("gpsimd", f"cbc{sl}")
                cb_v = p.dma("gpsimd", cbc[:, sl * TOK:(sl + 1) * TOK],
                             comb_dram[e_:e_ + 1, :].rearrange("o c -> (o c)").partition_broadcast(128),
                             "LCB")
            else:
                cb_v = None
            for fc in range(FC):
                wsl = fc % 2
                p.gwait("gpsimd", f"wg{wsl}")
                g_v = p.dma("gpsimd", wgst[:, wsl * DM:(wsl + 1) * DM],
                            WG[e_, fc, :, :, :].rearrange("p k c -> p (k c)"), f"LG{wsl}")
                u_v = p.dma("gpsimd", wust[:, wsl * DM:(wsl + 1) * DM],
                            WU[e_, fc, :, :, :].rearrange("p k c -> p (k c)"), f"LG{wsl}")
                p.wait("tensor", f"LG{wsl}", u_v)
                for tc in range(2):
                    big, biu = (tc, 2 + tc)
                    psg_, psu_ = banks[big], banks[biu]
                    p.gwait("tensor", bnames[big])
                    p.gwait("tensor", bnames[biu])
                    for k in range(16):
                        p.op("tensor", lambda e, ps=psg_, k=k, tc=tc, wsl=wsl:
                             e.matmul(ps[:, 0:512],
                                      wgst[:, wsl * DM + k * 128: wsl * DM + (k + 1) * 128],
                                      x1_fm[:, k * TOK + tc * 512: k * TOK + (tc + 1) * 512],
                                      start=(k == 0), stop=(k == 15)),
                             inc=("PE", 1) if k == 15 else None)
                    for k in range(16):
                        p.op("tensor", lambda e, ps=psu_, k=k, tc=tc, wsl=wsl:
                             e.matmul(ps[:, 0:512],
                                      wust[:, wsl * DM + k * 128: wsl * DM + (k + 1) * 128],
                                      x1_fm[:, k * TOK + tc * 512: k * TOK + (tc + 1) * 512],
                                      start=(k == 0), stop=(k == 15)),
                             inc=("PE", 1) if k == 15 else None)
                    pev = p.cnt["PE"]
                    p.wait("scalar", "PE", pev)
                    p.gwait("scalar", f"sg1{tc}")
                    sv = sop(lambda e, ps=psg_, tc=tc: e.activation(sg1[:, tc * 512:(tc + 1) * 512],
                                                                    ps[:, 0:512], AF.Silu))
                    p.gset(bnames[big], "SE", sv)
                    p.wait("vector", "SE", sv)
                    p.spacer("vector", 256)
                    if e_ < EXP:
                        p.wait("vector", "LCB", cb_v)
                        vu = vop(lambda e, ps=psu_, tc=tc, sl=sl:
                                 e.tensor_tensor(sg2[:, tc * 512:(tc + 1) * 512], ps[:, 0:512],
                                                 cbc[:, sl * TOK + tc * 512: sl * TOK + (tc + 1) * 512],
                                                 op=ALU.mult))
                        vsync()
                        vh = vop(lambda e, ei=ei, fc=fc, tc=tc:
                                 e.tensor_tensor(hbuf[:, (ei * FC + fc) * TOK + tc * 512:
                                                      (ei * FC + fc) * TOK + (tc + 1) * 512],
                                                 sg2[:, tc * 512:(tc + 1) * 512],
                                                 sg1[:, tc * 512:(tc + 1) * 512], op=ALU.mult))
                    else:
                        vh = vop(lambda e, ei=ei, fc=fc, tc=tc, ps=psu_:
                                 e.tensor_tensor(hbuf[:, (ei * FC + fc) * TOK + tc * 512:
                                                      (ei * FC + fc) * TOK + (tc + 1) * 512],
                                                 ps[:, 0:512],
                                                 sg1[:, tc * 512:(tc + 1) * 512], op=ALU.mult))
                    p.gset(bnames[biu], "VE", vh)
                    p.gset(f"sg1{tc}", "VE", vh)
                p.gset(f"wg{wsl}", "PE", p.cnt["PE"])
            if e_ < EXP:
                p.gset(f"cbc{sl}", "VE", p.cnt["VE"])
        ve_h = p.cnt["VE"]
        p.wait("tensor", "VE", ve_h)
        for dsl in range(4):
            for ei, e_ in enumerate(grp):
                wsl = ei
                p.gwait("gpsimd", f"wd{wsl}")
                d_v = p.dma("gpsimd", wdst[:, wsl * FC * 512:(wsl + 1) * FC * 512],
                            WD[e_, dsl, :, :, :, :].rearrange("p f d c -> p (f d c)"), f"LD{wsl}")
                p.wait("tensor", f"LD{wsl}", d_v)
                for tc in range(2):
                    for dj in range(4):
                        bi = tc * 4 + dj
                        ps = dbanks[bi]
                        first = (ei == 0)
                        last = (ei == len(grp) - 1)
                        if first:
                            p.gwait("tensor", dbnames[bi])
                        ng = len(grp)
                        for fc in range(FC):
                            p.op("tensor", lambda e, ps=ps, fc=fc, dj=dj, tc=tc, ei=ei, wsl=wsl, ng=ng:
                                 e.matmul(ps[:, 0:512],
                                          wdst[:, wsl * FC * 512 + (fc * 4 + dj) * 128:
                                               wsl * FC * 512 + (fc * 4 + dj) * 128 + 128],
                                          hbuf[:, (ei * FC + fc) * TOK + tc * 512:
                                               (ei * FC + fc) * TOK + (tc + 1) * 512],
                                          start=(ei == 0 and fc == 0),
                                          stop=(ei == ng - 1 and fc == FC - 1)),
                                 inc=("PE", 1) if fc == FC - 1 else None)
                        if last:
                            pev = p.cnt["PE"]
                            p.wait("vector", "PE", pev)
                            p.spacer("vector", 256)
                            col = (dsl * 4 + dj) * TOK + tc * 512
                            if gi == 0:
                                vv = vop(lambda e, ps=ps, col=col:
                                         e.tensor_copy(acc[:, col:col + 512], ps[:, 0:512]))
                            else:
                                vv = vop(lambda e, ps=ps, col=col:
                                         e.scalar_tensor_tensor(acc[:, col:col + 512], ps[:, 0:512],
                                                                1.0, acc[:, col:col + 512],
                                                                op0=ALU.mult, op1=ALU.add))
                            p.gset(dbnames[bi], "VE", vv)
                p.gset(f"wd{wsl}", "PE", p.cnt["PE"])
    ve_ffn = p.cnt["VE"]
    if "acc" in dbg:
        p.wait("gpsimd", "VE", ve_ffn)
        p.dma("gpsimd", dbg["acc"][:, :], acc[:, :], "DBS")
        p.wait("vector", "DBS", p.cnt["DBS"])

    for nm in ("sg2", "sg1", "cbc", "wdst", "wust", "wgst"):
        p.free(nm)
    accb = p.alloc("accb", [128, 16 * 128], BF16)
    accT = p.alloc("accT", [128, 2 * DM], BF16)
    x2t = p.alloc("x2t", [128, 2 * DM], F32)
    bnscr2 = p.alloc("bnscr2", [128, 24], F32)
    mv3 = p.alloc("mv3", [128, 2], F32)
    mc3 = p.alloc("mc3", [128, 2], F32)
    outs = p.alloc("outs", [128, 2 * DM], F32)
    p.wait("vector", "VE", ve_ffn)
    p.wait("gpsimd", "VE", ve_ffn)
    p.wait("sync", "VE", ve_ffn)
    for tb in range(8):
        sl = tb % 2
        p.gwait("vector", f"accb{sl}")
        for dc in range(16):
            vop(lambda e, dc=dc, tb=tb: e.tensor_copy(accb[:, dc * 128:(dc + 1) * 128],
                                                      acc[:, dc * TOK + tb * 128: dc * TOK + (tb + 1) * 128]))
        vsync()
        vab = p.cnt["VE"]
        p.wait("sync", "VE", vab)
        p.gwait("sync", f"accT{sl}")
        for dc in range(16):
            p.emit("sync", lambda e, dc=dc, sl=sl, s=XTS:
                   e.dma_start(out=accT[:, sl * DM + dc * 128: sl * DM + (dc + 1) * 128],
                               in_=accb[:, dc * 128:(dc + 1) * 128], transpose=True).then_inc(s, 16))
            p.bump("XT", 16)
        at_v = p.cnt["XT"]
        p.gset(f"accb{sl}", "XT", at_v)
        p.gwait("gpsimd", f"x2t{sl}")
        lx = p.dma("gpsimd", x2t[:, sl * DM:(sl + 1) * DM], x1_dram[:, tb * DM:(tb + 1) * DM],
                   f"LX{sl}")
        p.wait("vector", f"LX{sl}", lx)
        p.wait("vector", "XT", at_v)
        xs = x2t[:, sl * DM:(sl + 1) * DM]
        vv0 = vop(lambda e, xs=xs, sl=sl: e.tensor_tensor(xs, xs, accT[:, sl * DM:(sl + 1) * DM],
                                                          op=ALU.add))
        p.gset(f"accT{sl}", "VE", vv0)
        vsync()
        for q in range(4):
            vop(lambda e, xs=xs, q=q: e.bn_stats(bnscr2[:, q * 6:(q + 1) * 6],
                                                 xs[:, q * 512:(q + 1) * 512]))
            vsync()
        vop(lambda e: e.bn_aggr(mv3[:, :], bnscr2[:, :]))
        vsync()
        vop(lambda e: e.tensor_scalar(mc3[:, 0:1], mv3[:, 0:1], -1.0, None, op0=ALU.mult))
        vsync()
        vop(lambda e: e.tensor_scalar(mv3[:, 1:2], mv3[:, 1:2], 1.0, EPS,
                                      op0=ALU.mult, op1=ALU.add))
        vsync()
        v_mv = p.cnt["VE"]
        p.wait("scalar", "VE", v_mv)
        s1 = sop(lambda e: e.activation(mc3[:, 1:2], mv3[:, 1:2], AF.Ln))
        p.wait("scalar", "SE", s1)
        s2 = sop(lambda e: e.activation(mc3[:, 1:2], mc3[:, 1:2], AF.Exp, scale=-0.5))
        p.wait("vector", "SE", s2)
        p.gwait("vector", f"outs{sl}")
        vop(lambda e, xs=xs, sl=sl: e.scalar_tensor_tensor(outs[:, sl * DM:(sl + 1) * DM], xs,
                                                           1.0, xs, op0=ALU.bypass,
                                                           op1=ALU.bypass)) if False else None
        vop(lambda e, xs=xs: e.tensor_scalar(xs, xs, mc3[:, 0:1], None, op0=ALU.add))
        vsync()
        vo = vop(lambda e, xs=xs, sl=sl: e.tensor_scalar(outs[:, sl * DM:(sl + 1) * DM], xs,
                                                         mc3[:, 1:2], None, op0=ALU.mult))
        p.wait("sync", "VE", vo)
        ost = p.dma("sync", out[tb * 128:(tb + 1) * 128, :], outs[:, sl * DM:(sl + 1) * DM], "STS")
        p.gset(f"outs{sl}", "STS", ost)
        p.gset(f"x2t{sl}", "VE", vo)



def _host_prep(inputs):
    f32 = np.float32
    u = np.asarray(inputs["L0_attn_u"], f32)
    Win = np.asarray(inputs["L0_attn_W_in"], f32)
    convw = np.asarray(inputs["L0_attn_conv_w"], f32)[:, 0, :]
    convb = np.asarray(inputs["L0_attn_conv_b"], f32)
    dtb = np.asarray(inputs["L0_attn_dt_bias"], f32)
    alog = np.asarray(inputs["L0_attn_A_log"], f32)
    Dv = np.asarray(inputs["L0_attn_D"], f32)
    normw = np.asarray(inputs["L0_attn_norm_w"], f32)
    Wout = np.asarray(inputs["L0_attn_W_out"], f32)
    gw = np.asarray(inputs["L0_ffn_gate_w"], f32)
    wg = np.asarray(inputs["L0_ffn_w_gate"], f32)
    wu = np.asarray(inputs["L0_ffn_w_up"], f32)
    wd = np.asarray(inputs["L0_ffn_w_down"], f32)
    swg = np.asarray(inputs["L0_ffn_sw_gate"], f32)
    swu = np.asarray(inputs["L0_ffn_sw_up"], f32)
    swd = np.asarray(inputs["L0_ffn_sw_down"], f32)
    eb = np.asarray(inputs["L0_ffn_expert_bias"], f32)

    uf = np.ascontiguousarray(u.reshape(8192, DM))
    uT = np.ascontiguousarray(uf.T)

    WX = Win[DIN:DIN + 4352]
    WXT = np.ascontiguousarray(WX.reshape(NG, 128, 16, 128).transpose(0, 3, 2, 1))
    WDT = np.ascontiguousarray(Win[8448:8512].reshape(64, 16, 128).transpose(2, 1, 0))
    WZT = np.ascontiguousarray(Win[0:DIN].reshape(8, 512, 16, 128).transpose(0, 3, 2, 1))
    WOT = np.ascontiguousarray(Wout.T.reshape(32, 128, 16, 128).transpose(2, 1, 0, 3))
    CDIAG = np.zeros((NG, 128, 4, 128), f32)
    cwg = convw.reshape(NG, 128, 4)
    for rix in range(128):
        CDIAG[:, rix, :, rix] = cwg[:, rix, :]
    CB = np.ascontiguousarray(convb.reshape(NG, 128).T)
    NW = np.ascontiguousarray(normw.reshape(32, 128).T)
    WG17 = np.concatenate([wg, swg[None]], 0)
    WU17 = np.concatenate([wu, swu[None]], 0)
    WD17 = np.concatenate([wd, swd[None]], 0)
    WGt = np.ascontiguousarray(WG17.reshape(17, FC, 128, 16, 128).transpose(0, 1, 4, 3, 2))
    WUt = np.ascontiguousarray(WU17.reshape(17, FC, 128, 16, 128).transpose(0, 1, 4, 3, 2))
    WDt = np.ascontiguousarray(WD17.reshape(17, 4, 4, 128, FC, 128).transpose(0, 1, 5, 4, 2, 3))
    OH = np.zeros((24, 24 * 128), f32)
    for rix in range(24):
        OH[rix, rix * 128:(rix + 1) * 128] = 1.0
    common = dict(
        WXT=WXT, WDT=WDT, WZT=WZT, WOT=WOT, CDIAG=CDIAG, CB=CB, NW=NW,
        NWR=normw.reshape(1, DIN),
        DTB=dtb.reshape(NH, 1), ALOG=alog.reshape(NH, 1), DVEC=Dv.reshape(1, NH),
        GW=gw, EB=eb.reshape(EXP, 1), WG=WGt, WU=WUt, WD=WDt,
        IDENT=np.eye(128, dtype=f32), ONES1=np.ones((1, 128), f32), OH16=OH,
    )
    per_core = []
    for c in range(NCORES):
        g0 = c * TOK
        rank = c % 4
        if rank == 0:
            uTc = np.concatenate([np.zeros((DM, 3), f32), uT[:, g0:g0 + TOK]], axis=1)
        else:
            uTc = uT[:, g0 - 3:g0 + TOK]
        rsel = np.zeros((128, 4), f32)
        rsel[:, :rank] = 1.0
        m = dict(common)
        m["uT"] = np.ascontiguousarray(uTc)
        m["uTm"] = np.ascontiguousarray(uT[:, g0:g0 + TOK])
        m["UF"] = np.ascontiguousarray(uf[g0:g0 + TOK, :])
        m["RSEL"] = rsel
        per_core.append(m)
    return per_core


def kernel(**inputs):
    in_maps = _host_prep(inputs)
    nc = build()
    res = run_bass_kernel_spmd(nc, in_maps, core_ids=list(range(NCORES)))
    kernel.debug_results = res.results if DEBUG else None
    outs = [res.results[c]["out"] for c in range(NCORES)]
    return np.concatenate(outs, axis=0).reshape(2, 4096, DM).astype(np.float32)


kernel.debug_results = None
